# version 01
# baseline (speedup 1.0000x reference)
import contextlib
import math
import numpy as np
import concourse.bass as bass
import concourse.mybir as mybir
from concourse.bass_utils import run_bass_kernel_spmd

F32 = mybir.dt.float32
BF16 = mybir.dt.bfloat16
I16 = mybir.dt.int16
ALU = mybir.AluOpType
ACT = mybir.ActivationFunctionType
AX = mybir.AxisListType

S = 4096
D = 1024
NB = S // 128
INNER = 4096
H = 16
DH = 64
NEI = 8
BPE = NB // NEI
EPS = 1e-5
TWO_PI = 6.283185307179586
PI_CLAMP = 3.141592
DECAY = [1.0 - 2.0 ** (-5.0 - h) for h in range(H)]


class Prog:
    ENGS = ("sync", "vector", "scalar", "tensor", "gpsimd")

    DRAIN_ENGS = ("vector", "scalar", "gpsimd")

    def __init__(self, nc):
        self.nc = nc
        self.q = {e: [] for e in self.ENGS}
        self.sems = {}
        self._dirty = {e: False for e in self.ENGS}

    def sem(self, name):
        self.sems[name] = [self.nc.alloc_semaphore(name), 0]

    def n(self, name):
        return self.sems[name][1]

    def _pre(self, eng):
        if eng in self.DRAIN_ENGS and self._dirty[eng]:
            self.q[eng].append(lambda e: e.drain())
            self._dirty[eng] = False

    def op(self, eng, fn, sig=None, inc=1):
        self._pre(eng)
        self._dirty[eng] = eng in self.DRAIN_ENGS
        if sig is not None:
            rec = self.sems[sig]
            rec[1] += inc
            h = rec[0]

            def wrapped(e, fn=fn, h=h, inc=inc):
                fn(e).then_inc(h, inc)
            self.q[eng].append(wrapped)
            return rec[1]
        self.q[eng].append(fn)
        return None

    def dma(self, eng, out, in_, sig="ld"):
        rec = self.sems[sig]
        rec[1] += 16
        h = rec[0]
        self.q[eng].append(lambda e, o=out, i=in_, h=h: e.dma_start(out=o, in_=i)
                           .then_inc(h, 16))
        return rec[1]

    def wait(self, eng, sem, val):
        if val is None:
            return
        h = self.sems[sem][0]
        self.q[eng].append(lambda e, h=h, v=val: e.wait_ge(h, v))

    def inc(self, eng, sem, by=1):
        self._pre(eng)
        rec = self.sems[sem]
        rec[1] += by
        h = rec[0]
        self.q[eng].append(lambda e, h=h, b=by: e.sem_inc(h, b))
        return rec[1]

    DMA_SEMS = ("lw", "lx0", "lx1", "lk0", "lk1", "stK", "stX0", "stX1", "stM", "sd")

    def barrier(self):
        for s_ in self.DMA_SEMS:
            if s_ in self.sems:
                self.wait("sync", s_, self.sems[s_][1] or None)
        for e in self.ENGS:
            self._pre(e)
            self.inc(e, "bar", 1)
        tgt = self.n("bar")
        for e in self.ENGS:
            self.wait(e, "bar", tgt)

    def drain(self, eng="vector"):
        self.q[eng].append(lambda e: e.drain())
        self._dirty[eng] = False

    def emit(self):
        with self.nc.Block() as block:
            @block.sync
            def _(e):
                for fn in self.q["sync"]:
                    fn(e)

            @block.vector
            def _(e):
                for fn in self.q["vector"]:
                    fn(e)

            @block.scalar
            def _(e):
                for fn in self.q["scalar"]:
                    fn(e)

            @block.tensor
            def _(e):
                for fn in self.q["tensor"]:
                    fn(e)

            @block.gpsimd
            def _(e):
                for fn in self.q["gpsimd"]:
                    fn(e)


def build_nc(stage="full"):
    nc = bass.Bass("TRN2", target_bir_lowering=False)

    x_in = nc.dram_tensor("x", [S, D], F32, kind="ExternalInput").ap()
    wgt = nc.dram_tensor("wgT", [D, INNER], F32, kind="ExternalInput").ap()
    wut = nc.dram_tensor("wuT", [D, INNER], F32, kind="ExternalInput").ap()
    wdt = nc.dram_tensor("wdT", [INNER, D], F32, kind="ExternalInput").ap()
    wqt = nc.dram_tensor("wqT", [D, D], F32, kind="ExternalInput").ap()
    wkt = nc.dram_tensor("wkT", [D, D], F32, kind="ExternalInput").ap()
    wvt = nc.dram_tensor("wvT", [D, D], F32, kind="ExternalInput").ap()
    wo_d = nc.dram_tensor("wo", [D, D], F32, kind="ExternalInput").ap()
    gam_d = nc.dram_tensor("gam", [1, D], F32, kind="ExternalInput").ap()
    bet_d = nc.dram_tensor("bet", [1, D], F32, kind="ExternalInput").ap()
    y_out = nc.dram_tensor("y", [S, D], F32, kind="ExternalOutput").ap()

    Xs = nc.dram_tensor("Xs", [S, D], F32, kind="Internal").ap()
    K2s = nc.dram_tensor("K2s", [NB, 128, INNER], BF16, kind="Internal").ap()
    Mgs = nc.dram_tensor("Mgs", [128, 8 * INNER], BF16, kind="Internal").ap()
    Mus = nc.dram_tensor("Mus", [128, 8 * INNER], BF16, kind="Internal").ap()
    Mds = nc.dram_tensor("Mds", [128, 32 * D], BF16, kind="Internal").ap()

    p = Prog(nc)
    for s_ in ("lw", "lx0", "lx1", "lk0", "lk1", "stK", "stX0", "stX1", "stM",
               "v", "sc", "t", "g", "bar", "sd"):
        p.sem(s_)

    al = nc.alloc_sbuf_tensor
    alp = nc.alloc_psum_tensor
    _uid = [0]

    def _uniq(n):
        _uid[0] += 1
        return f"{n}_{_uid[0]}"


    psA = alp("psA", [128, 1024], F32).ap()
    psB = alp("psB", [128, 1024], F32).ap()
    psC = alp("psC", [128, 1024], F32).ap()
    psD = alp("psD", [128, 1024], F32).ap()

    cst = al("cst", [128, 8], F32).ap()
    C_HALF = cst[:, 0:1]
    C_NEGH = cst[:, 1:2]
    C_EPS = cst[:, 2:3]
    C_I2PI = cst[:, 3:4]
    C_PIH = cst[:, 4:5]
    C_LNA = cst[:, 5:6]
    C_128 = cst[:, 6:7]
    ones128 = al("ones128", [128, 1], F32).ap()
    ones1r = al("ones1r", [1, 128], F32).ap()
    idf = al("idf", [128, 128], F32).ap()
    idb = al("idb", [128, 128], BF16).ap()
    wmg = al("wmg", [128, 1], F32).ap()
    wmu = al("wmu", [128, 1], F32).ap()
    wmd = al("wmd", [128, 1], F32).ap()
    mu1 = al("mu1", [128, NB], F32).ap()
    rs1 = al("rs1", [128, NB], F32).ap()
    mu2 = al("mu2", [128, NB], F32).ap()
    rs2 = al("rs2", [128, NB], F32).ap()
    mu3 = al("mu3", [128, NB], F32).ap()
    rs3 = al("rs3", [128, NB], F32).ap()
    f2b1 = al("f2b1", [128, NB], F32).ap()
    f2b2 = al("f2b2", [128, NB], F32).ap()
    stt = al("stt", [128, 8], F32).ap()
    dump = al("dump", [128, D], F32).ap()

    def ln_stats_tile(xtile, mu_c, rs_c):
        p.op("vector", lambda e: e.tensor_reduce(stt[:, 0:1], xtile[:], AX.X, ALU.add))
        p.drain()
        p.op("vector", lambda e: e.tensor_single_scalar(mu_c, stt[:, 0:1], 1.0 / D, ALU.mult))
        p.drain()
        p.op("vector", lambda e: e.tensor_tensor(stt[:, 2:3], mu_c, mu_c, ALU.mult), sig="v")
        vt = p.n("v")
        p.op("scalar", lambda e: e.activation(dump[:], xtile[:], ACT.Square,
                                              accum_out=stt[:, 1:2]))
        p.wait("scalar", "v", vt)
        p.op("scalar", lambda e: e.copy(stt[:, 3:4], stt[:, 1:2]), sig="sc")
        p.wait("vector", "sc", p.n("sc"))
        p.drain()
        p.op("vector", lambda e: e.scalar_tensor_tensor(stt[:, 4:5], stt[:, 3:4], 1.0 / D,
                                                        stt[:, 2:3], ALU.mult, ALU.subtract),
             sig="v")
        p.wait("scalar", "v", p.n("v"))
        p.op("scalar", lambda e: e.activation(stt[:, 5:6], stt[:, 4:5], ACT.Ln, bias=C_EPS))
        p.op("scalar", lambda e: e.activation(rs_c, stt[:, 5:6], ACT.Exp, scale=C_NEGH),
             sig="sc")
        p.wait("vector", "sc", p.n("sc"))

    def round_to_ints(src, scale_ap, ti, cf, ei, tg, out, width):
        p.wait("scalar", "v", p.n("v"))
        p.op("scalar", lambda e: e.activation(ti[:, 0:width], src, ACT.Identity,
                                              scale=scale_ap, bias=C_HALF))
        p.op("scalar", lambda e: e.copy(cf[:, 0:width], ti[:, 0:width]), sig="sc")
        p.wait("vector", "sc", p.n("sc"))
        p.op("vector", lambda e: e.scalar_tensor_tensor(ei[:, 0:width], src, scale_ap,
                                                        cf[:, 0:width], ALU.mult, ALU.subtract))
        p.op("vector", lambda e: e.tensor_single_scalar(tg[:, 0:width], ei[:, 0:width],
                                                        -0.5, ALU.is_lt))
        p.op("vector", lambda e: e.tensor_tensor(out, cf[:, 0:width], tg[:, 0:width],
                                                 ALU.subtract), sig="v")
        return p.n("v")

    def prelude():
        for i, v in enumerate([0.5, -0.5, EPS, 1.0 / TWO_PI, np.pi / 2,
                               -math.log(10.0) / 8.0, 128.0]):
            p.op("vector", lambda e, i=i, v=v: e.memset(cst[:, i:i + 1], v))
        p.op("vector", lambda e: e.memset(ones128[:], 1.0))
        p.op("vector", lambda e: e.memset(ones1r[:], 1.0))
        p.op("gpsimd", lambda e: e.iota(idf[:], [[1, 128]], base=0, channel_multiplier=-1,
                                        allow_small_or_imprecise_dtypes=True), sig="g")
        p.wait("vector", "g", p.n("g"))
        p.op("vector", lambda e: e.tensor_single_scalar(idf[:], idf[:], 0.0, ALU.is_equal))
        p.op("vector", lambda e: e.tensor_copy(idb[:], idf[:]), sig="v")
        p.wait("tensor", "v", p.n("v"))
        p.wait("scalar", "v", p.n("v"))

    def ln1_stats():
        with nc.sbuf_tensor(_uniq("p0a_x"), [128, D], F32) as xg:
            xt = xg.ap()
            for b in range(NB):
                ldt = p.dma("sync", xt[:], x_in[b * 128:(b + 1) * 128, :], sig="lw")
                p.wait("vector", "lw", ldt)
                p.wait("scalar", "lw", ldt)
                ln_stats_tile(xt, mu1[:, b:b + 1], rs1[:, b:b + 1])
                p.inc("vector", "v", 1)
                p.wait("sync", "v", p.n("v"))
        p.barrier()

    def ternarize(src, n_chunks, cols, wm_tile, dst_sbuf, dst_hbm):
        with contextlib.ExitStack() as st_:
            wstg = st_.enter_context(nc.sbuf_tensor(_uniq("p0_wstg"), [128, cols], F32)).ap()
            mtmp = st_.enter_context(nc.sbuf_tensor(_uniq("p0_mtmp"), [128, cols], BF16)).ap()
            acc = st_.enter_context(nc.sbuf_tensor(_uniq("p0_acc"), [128, 1], F32)).ap()
            w11 = st_.enter_context(nc.sbuf_tensor(_uniq("p0_w11"), [1, 1], F32)).ap()
            ti = st_.enter_context(nc.sbuf_tensor(_uniq("p0_ti"), [128, cols], I16)).ap()
            cf = st_.enter_context(nc.sbuf_tensor(_uniq("p0_cf"), [128, cols], BF16)).ap()
            tg = st_.enter_context(nc.sbuf_tensor(_uniq("p0_tg"), [128, cols], BF16)).ap()
            r_ = st_.enter_context(nc.sbuf_tensor(_uniq("p0_r"), [128, cols], F32)).ap()
            isc = st_.enter_context(nc.sbuf_tensor(_uniq("p0_isc"), [128, 1], F32)).ap()

            p.op("vector", lambda e: e.memset(acc[:], 0.0))
            for c_ in range(n_chunks):
                ldt = p.dma("sync", wstg[:], src[c_ * 128:(c_ + 1) * 128, :], sig="lw")
                p.wait("vector", "lw", ldt)
                p.op("vector", lambda e: e.tensor_reduce(stt[:, 0:1], wstg[:], AX.X,
                                                         ALU.add, apply_absolute_value=True))
                p.drain()
                p.op("vector", lambda e: e.tensor_tensor(acc[:], acc[:], stt[:, 0:1],
                                                         ALU.add), sig="v")
                p.wait("sync", "v", p.n("v"))
            p.drain()
            p.inc("vector", "v", 1)
            p.wait("tensor", "v", p.n("v"))
            p.op("tensor", lambda e: e.matmul(psA[0:1, 0:1], ones128[:], acc[:],
                                              start=True, stop=True), sig="t")
            p.wait("vector", "t", p.n("t"))
            ntot = float(n_chunks * 128 * cols)
            p.op("vector", lambda e: e.tensor_copy(w11[:], psA[0:1, 0:1]))
            p.drain()
            p.op("vector", lambda e: e.tensor_scalar(w11[:], w11[:], 1.0 / ntot, 1e-5,
                                                     ALU.mult, ALU.max), sig="v")
            p.wait("tensor", "v", p.n("v"))
            p.op("tensor", lambda e: e.matmul(psA[0:128, 1:2], ones1r[:], w11[:],
                                              start=True, stop=True), sig="t")
            p.wait("vector", "t", p.n("t"))
            p.op("vector", lambda e: e.tensor_copy(wm_tile[:], psA[0:128, 1:2]))
            p.drain()
            p.op("vector", lambda e: e.reciprocal(isc[:], wm_tile[:]))
            p.drain()
            p.inc("vector", "v", 1)
            vsc = p.n("v")
            p.wait("scalar", "v", vsc)

            for c_ in range(n_chunks):
                ldt = p.dma("sync", wstg[:], src[c_ * 128:(c_ + 1) * 128, :], sig="lw")
                p.wait("vector", "lw", ldt)
                p.wait("scalar", "lw", ldt)
                vq = round_to_ints(wstg[:], isc[:, 0:1], ti, cf, r_, tg, mtmp[:], cols)
                p.wait("vector", "v", vq)
                p.op("vector", lambda e: e.tensor_scalar(mtmp[:], mtmp[:], 1.0, -1.0,
                                                         ALU.min, ALU.max), sig="v")
                vm = p.n("v")
                if dst_sbuf is not None:
                    p.op("vector", lambda e, c_=c_: e.tensor_copy(
                        dst_sbuf[:, c_ * cols:(c_ + 1) * cols], mtmp[:]), sig="v")
                    vm = p.n("v")
                p.wait("scalar", "v", vm)
                sdt = p.dma("scalar", dst_hbm[:, c_ * cols:(c_ + 1) * cols], mtmp[:],
                            sig="stM")
                p.wait("sync", "stM", sdt)
                p.wait("vector", "stM", sdt)

    def ffn_phase(src_hbm, dst_hbm, mu_t, rs_t, f2_t, mg, mu_, post_stats):
        HW2 = INNER // 2
        with contextlib.ExitStack() as st_:
            sb = lambda n, sh, dt: st_.enter_context(nc.sbuf_tensor(_uniq(n), sh, dt)).ap()
            if mg is None:
                mg = sb("fa_mg", [128, 8 * INNER], BF16)
                mu_ = sb("fa_mu", [128, 8 * INNER], BF16)
                for c_ in range(8):
                    p.dma("sync", mg[:, c_ * INNER:(c_ + 1) * INNER],
                          Mgs[:, c_ * INNER:(c_ + 1) * INNER], sig="lw")
                for c_ in range(8):
                    p.dma("sync", mu_[:, c_ * INNER:(c_ + 1) * INNER],
                          Mus[:, c_ * INNER:(c_ + 1) * INNER], sig="lw")
                p.wait("tensor", "lw", p.n("lw"))
            h_t = sb("fa_h", [128, INNER], F32)
            gsm = sb("fa_g", [128, 512], F32)
            xin = sb("fa_xin", [128, 2 * D], F32)
            xn = sb("fa_xn", [128, D], F32)
            ti1 = sb("fa_ti1", [128, D], I16)
            cf1 = sb("fa_cf1", [128, D], BF16)
            tg1 = ti1.bitcast(BF16)
            e1 = h_t[:, 0:D]
            k1 = sb("fa_k1", [128, D], BF16)
            k1T = sb("fa_k1T", [128, D], BF16)
            HW4 = INNER // 4
            ti2 = sb("fa_ti2", [128, HW4], I16)
            cf2 = sb("fa_cf2", [128, HW4], BF16)
            tg2 = ti2.bitcast(BF16)
            e2 = sb("fa_e2", [128, HW4], F32)
            k2h = sb("fa_k2h", [128, HW4], BF16)
            k2T = sb("fa_k2T", [128, INNER], BF16)
            sc1c = sb("fa_sc1", [128, 1], F32)
            sc2c = sb("fa_sc2", [128, 1], F32)
            f1g = sb("fa_f1g", [128, 1], F32)
            f1u = sb("fa_f1u", [128, 1], F32)

            banks = [psA[:, 0:512], psA[:, 512:1024], psB[:, 0:512], psB[:, 512:1024],
                     psC[:, 0:512], psC[:, 512:1024], psD[:, 0:512], psD[:, 512:1024]]
            pair_evac = [None, None]
            k2_store = [None, None]
            xin_free = [None, None]

            for b in range(NB):
                xb = xin[:, (b % 2) * D:(b % 2) * D + D]
                lxs = "lx%d" % (b % 2)
                p.wait("sync", "v", xin_free[b % 2])
                ldt = p.dma("sync", xb[:], src_hbm[b * 128:(b + 1) * 128, :], sig=lxs)
                p.wait("vector", lxs, ldt)
                p.op("vector", lambda e, b=b, xb=xb: e.tensor_scalar(
                    xn[:], xb[:], mu_t[:, b:b + 1], rs_t[:, b:b + 1],
                    ALU.subtract, ALU.mult), sig="v")
                xin_free[b % 2] = p.n("v")
                p.op("vector", lambda e: e.tensor_reduce(stt[:, 0:1], xn[:], AX.X, ALU.max,
                                                         apply_absolute_value=True))
                p.drain()
                p.op("vector", lambda e: e.tensor_single_scalar(stt[:, 0:1], stt[:, 0:1],
                                                                1e-5, ALU.max))
                p.drain()
                p.op("vector", lambda e: e.scalar_tensor_tensor(f1g[:], stt[:, 0:1],
                                                                1.0 / 127.0, wmg[:],
                                                                ALU.mult, ALU.mult))
                p.op("vector", lambda e: e.scalar_tensor_tensor(f1u[:], stt[:, 0:1],
                                                                1.0 / 127.0, wmu[:],
                                                                ALU.mult, ALU.mult))
                p.op("vector", lambda e: e.reciprocal(stt[:, 1:2], stt[:, 0:1]))
                p.drain()
                p.op("vector", lambda e: e.tensor_single_scalar(sc1c[:], stt[:, 1:2],
                                                                127.0, ALU.mult))
                p.drain()
                p.inc("vector", "v", 1)
                p.wait("scalar", "v", p.n("v"))
                vk1 = round_to_ints(xn[:], sc1c[:, 0:1], ti1, cf1, e1, tg1, k1[:], D)
                p.wait("tensor", "v", vk1)
                psAb = psA.bitcast(BF16)
                for c_ in range(8):
                    fn = lambda e, c_=c_, psAb=psAb: e.transpose(
                        psAb[:, c_ * 128:(c_ + 1) * 128],
                        k1[:, c_ * 128:(c_ + 1) * 128], idb[:])
                    p.op("tensor", fn, sig="t" if c_ == 7 else None)
                p.wait("vector", "t", p.n("t"))
                p.op("vector", lambda e, psAb=psAb: e.tensor_copy(k1T[:], psAb[:, 0:1024]),
                     sig="v")
                p.wait("tensor", "v", p.n("v"))

                for pair in range(4):
                    grp = pair % 2
                    g0, u0, g1, u1 = banks[grp * 4: grp * 4 + 4]
                    ot0, ot1 = 2 * pair, 2 * pair + 1
                    p.wait("tensor", "v", pair_evac[grp])
                    for i in range(8):
                        for j, (dst, wsrc, ot) in enumerate(((g0, mg, ot0), (u0, mu_, ot0),
                                                             (g1, mg, ot1), (u1, mu_, ot1))):
                            last = (i == 7 and j == 3)
                            p.op("tensor", lambda e, dst=dst, i=i, wsrc=wsrc, ot=ot:
                                 e.matmul(dst, k1T[:, i * 128:(i + 1) * 128],
                                          wsrc[:, i * INNER + ot * 512:
                                               i * INNER + ot * 512 + 512],
                                          start=(i == 0), stop=(i == 7)),
                                 sig="t" if last else None)
                    tt = p.n("t")
                    p.wait("scalar", "t", tt)
                    p.wait("vector", "t", tt)
                    for gbank, ubank, ot in ((g0, u0, ot0), (g1, u1, ot1)):
                        p.op("scalar", lambda e, gbank=gbank: e.activation(
                            gsm[:], gbank, ACT.Silu, scale=f1g[:, 0:1]), sig="sc")
                        p.wait("vector", "sc", p.n("sc"))
                        p.op("vector", lambda e, ubank=ubank, ot=ot: e.scalar_tensor_tensor(
                            h_t[:, ot * 512:(ot + 1) * 512], ubank, f1u[:, 0:1], gsm[:],
                            ALU.mult, ALU.mult), sig="v")
                        p.wait("scalar", "v", p.n("v"))
                    pair_evac[grp] = p.n("v")
                p.wait("vector", "t", p.n("t"))

                p.op("vector", lambda e: e.tensor_reduce(stt[:, 0:1], h_t[:], AX.X, ALU.max,
                                                         apply_absolute_value=True))
                p.drain()
                p.op("vector", lambda e: e.tensor_single_scalar(stt[:, 0:1], stt[:, 0:1],
                                                                1e-5, ALU.max))
                p.drain()
                p.op("vector", lambda e, b=b: e.scalar_tensor_tensor(
                    f2_t[:, b:b + 1], stt[:, 0:1], 0.5 / 127.0, wmd[:], ALU.mult, ALU.mult))
                p.op("vector", lambda e: e.reciprocal(stt[:, 1:2], stt[:, 0:1]))
                p.drain()
                p.op("vector", lambda e: e.tensor_single_scalar(sc2c[:], stt[:, 1:2],
                                                                127.0, ALU.mult))
                p.drain()
                p.inc("vector", "v", 1)
                p.wait("scalar", "v", p.n("v"))
                p.wait("vector", "stK", k2_store[0])
                for hh in range(4):
                    hsl = h_t[:, hh * HW4:(hh + 1) * HW4]
                    vq = round_to_ints(hsl, sc2c[:, 0:1], ti2, cf2, e2, tg2, k2h[:], HW4)
                    p.wait("tensor", "v", vq)
                    psAb2 = psA.bitcast(BF16)
                    for c_ in range(8):
                        fn = lambda e, c_=c_, psAb2=psAb2: e.transpose(
                            psAb2[:, c_ * 128:(c_ + 1) * 128],
                            k2h[:, c_ * 128:(c_ + 1) * 128], idb[:])
                        p.op("tensor", fn, sig="t" if c_ == 7 else None)
                    p.wait("vector", "t", p.n("t"))
                    p.op("vector", lambda e, hh=hh, psAb2=psAb2: e.tensor_copy(
                        k2T[:, hh * HW4:(hh + 1) * HW4], psAb2[:, 0:1024]), sig="v")
                    p.wait("tensor", "v", p.n("v"))
                p.wait("scalar", "v", p.n("v"))
                k2_store[0] = p.dma("scalar", K2s[b], k2T[:], sig="stK")
            p.barrier()

        with contextlib.ExitStack() as st_:
            sb = lambda n, sh, dt: st_.enter_context(nc.sbuf_tensor(_uniq(n), sh, dt)).ap()
            md = sb("fb_md", [128, 32 * D], BF16)
            k2i = sb("fb_k2i", [128, 2 * INNER], BF16)
            xin2 = sb("fb_xin2", [128, 2 * D], F32)
            xo = sb("fb_xo", [128, 2 * D], F32)
            for c_ in range(8):
                p.dma("sync", md[:, c_ * 4 * D:(c_ + 1) * 4 * D],
                      Mds[:, c_ * 4 * D:(c_ + 1) * 4 * D], sig="lw")
            p.wait("tensor", "lw", p.n("lw"))
            mm_done = [None, None]
            xo_free = [None, None]
            k2_free = [None, None]
            xin_free = [None, None]
            for b in range(NB):
                kb = k2i[:, (b % 2) * INNER:(b % 2) * INNER + INNER]
                xb = xin2[:, (b % 2) * D:(b % 2) * D + D]
                xob = xo[:, (b % 2) * D:(b % 2) * D + D]
                lks = "lk%d" % (b % 2)
                lxs = "lx%d" % (b % 2)
                p.wait("sync", "t", k2_free[b % 2])
                l1 = p.dma("sync", kb[:], K2s[b], sig=lks)
                p.wait("sync", "v", xin_free[b % 2])
                l2 = p.dma("sync", xb[:], src_hbm[b * 128:(b + 1) * 128, :], sig=lxs)
                p.wait("tensor", lks, l1)
                p.wait("tensor", "v", mm_done[0])
                for oc in range(32):
                    for d2 in range(2):
                        last = (oc == 31 and d2 == 1)
                        p.op("tensor", lambda e, oc=oc, d2=d2, kb=kb: e.matmul(
                            psA[:, d2 * 512:(d2 + 1) * 512],
                            kb[:, oc * 128:(oc + 1) * 128],
                            md[:, oc * D + d2 * 512: oc * D + d2 * 512 + 512],
                            start=(oc == 0), stop=(oc == 31)),
                            sig="t" if last else None)
                k2_free[b % 2] = p.n("t")
                p.wait("vector", "t", p.n("t"))
                p.wait("vector", lxs, l2)
                p.wait("vector", "stX%d" % (b % 2), xo_free[b % 2])
                p.op("vector", lambda e, b=b, xb=xb, xob=xob: e.scalar_tensor_tensor(
                    xob[:], psA[:], f2_t[:, b:b + 1], xb[:], ALU.mult, ALU.add), sig="v")
                mm_done[0] = p.n("v")
                xin_free[b % 2] = p.n("v")
                p.wait("scalar", "v", p.n("v"))
                if post_stats is not None:
                    mun, rsn = post_stats
                    ln_stats_tile(xob, mun[:, b:b + 1], rsn[:, b:b + 1])
                xo_free[b % 2] = p.dma("scalar", dst_hbm[b * 128:(b + 1) * 128, :], xob[:],
                                       sig="stX%d" % (b % 2))
            p.barrier()

    def attention(src_hbm, dst_hbm, mu_t, rs_t, post_stats):
        with contextlib.ExitStack() as st_:
            sb = lambda n, sh, dt: st_.enter_context(nc.sbuf_tensor(_uniq(n), sh, dt)).ap()
            sintab = sb("at_sin", [128, NB * 32], F32)
            costab = sb("at_cos", [128, NB * 32], F32)
            dmask = sb("at_dmask", [128, H * 128], F32)
            mstab = sb("at_ms", [128, 8 * 128], F32)
            mbc = sb("at_mbc", [128, H], F32)
            scc = sb("at_scc", [128, H], F32)
            lns = sb("at_lns", [128, H], F32)
            lns2 = sb("at_lns2", [128, 8], F32)
            svals = sb("at_sv", [128, H], F32)
            gbc = sb("at_gbc", [128, D], F32)
            bbc = sb("at_bbc", [128, D], F32)
            grow = sb("at_grow", [1, D], F32)
            brow = sb("at_brow", [1, D], F32)
            ssb = sb("at_ssb", [128, H * DH], F32)
            wstg = sb("at_wstg", [128, 8 * D], F32)
            xnT = sb("at_xnT", [128, BPE * D], F32)
            vq_t = sb("at_vq", [128, BPE * D], F32)
            qT = sb("at_qT", [128, BPE * D], F32)
            kT = sb("at_kT", [128, BPE * D], F32)
            kbq = sb("at_kbq", [128, BPE * D], F32)
            xab = sb("at_xab", [128, 2 * D], F32)
            xn2 = sb("at_xn2", [128, D], F32)
            rot = sb("at_rot", [128, D], F32)
            adt = sb("at_adt", [128, 128], F32)
            qst = sb("at_qst", [128, 128], F32)
            rr1 = sb("at_rr1", [128, D], F32)
            rr2 = sb("at_rr2", [128, D], F32)
            rrT = sb("at_rrT", [128, D], F32)
            gnq = sb("at_gnq", [128, H], F32)
            gnm = sb("at_gnm", [128, H], F32)
            gnr = sb("at_gnr", [128, H], F32)
            xo2 = sb("at_xo2", [128, 2 * D], F32)
            ta = sb("at_ta", [128, 1024], F32)
            tb_ = sb("at_tb", [128, 1024], F32)
            ti3 = sb("at_ti3", [128, 1024], I16)

            g1 = p.dma("sync", grow[:], gam_d[:], sig="lw")
            g2 = p.dma("sync", brow[:], bet_d[:], sig="lw")
            p.wait("tensor", "lw", g2)
            for half in range(2):
                p.op("tensor", lambda e, half=half: e.matmul(
                    psA[:, half * 512:(half + 1) * 512], ones1r[:],
                    grow[0:1, half * 512:(half + 1) * 512], start=True, stop=True))
                p.op("tensor", lambda e, half=half: e.matmul(
                    psB[:, half * 512:(half + 1) * 512], ones1r[:],
                    brow[0:1, half * 512:(half + 1) * 512], start=True, stop=True),
                    sig="t" if half == 1 else None)
            p.wait("vector", "t", p.n("t"))
            p.op("vector", lambda e: e.tensor_copy(gbc[:], psA[:]))
            p.op("vector", lambda e: e.tensor_copy(bbc[:], psB[:]), sig="v")
            p.wait("tensor", "v", p.n("v"))

            for hh in range(H):
                p.op("vector", lambda e, hh=hh: e.memset(svals[:, hh:hh + 1], DECAY[hh]))
            p.inc("vector", "v", 1)
            p.wait("scalar", "v", p.n("v"))
            p.op("scalar", lambda e: e.activation(lns[:], svals[:], ACT.Ln), sig="sc")
            p.wait("vector", "sc", p.n("sc"))
            for pc in range(8):
                p.op("vector", lambda e, pc=pc: e.tensor_copy(lns2[0:64, pc:pc + 1],
                                                              lns[0:64, 2 * pc:2 * pc + 1]))
                p.op("vector", lambda e, pc=pc: e.tensor_copy(
                    lns2[64:128, pc:pc + 1], lns[64:128, 2 * pc + 1:2 * pc + 2]))
            p.inc("vector", "v", 1)
            p.wait("scalar", "v", p.n("v"))
            p.wait("gpsimd", "v", p.n("v"))
            p.op("gpsimd", lambda e: e.iota(ta[:, 0:128], [[1, 128]], base=0,
                                            channel_multiplier=-1,
                                            allow_small_or_imprecise_dtypes=True))
            p.op("gpsimd", lambda e: e.iota(ta[:, 128:256], [[1, 128]], base=1,
                                            channel_multiplier=0,
                                            allow_small_or_imprecise_dtypes=True))
            p.op("gpsimd", lambda e: e.iota(ta[:, 256:257], [[1, 1]], base=127,
                                            channel_multiplier=-1,
                                            allow_small_or_imprecise_dtypes=True))
            p.op("gpsimd", lambda e: e.iota(ta[:, 512:544], [[128, 32]], base=0,
                                            channel_multiplier=1,
                                            allow_small_or_imprecise_dtypes=True),
                 sig="g")
            gt_ = p.n("g")
            p.wait("vector", "g", gt_)
            p.wait("scalar", "g", gt_)
            p.op("vector", lambda e: e.tensor_single_scalar(tb_[:, 0:128], ta[:, 0:128],
                                                            0.0, ALU.is_ge), sig="v")
            p.wait("scalar", "v", p.n("v"))
            for hh in range(H):
                p.op("scalar", lambda e, hh=hh: e.activation(
                    dmask[:, hh * 128:(hh + 1) * 128], ta[:, 0:128], ACT.Exp,
                    scale=lns[:, hh:hh + 1]))
                p.op("scalar", lambda e, hh=hh: e.activation(
                    mbc[:, hh:hh + 1], ta[:, 256:257], ACT.Exp, scale=lns[:, hh:hh + 1]))
                p.op("scalar", lambda e, hh=hh: e.activation(
                    scc[:, hh:hh + 1], C_128, ACT.Exp, scale=lns[:, hh:hh + 1]))
            for pc in range(8):
                p.op("scalar", lambda e, pc=pc: e.activation(
                    mstab[:, pc * 128:(pc + 1) * 128], ta[:, 128:256], ACT.Exp,
                    scale=lns2[:, pc:pc + 1]), sig="sc" if pc == 7 else None)
            p.wait("vector", "sc", p.n("sc"))
            for hh in range(H):
                p.op("vector", lambda e, hh=hh: e.tensor_tensor(
                    dmask[:, hh * 128:(hh + 1) * 128], dmask[:, hh * 128:(hh + 1) * 128],
                    tb_[:, 0:128], ALU.mult))
            p.op("vector", lambda e: e.memset(ssb[:], 0.0))
            p.inc("vector", "v", 1)
            p.wait("gpsimd", "v", p.n("v"))
            p.op("gpsimd", lambda e: e.iota(tb_[:, 224:256], [[128, 32]], base=0,
                                            channel_multiplier=1,
                                            allow_small_or_imprecise_dtypes=True))
            p.op("gpsimd", lambda e: e.iota(tb_[:, 160:192], [[1, 32]], base=0,
                                            channel_multiplier=0,
                                            allow_small_or_imprecise_dtypes=True), sig="g")
            p.wait("scalar", "g", p.n("g"))
            p.op("scalar", lambda e: e.activation(tb_[:, 192:224], tb_[:, 160:192], ACT.Exp,
                                                  scale=C_LNA), sig="sc")
            p.wait("vector", "sc", p.n("sc"))
            p.op("vector", lambda e: e.tensor_tensor(
                ta[:].rearrange("p (b u) -> p b u", u=32),
                tb_[:, 192:224].rearrange("p (a u) -> p a u", a=1).broadcast_to([128, 32, 32]),
                tb_[:, 224:256].rearrange("p (b a) -> p b a", a=1).broadcast_to([128, 32, 32]),
                ALU.mult), sig="v")
            p.wait("scalar", "v", p.n("v"))

            def sincos(dst, phase_shift):
                p.op("vector", lambda e: e.tensor_single_scalar(tb_[:], ta[:],
                                                                phase_shift, ALU.add))
                p.op("vector", lambda e: e.tensor_single_scalar(rot[:], tb_[:],
                                                                1.0 / TWO_PI, ALU.mult),
                     sig="v")
                p.wait("scalar", "v", p.n("v"))
                p.op("scalar", lambda e: e.activation(ti3[:], rot[:], ACT.Identity,
                                                      bias=C_HALF))
                p.op("scalar", lambda e: e.copy(xn2[:], ti3[:]), sig="sc")
                p.wait("vector", "sc", p.n("sc"))
                p.op("vector", lambda e: e.tensor_tensor(rot[:], rot[:], xn2[:],
                                                         ALU.subtract))
                p.op("vector", lambda e: e.tensor_single_scalar(rot[:], rot[:], -0.5,
                                                                ALU.is_lt))
                p.op("vector", lambda e: e.tensor_tensor(xn2[:], xn2[:], rot[:],
                                                         ALU.subtract))
                p.op("vector", lambda e: e.scalar_tensor_tensor(tb_[:], xn2[:], -TWO_PI,
                                                                tb_[:], ALU.mult, ALU.add))
                p.op("vector", lambda e: e.tensor_scalar(tb_[:], tb_[:], PI_CLAMP,
                                                         -PI_CLAMP, ALU.min, ALU.max),
                     sig="v")
                p.wait("scalar", "v", p.n("v"))
                p.op("scalar", lambda e, dst=dst: e.activation(dst, tb_[:], ACT.Sin),
                     sig="sc")
                p.wait("vector", "sc", p.n("sc"))

            sincos(sintab, 0.0)
            sincos(costab, np.pi / 2)
            p.inc("scalar", "sc", 1)
            p.wait("tensor", "sc", p.n("sc"))
            p.wait("sync", "sc", p.n("sc"))

            s_dma = [None]
            s_copy = [None]
            wo_evac = [None, None]
            xo_free2 = [None, None]
            xab_free = [None, None]

            def load_w(src, gate_t):
                p.wait("sync", "t", gate_t)
                for i in range(8):
                    tgt = p.dma("sync", wstg[:, i * D:(i + 1) * D],
                                src[i * 128:(i + 1) * 128, :], sig="lw")
                return tgt

            wstg_gate = [None]
            for ei in range(NEI):
                blocks = [ei * BPE + k for k in range(BPE)]
                wl = load_w(wvt, wstg_gate[0])
                p.wait("tensor", "lw", wl)
                for lb, b in enumerate(blocks):
                    xb = xab[:, (lb % 2) * D:(lb % 2) * D + D]
                    lxs = "lx%d" % (lb % 2)
                    p.wait("sync", "v", xab_free[lb % 2])
                    ldt = p.dma("sync", xb[:], src_hbm[b * 128:(b + 1) * 128, :], sig=lxs)
                    p.wait("vector", lxs, ldt)
                    p.op("vector", lambda e, b=b, xb=xb: e.tensor_scalar(
                        xn2[:], xb[:], mu_t[:, b:b + 1], rs_t[:, b:b + 1],
                        ALU.subtract, ALU.mult), sig="v")
                    xab_free[lb % 2] = p.n("v")
                    p.wait("tensor", "v", p.n("v"))
                    for c_ in range(8):
                        p.op("tensor", lambda e, c_=c_: e.transpose(
                            psA[:, c_ * 128:(c_ + 1) * 128],
                            xn2[:, c_ * 128:(c_ + 1) * 128], idf[:]),
                            sig="t" if c_ == 7 else None)
                    p.wait("vector", "t", p.n("t"))
                    p.op("vector", lambda e, lb=lb: e.tensor_copy(
                        xnT[:, lb * D:(lb + 1) * D], psA[:]), sig="v")
                    p.wait("tensor", "v", p.n("v"))
                    for i in range(8):
                        for v2 in range(2):
                            p.op("tensor", lambda e, i=i, v2=v2, lb=lb: e.matmul(
                                psB[:, v2 * 512:(v2 + 1) * 512],
                                xnT[:, lb * D + i * 128: lb * D + (i + 1) * 128],
                                wstg[:, i * D + v2 * 512: i * D + v2 * 512 + 512],
                                start=(i == 0), stop=(i == 7)),
                                sig="t" if (i == 7 and v2 == 1) else None)
                    p.wait("vector", "t", p.n("t"))
                    p.op("vector", lambda e, lb=lb: e.tensor_copy(
                        vq_t[:, lb * D:(lb + 1) * D], psB[:]), sig="v")
                    p.wait("tensor", "v", p.n("v"))
                    wstg_gate[0] = p.n("t")

                for which, wsrc, dstT in (("q", wqt, qT), ("k", wkt, kT)):
                    wl = load_w(wsrc, wstg_gate[0])
                    p.wait("tensor", "lw", wl)
                    for lb, b in enumerate(blocks):
                        for i in range(8):
                            for v2 in range(2):
                                p.op("tensor", lambda e, i=i, v2=v2, lb=lb: e.matmul(
                                    psB[:, v2 * 512:(v2 + 1) * 512],
                                    xnT[:, lb * D + i * 128: lb * D + (i + 1) * 128],
                                    wstg[:, i * D + v2 * 512: i * D + v2 * 512 + 512],
                                    start=(i == 0), stop=(i == 7)),
                                    sig="t" if (i == 7 and v2 == 1) else None)
                        p.wait("vector", "t", p.n("t"))
                        wstg_gate[0] = p.n("t")
                        cb = costab[:, b * 32:(b + 1) * 32]
                        sb_ = sintab[:, b * 32:(b + 1) * 32]
                        bc = lambda t: t.rearrange("p (a u) -> p a u", a=1).broadcast_to(
                            [128, H, 32])
                        x0 = psB[:].rearrange("p (h c u) -> p h c u", h=H, c=2)[:, :, 0, :]
                        x1 = psB[:].rearrange("p (h c u) -> p h c u", h=H, c=2)[:, :, 1, :]
                        r0 = rot[:].rearrange("p (h c u) -> p h c u", h=H, c=2)[:, :, 0, :]
                        r1 = rot[:].rearrange("p (h c u) -> p h c u", h=H, c=2)[:, :, 1, :]
                        ta0 = ta[:, 0:512].rearrange("p (h u) -> p h u", h=H)
                        ta1 = ta[:, 512:1024].rearrange("p (h u) -> p h u", h=H)
                        p.op("vector", lambda e, x0=x0, cb=cb, ta0=ta0: e.tensor_tensor(
                            ta0, x0, bc(cb), ALU.mult))
                        p.op("vector", lambda e, x1=x1, sb_=sb_, ta1=ta1: e.tensor_tensor(
                            ta1, x1, bc(sb_), ALU.mult))
                        p.op("vector", lambda e, r0=r0, ta0=ta0, ta1=ta1: e.tensor_tensor(
                            r0, ta0, ta1, ALU.subtract))
                        p.op("vector", lambda e, x0=x0, sb_=sb_, ta0=ta0: e.tensor_tensor(
                            ta0, x0, bc(sb_), ALU.mult))
                        p.op("vector", lambda e, x1=x1, cb=cb, ta1=ta1: e.tensor_tensor(
                            ta1, x1, bc(cb), ALU.mult))
                        p.op("vector", lambda e, r1=r1, ta0=ta0, ta1=ta1: e.tensor_tensor(
                            r1, ta0, ta1, ALU.add), sig="v")
                        vrot = p.n("v")
                        p.wait("tensor", "v", vrot)
                        if which == "k":
                            p.op("vector", lambda e, lb=lb: e.tensor_tensor(
                                kbq[:, lb * D:(lb + 1) * D].rearrange(
                                    "p (h u) -> p h u", h=H),
                                rot[:].rearrange("p (h u) -> p h u", h=H),
                                mbc[:].rearrange("p (h a) -> p h a", a=1).broadcast_to(
                                    [128, H, DH]), ALU.mult), sig="v")
                        for c_ in range(8):
                            p.op("tensor", lambda e, c_=c_: e.transpose(
                                psA[:, c_ * 128:(c_ + 1) * 128],
                                rot[:, c_ * 128:(c_ + 1) * 128], idf[:]),
                                sig="t" if c_ == 7 else None)
                        p.wait("vector", "t", p.n("t"))
                        p.op("vector", lambda e, lb=lb, dstT=dstT: e.tensor_copy(
                            dstT[:].rearrange("p (c t) -> p c t", c=8)[:, :, lb * 128:
                                                                      (lb + 1) * 128],
                            psA[:].rearrange("p (c t) -> p c t", c=8)), sig="v")
                        p.wait("tensor", "v", p.n("v"))

                wl = load_w(wo_d, wstg_gate[0])

                for lc, b in enumerate(blocks):
                    c_glob = b
                    first = (c_glob == 0)
                    p.wait("tensor", "lw", wl)
                    p.wait("sync", "sd", s_dma[0])
                    p.wait("tensor", "sd", s_dma[0])
                    for hh in range(H):
                        pc = hh // 2
                        base = 64 * (hh % 2)
                        atile = psB[:, (hh % 2) * 512:(hh % 2) * 512 + 128]
                        ksl = kT[base:base + 64, pc * 512 + lc * 128: pc * 512 + (lc + 1) * 128]
                        qsl = qT[base:base + 64, pc * 512 + lc * 128: pc * 512 + (lc + 1) * 128]
                        p.op("tensor", lambda e, atile=atile, ksl=ksl, qsl=qsl: e.matmul(
                            atile, ksl, qsl, start=True, stop=True), sig="t")
                        p.wait("vector", "t", p.n("t"))
                        p.op("vector", lambda e, atile=atile, hh=hh: e.tensor_tensor(
                            adt[:], atile, dmask[:, hh * 128:(hh + 1) * 128], ALU.mult),
                            sig="v")
                        if not first:
                            msl = mstab[base:base + 64, pc * 128:(pc + 1) * 128]
                            p.op("vector", lambda e, qsl=qsl, msl=msl, base=base:
                                 e.tensor_tensor(qst[base:base + 64, :], qsl, msl,
                                                 ALU.mult), sig="v")
                        vq_ = p.n("v")
                        p.wait("tensor", "v", vq_)
                        vsl = vq_t[:, lc * D + hh * DH: lc * D + (hh + 1) * DH]
                        p.op("tensor", lambda e, hh=hh, vsl=vsl, first=first: e.matmul(
                            psC[:, hh * DH:(hh + 1) * DH], adt[:], vsl,
                            start=True, stop=first), sig="t" if first else None)
                        if not first:
                            p.op("tensor", lambda e, hh=hh, base=base: e.matmul(
                                psC[:, hh * DH:(hh + 1) * DH], qst[base:base + 64, :],
                                ssb[base:base + 64, hh * DH:(hh + 1) * DH],
                                start=False, stop=True), sig="t")
                    t_heads = p.n("t")
                    p.wait("vector", "t", t_heads)
                    if not first:
                        p.op("vector", lambda e: e.tensor_tensor(
                            psD[0:64, :].rearrange("p (h u) -> p h u", h=H),
                            psD[0:64, :].rearrange("p (h u) -> p h u", h=H),
                            scc[0:64, :].rearrange("p (h a) -> p h a", a=1).broadcast_to(
                                [64, H, DH]), ALU.mult), sig="v")
                        p.wait("tensor", "v", p.n("v"))
                    for hh in range(H):
                        ksl2 = kbq[:, lc * D + hh * DH: lc * D + (hh + 1) * DH]
                        vsl2 = vq_t[:, lc * D + hh * DH: lc * D + (hh + 1) * DH]
                        st_flag = first and (hh % 8 == 0)
                        sp_flag = (hh % 8 == 7)
                        p.op("tensor", lambda e, hh=hh, ksl2=ksl2, vsl2=vsl2,
                             st_flag=st_flag, sp_flag=sp_flag, sk=not first: e.matmul(
                                 psD[0:64, hh * DH:(hh + 1) * DH], ksl2, vsl2,
                                 start=st_flag, stop=sp_flag, skip_group_check=sk),
                             sig="t" if hh == H - 1 else None)
                    p.wait("vector", "t", p.n("t"))
                    p.op("vector", lambda e: e.tensor_copy(ssb[0:64, :], psD[0:64, :]),
                         sig="v")
                    s_copy[0] = p.n("v")
                    p.wait("sync", "v", s_copy[0])
                    s_dma[0] = p.dma("sync", ssb[64:128, :], ssb[0:64, :], sig="sd")

                    p.wait("scalar", "t", t_heads)
                    p.op("scalar", lambda e: e.activation(dump[:], psC[:], ACT.Square),
                         sig="sc")
                    p.wait("vector", "sc", p.n("sc"))
                    p.op("vector", lambda e: e.tensor_reduce(
                        gnm[:], psC[:].rearrange("p (h u) -> p h u", h=H), AX.X, ALU.add))
                    p.op("vector", lambda e: e.tensor_reduce(
                        gnq[:], dump[:].rearrange("p (h u) -> p h u", h=H), AX.X, ALU.add))
                    p.drain()
                    p.op("vector", lambda e: e.tensor_single_scalar(gnm[:], gnm[:],
                                                                    1.0 / DH, ALU.mult))
                    p.drain()
                    p.op("vector", lambda e: e.tensor_tensor(ta[:, 0:H], gnm[:], gnm[:],
                                                             ALU.mult))
                    p.drain()
                    p.op("vector", lambda e: e.scalar_tensor_tensor(
                        gnr[:], gnq[:], 1.0 / DH, ta[:, 0:H], ALU.mult, ALU.subtract),
                        sig="v")
                    p.wait("scalar", "v", p.n("v"))
                    p.op("scalar", lambda e: e.activation(gnr[:], gnr[:], ACT.Ln,
                                                          bias=C_EPS))
                    p.op("scalar", lambda e: e.activation(gnr[:], gnr[:], ACT.Exp,
                                                          scale=C_NEGH), sig="sc")
                    p.wait("vector", "sc", p.n("sc"))
                    hview = lambda t: t.rearrange("p (h u) -> p h u", h=H)
                    hbc = lambda t: t.rearrange("p (h a) -> p h a", a=1).broadcast_to(
                        [128, H, DH])
                    p.op("vector", lambda e: e.tensor_tensor(hview(rr1[:]), psC[:].rearrange(
                        "p (h u) -> p h u", h=H), hbc(gnm[:]), ALU.subtract))
                    p.op("vector", lambda e: e.tensor_tensor(hview(rr2[:]), hview(gbc[:]),
                                                             hbc(gnr[:]), ALU.mult))
                    p.op("vector", lambda e: e.tensor_tensor(rr1[:], rr1[:], rr2[:],
                                                             ALU.mult))
                    p.op("vector", lambda e: e.tensor_tensor(rr1[:], rr1[:], bbc[:],
                                                             ALU.add), sig="v")
                    p.wait("tensor", "v", p.n("v"))
                    for c_ in range(8):
                        p.op("tensor", lambda e, c_=c_: e.transpose(
                            psA[:, c_ * 128:(c_ + 1) * 128], rr1[:, c_ * 128:(c_ + 1) * 128],
                            idf[:]), sig="t" if c_ == 7 else None)
                    p.wait("vector", "t", p.n("t"))
                    p.op("vector", lambda e: e.tensor_copy(rrT[:], psA[:]), sig="v")
                    p.wait("tensor", "v", p.n("v"))
                    for i in range(8):
                        for d2 in range(2):
                            p.op("tensor", lambda e, i=i, d2=d2: e.matmul(
                                psA[:, d2 * 512:(d2 + 1) * 512], rrT[:, i * 128:(i + 1) * 128],
                                wstg[:, i * D + d2 * 512: i * D + d2 * 512 + 512],
                                start=(i == 0), stop=(i == 7)),
                                sig="t" if (i == 7 and d2 == 1) else None)
                    wo_t = p.n("t")
                    wstg_gate[0] = wo_t
                    xb2 = xab[:, (lc % 2) * D:(lc % 2) * D + D]
                    lxs2 = "lx%d" % (lc % 2)
                    p.wait("sync", "v", xab_free[lc % 2])
                    l2 = p.dma("sync", xb2[:], src_hbm[b * 128:(b + 1) * 128, :], sig=lxs2)
                    p.wait("vector", "t", wo_t)
                    p.wait("vector", lxs2, l2)
                    xob2 = xo2[:, (lc % 2) * D:(lc % 2) * D + D]
                    p.wait("vector", "stX%d" % (lc % 2), xo_free2[lc % 2])
                    p.op("vector", lambda e, xb2=xb2, xob2=xob2: e.tensor_tensor(
                        xob2[:], psA[:], xb2[:], ALU.add), sig="v")
                    xab_free[lc % 2] = p.n("v")
                    wo_evac[0] = p.n("v")
                    p.wait("scalar", "v", p.n("v"))
                    if post_stats is not None:
                        mun, rsn = post_stats
                        ln_stats_tile(xob2, mun[:, b:b + 1], rsn[:, b:b + 1])
                    xo_free2[lc % 2] = p.dma("scalar", dst_hbm[b * 128:(b + 1) * 128, :],
                                             xob2[:], sig="stX%d" % (lc % 2))
            p.barrier()

    prelude()
    ln1_stats()

    only_ffn1 = (stage == "ffn1")
    ternarize(wgt, 8, INNER, wmg, None, Mgs)
    ternarize(wut, 8, INNER, wmu, None, Mus)
    ternarize(wdt, 32, D, wmd, None, Mds)
    p.barrier()
    ffn_phase(x_in, y_out if only_ffn1 else Xs, mu1, rs1, f2b1, None, None,
              post_stats=None if only_ffn1 else (mu2, rs2))

    if stage in ("attn", "full"):
        attention(Xs, Xs if stage == "full" else y_out, mu2, rs2,
                  post_stats=(mu3, rs3) if stage == "full" else None)

    if stage == "full":
        ffn_phase(Xs, y_out, mu3, rs3, f2b2, None, None, post_stats=None)

    for s_ in ("stK", "stX0", "stX1", "stM"):
        p.wait("sync", s_, p.n(s_))
        p.wait("vector", s_, p.n(s_))

    p.emit()
    return nc


def _host_layouts(WQ, WK, WV, WO, w_gate, w_up, w_down, gamma, beta):
    perm = np.concatenate([np.arange(0, DH, 2), np.arange(1, DH, 2)])
    wq = WQ[:, perm, :]
    wk = WK[:, perm, :]
    return dict(
        wgT=np.ascontiguousarray(w_gate.T),
        wuT=np.ascontiguousarray(w_up.T),
        wdT=np.ascontiguousarray(w_down.T),
        wqT=np.ascontiguousarray(wq.transpose(2, 0, 1).reshape(D, D)),
        wkT=np.ascontiguousarray(wk.transpose(2, 0, 1).reshape(D, D)),
        wvT=np.ascontiguousarray(WV.transpose(2, 0, 1).reshape(D, D)),
        wo=np.ascontiguousarray(WO),
        gam=np.ascontiguousarray(gamma.reshape(1, D)),
        bet=np.ascontiguousarray(beta.reshape(1, D)))


def kernel(L0_attn_x, L0_attn_WQ, L0_attn_WK, L0_attn_WV, L0_attn_WO,
           L0_attn_gamma, L0_attn_beta, L0_ffn_w_gate, L0_ffn_w_up,
           L0_ffn_w_down):
    shared = _host_layouts(L0_attn_WQ, L0_attn_WK, L0_attn_WV, L0_attn_WO,
                           L0_ffn_w_gate, L0_ffn_w_up, L0_ffn_w_down,
                           L0_attn_gamma, L0_attn_beta)
    nc = build_nc("full")
    in_maps = []
    for i in range(8):
        m = dict(shared)
        m["x"] = np.ascontiguousarray(np.asarray(L0_attn_x)[i])
        in_maps.append(m)
    res = run_bass_kernel_spmd(nc, in_maps, core_ids=list(range(8)))
    return np.stack([np.asarray(r["y"]) for r in res.results], axis=0)


# version 03
# speedup vs baseline: 1.1126x; 1.1126x over previous
import contextlib
import math
import numpy as np
import concourse.bass as bass
import concourse.mybir as mybir
from concourse.bass_utils import run_bass_kernel_spmd

F32 = mybir.dt.float32
BF16 = mybir.dt.bfloat16
I16 = mybir.dt.int16
ALU = mybir.AluOpType
ACT = mybir.ActivationFunctionType
AX = mybir.AxisListType

SCAN_QST_BATCH = True
SCAN_ADT_BATCH = True

S = 4096
D = 1024
NB = S // 128
INNER = 4096
H = 16
DH = 64
NEI = 8
BPE = NB // NEI
EPS = 1e-5
TWO_PI = 6.283185307179586
PI_CLAMP = 3.141592
DECAY = [1.0 - 2.0 ** (-5.0 - h) for h in range(H)]


class Prog:
    ENGS = ("sync", "vector", "scalar", "tensor", "gpsimd")

    DRAIN_ENGS = ("vector", "scalar", "gpsimd")

    def __init__(self, nc):
        self.nc = nc
        self.q = {e: [] for e in self.ENGS}
        self.sems = {}
        self._dirty = {e: False for e in self.ENGS}

    def sem(self, name):
        self.sems[name] = [self.nc.alloc_semaphore(name), 0]

    def n(self, name):
        return self.sems[name][1]

    def _pre(self, eng):
        if eng in self.DRAIN_ENGS and self._dirty[eng]:
            self.q[eng].append(lambda e: e.drain())
            self._dirty[eng] = False

    def op(self, eng, fn, sig=None, inc=1):
        self._pre(eng)
        self._dirty[eng] = eng in self.DRAIN_ENGS
        if sig is not None:
            rec = self.sems[sig]
            rec[1] += inc
            h = rec[0]

            def wrapped(e, fn=fn, h=h, inc=inc):
                fn(e).then_inc(h, inc)
            self.q[eng].append(wrapped)
            return rec[1]
        self.q[eng].append(fn)
        return None

    def dma(self, eng, out, in_, sig="ld"):
        rec = self.sems[sig]
        rec[1] += 16
        h = rec[0]
        self.q[eng].append(lambda e, o=out, i=in_, h=h: e.dma_start(out=o, in_=i)
                           .then_inc(h, 16))
        return rec[1]

    def wait(self, eng, sem, val):
        if val is None:
            return
        h = self.sems[sem][0]
        self.q[eng].append(lambda e, h=h, v=val: e.wait_ge(h, v))

    def inc(self, eng, sem, by=1):
        self._pre(eng)
        rec = self.sems[sem]
        rec[1] += by
        h = rec[0]
        self.q[eng].append(lambda e, h=h, b=by: e.sem_inc(h, b))
        return rec[1]

    DMA_SEMS = ("lw", "lx0", "lx1", "lk0", "lk1", "stK", "stX0", "stX1", "stM0",
                "stM1", "sd")

    def barrier(self):
        for s_ in self.DMA_SEMS:
            if s_ in self.sems:
                self.wait("sync", s_, self.sems[s_][1] or None)
        for e in self.ENGS:
            self._pre(e)
            self.inc(e, "bar", 1)
        tgt = self.n("bar")
        for e in self.ENGS:
            self.wait(e, "bar", tgt)

    def drain(self, eng="vector"):
        self.q[eng].append(lambda e: e.drain())
        self._dirty[eng] = False

    def emit(self):
        with self.nc.Block() as block:
            @block.sync
            def _(e):
                for fn in self.q["sync"]:
                    fn(e)

            @block.vector
            def _(e):
                for fn in self.q["vector"]:
                    fn(e)

            @block.scalar
            def _(e):
                for fn in self.q["scalar"]:
                    fn(e)

            @block.tensor
            def _(e):
                for fn in self.q["tensor"]:
                    fn(e)

            @block.gpsimd
            def _(e):
                for fn in self.q["gpsimd"]:
                    fn(e)


def _qst_ap(qstall, hh, lc):
    return qstall[64 * (hh % 2):64 * (hh % 2) + 64,
                  (hh // 2) * 128:(hh // 2) * 128 + 128]


def build_nc(stage="full"):
    nc = bass.Bass("TRN2", target_bir_lowering=False)

    x_in = nc.dram_tensor("x", [S, D], F32, kind="ExternalInput").ap()
    wgt = nc.dram_tensor("wgT", [D, INNER], F32, kind="ExternalInput").ap()
    wut = nc.dram_tensor("wuT", [D, INNER], F32, kind="ExternalInput").ap()
    wdt = nc.dram_tensor("wdT", [INNER, D], F32, kind="ExternalInput").ap()
    wqt = nc.dram_tensor("wqT", [D, D], F32, kind="ExternalInput").ap()
    wkt = nc.dram_tensor("wkT", [D, D], F32, kind="ExternalInput").ap()
    wvt = nc.dram_tensor("wvT", [D, D], F32, kind="ExternalInput").ap()
    wo_d = nc.dram_tensor("wo", [D, D], F32, kind="ExternalInput").ap()
    gam_d = nc.dram_tensor("gam", [1, D], F32, kind="ExternalInput").ap()
    bet_d = nc.dram_tensor("bet", [1, D], F32, kind="ExternalInput").ap()
    y_out = nc.dram_tensor("y", [S, D], F32, kind="ExternalOutput").ap()

    Xs = nc.dram_tensor("Xs", [S, D], F32, kind="Internal").ap()
    K2s = nc.dram_tensor("K2s", [NB, 128, INNER], BF16, kind="Internal").ap()
    Mgs = nc.dram_tensor("Mgs", [128, 8 * INNER], BF16, kind="Internal").ap()
    Mus = nc.dram_tensor("Mus", [128, 8 * INNER], BF16, kind="Internal").ap()
    Mds = nc.dram_tensor("Mds", [128, 32 * D], BF16, kind="Internal").ap()

    p = Prog(nc)
    for s_ in ("lw", "lx0", "lx1", "lk0", "lk1", "stK", "stX0", "stX1", "stM0",
               "stM1", "v", "sc", "t", "g", "bar", "sd"):
        p.sem(s_)

    al = nc.alloc_sbuf_tensor
    alp = nc.alloc_psum_tensor
    _uid = [0]

    def _uniq(n):
        _uid[0] += 1
        return f"{n}_{_uid[0]}"


    psA = alp("psA", [128, 1024], F32).ap()
    psB = alp("psB", [128, 1024], F32).ap()
    psC = alp("psC", [128, 1024], F32).ap()
    psD = alp("psD", [128, 1024], F32).ap()

    cst = al("cst", [128, 8], F32).ap()
    C_HALF = cst[:, 0:1]
    C_NEGH = cst[:, 1:2]
    C_EPS = cst[:, 2:3]
    C_I2PI = cst[:, 3:4]
    C_PIH = cst[:, 4:5]
    C_LNA = cst[:, 5:6]
    C_128 = cst[:, 6:7]
    ones128 = al("ones128", [128, 1], F32).ap()
    ones1r = al("ones1r", [1, 128], F32).ap()
    idf = al("idf", [128, 128], F32).ap()
    idb = al("idb", [128, 128], BF16).ap()
    wmg = al("wmg", [128, 1], F32).ap()
    wmu = al("wmu", [128, 1], F32).ap()
    wmd = al("wmd", [128, 1], F32).ap()
    mu1 = al("mu1", [128, NB], F32).ap()
    rs1 = al("rs1", [128, NB], F32).ap()
    mu2 = al("mu2", [128, NB], F32).ap()
    rs2 = al("rs2", [128, NB], F32).ap()
    mu3 = al("mu3", [128, NB], F32).ap()
    rs3 = al("rs3", [128, NB], F32).ap()
    f2b1 = al("f2b1", [128, NB], F32).ap()
    f2b2 = al("f2b2", [128, NB], F32).ap()
    stt = al("stt", [128, 8], F32).ap()
    dump = al("dump", [128, D], F32).ap()

    def ln_stats_tile(xtile, mu_c, rs_c):
        p.op("vector", lambda e: e.tensor_reduce(stt[:, 0:1], xtile[:], AX.X, ALU.add))
        p.drain()
        p.op("vector", lambda e: e.tensor_single_scalar(mu_c, stt[:, 0:1], 1.0 / D, ALU.mult))
        p.drain()
        p.op("vector", lambda e: e.tensor_tensor(stt[:, 2:3], mu_c, mu_c, ALU.mult), sig="v")
        vt = p.n("v")
        p.op("scalar", lambda e: e.activation(dump[:], xtile[:], ACT.Square,
                                              accum_out=stt[:, 1:2]))
        p.wait("scalar", "v", vt)
        p.op("scalar", lambda e: e.copy(stt[:, 3:4], stt[:, 1:2]), sig="sc")
        p.wait("vector", "sc", p.n("sc"))
        p.drain()
        p.op("vector", lambda e: e.scalar_tensor_tensor(stt[:, 4:5], stt[:, 3:4], 1.0 / D,
                                                        stt[:, 2:3], ALU.mult, ALU.subtract),
             sig="v")
        p.wait("scalar", "v", p.n("v"))
        p.op("scalar", lambda e: e.activation(stt[:, 5:6], stt[:, 4:5], ACT.Ln, bias=C_EPS))
        p.op("scalar", lambda e: e.activation(rs_c, stt[:, 5:6], ACT.Exp, scale=C_NEGH),
             sig="sc")
        p.wait("vector", "sc", p.n("sc"))

    def round_to_ints(src, scale_ap, ti, cf, ei, tg, out, width):
        p.wait("scalar", "v", p.n("v"))
        p.op("scalar", lambda e: e.activation(ti[:, 0:width], src, ACT.Identity,
                                              scale=scale_ap, bias=C_HALF))
        p.op("scalar", lambda e: e.copy(cf[:, 0:width], ti[:, 0:width]), sig="sc")
        p.wait("vector", "sc", p.n("sc"))
        p.op("vector", lambda e: e.scalar_tensor_tensor(ei[:, 0:width], src, scale_ap,
                                                        cf[:, 0:width], ALU.mult, ALU.subtract))
        p.op("vector", lambda e: e.tensor_single_scalar(tg[:, 0:width], ei[:, 0:width],
                                                        -0.5, ALU.is_lt))
        p.op("vector", lambda e: e.tensor_tensor(out, cf[:, 0:width], tg[:, 0:width],
                                                 ALU.subtract), sig="v")
        return p.n("v")

    def prelude():
        for i, v in enumerate([0.5, -0.5, EPS, 1.0 / TWO_PI, np.pi / 2,
                               -math.log(10.0) / 8.0, 128.0]):
            p.op("vector", lambda e, i=i, v=v: e.memset(cst[:, i:i + 1], v))
        p.op("vector", lambda e: e.memset(ones128[:], 1.0))
        p.op("vector", lambda e: e.memset(ones1r[:], 1.0))
        p.op("gpsimd", lambda e: e.iota(idf[:], [[1, 128]], base=0, channel_multiplier=-1,
                                        allow_small_or_imprecise_dtypes=True), sig="g")
        p.wait("vector", "g", p.n("g"))
        p.op("vector", lambda e: e.tensor_single_scalar(idf[:], idf[:], 0.0, ALU.is_equal))
        p.op("vector", lambda e: e.tensor_copy(idb[:], idf[:]), sig="v")
        p.wait("tensor", "v", p.n("v"))
        p.wait("scalar", "v", p.n("v"))

    def ln1_stats():
        with nc.sbuf_tensor(_uniq("p0a_x"), [128, 2 * D], F32) as xg:
            xt = xg.ap()
            done = [None, None]
            for b in range(NB):
                sl = xt[:, (b % 2) * D:(b % 2) * D + D]
                lxs = "lx%d" % (b % 2)
                p.wait("sync", "v", done[b % 2])
                ldt = p.dma("sync", sl, x_in[b * 128:(b + 1) * 128, :], sig=lxs)
                p.wait("vector", lxs, ldt)
                p.wait("scalar", lxs, ldt)
                ln_stats_tile(sl, mu1[:, b:b + 1], rs1[:, b:b + 1])
                p.inc("vector", "v", 1)
                done[b % 2] = p.n("v")
        p.barrier()

    def ternarize(src, n_chunks, cols, wm_tile, dst_sbuf, dst_hbm):
        with contextlib.ExitStack() as st_:
            wstg = st_.enter_context(nc.sbuf_tensor(_uniq("p0_wstg"), [128, 2 * cols], F32)).ap()
            mtmp = st_.enter_context(nc.sbuf_tensor(_uniq("p0_mtmp"), [128, 2 * cols], BF16)).ap()
            acc = st_.enter_context(nc.sbuf_tensor(_uniq("p0_acc"), [128, 1], F32)).ap()
            w11 = st_.enter_context(nc.sbuf_tensor(_uniq("p0_w11"), [1, 1], F32)).ap()
            ti = st_.enter_context(nc.sbuf_tensor(_uniq("p0_ti"), [128, cols], I16)).ap()
            cf = st_.enter_context(nc.sbuf_tensor(_uniq("p0_cf"), [128, cols], BF16)).ap()
            tg = st_.enter_context(nc.sbuf_tensor(_uniq("p0_tg"), [128, cols], BF16)).ap()
            r_ = st_.enter_context(nc.sbuf_tensor(_uniq("p0_r"), [128, cols], F32)).ap()
            isc = st_.enter_context(nc.sbuf_tensor(_uniq("p0_isc"), [128, 1], F32)).ap()

            p.op("vector", lambda e: e.memset(acc[:], 0.0))
            done = [None, None]
            for c_ in range(n_chunks):
                sl = wstg[:, (c_ % 2) * cols:(c_ % 2) * cols + cols]
                lxs = "lx%d" % (c_ % 2)
                p.wait("sync", "v", done[c_ % 2])
                ldt = p.dma("sync", sl, src[c_ * 128:(c_ + 1) * 128, :], sig=lxs)
                p.wait("vector", lxs, ldt)
                p.op("vector", lambda e, sl=sl, c_=c_: e.tensor_reduce(
                    stt[:, c_ % 2:c_ % 2 + 1], sl, AX.X, ALU.add,
                    apply_absolute_value=True))
                p.op("vector", lambda e, c_=c_: e.tensor_tensor(
                    acc[:], acc[:], stt[:, c_ % 2:c_ % 2 + 1], ALU.add), sig="v")
                done[c_ % 2] = p.n("v")
            p.drain()
            p.inc("vector", "v", 1)
            p.wait("tensor", "v", p.n("v"))
            p.op("tensor", lambda e: e.matmul(psA[0:1, 0:1], ones128[:], acc[:],
                                              start=True, stop=True), sig="t")
            p.wait("vector", "t", p.n("t"))
            ntot = float(n_chunks * 128 * cols)
            p.op("vector", lambda e: e.tensor_copy(w11[:], psA[0:1, 0:1]))
            p.drain()
            p.op("vector", lambda e: e.tensor_scalar(w11[:], w11[:], 1.0 / ntot, 1e-5,
                                                     ALU.mult, ALU.max), sig="v")
            p.wait("tensor", "v", p.n("v"))
            p.op("tensor", lambda e: e.matmul(psA[0:128, 1:2], ones1r[:], w11[:],
                                              start=True, stop=True), sig="t")
            p.wait("vector", "t", p.n("t"))
            p.op("vector", lambda e: e.tensor_copy(wm_tile[:], psA[0:128, 1:2]))
            p.drain()
            p.op("vector", lambda e: e.reciprocal(isc[:], wm_tile[:]))
            p.drain()
            p.inc("vector", "v", 1)
            vsc = p.n("v")
            p.wait("scalar", "v", vsc)

            mfree = [None, None]
            for c_ in range(n_chunks):
                sl = wstg[:, (c_ % 2) * cols:(c_ % 2) * cols + cols]
                msl = mtmp[:, (c_ % 2) * cols:(c_ % 2) * cols + cols]
                lxs = "lx%d" % (c_ % 2)
                p.wait("sync", "v", done[c_ % 2])
                ldt = p.dma("sync", sl, src[c_ * 128:(c_ + 1) * 128, :], sig=lxs)
                p.wait("vector", lxs, ldt)
                p.wait("scalar", lxs, ldt)
                p.wait("vector", "stM%d" % (c_ % 2), mfree[c_ % 2])
                vq = round_to_ints(sl, isc[:, 0:1], ti, cf, r_, tg, msl, cols)
                p.wait("vector", "v", vq)
                p.op("vector", lambda e, msl=msl: e.tensor_scalar(
                    msl, msl, 1.0, -1.0, ALU.min, ALU.max), sig="v")
                done[c_ % 2] = p.n("v")
                p.wait("scalar", "v", p.n("v"))
                mfree[c_ % 2] = p.dma("scalar", dst_hbm[:, c_ * cols:(c_ + 1) * cols],
                                      msl, sig="stM%d" % (c_ % 2))
            p.barrier()

    def ffn_phase(src_hbm, dst_hbm, mu_t, rs_t, f2_t, mg, mu_, post_stats):
        HW2 = INNER // 2
        with contextlib.ExitStack() as st_:
            sb = lambda n, sh, dt: st_.enter_context(nc.sbuf_tensor(_uniq(n), sh, dt)).ap()
            if mg is None:
                mg = sb("fa_mg", [128, 8 * INNER], BF16)
                mu_ = sb("fa_mu", [128, 8 * INNER], BF16)
                for c_ in range(8):
                    p.dma("sync", mg[:, c_ * INNER:(c_ + 1) * INNER],
                          Mgs[:, c_ * INNER:(c_ + 1) * INNER], sig="lw")
                for c_ in range(8):
                    p.dma("sync", mu_[:, c_ * INNER:(c_ + 1) * INNER],
                          Mus[:, c_ * INNER:(c_ + 1) * INNER], sig="lw")
                p.wait("tensor", "lw", p.n("lw"))
            h_t = sb("fa_h", [128, INNER], F32)
            gsm = sb("fa_g", [128, 512], F32)
            xin = sb("fa_xin", [128, 2 * D], F32)
            xn = sb("fa_xn", [128, D], F32)
            ti1 = sb("fa_ti1", [128, D], I16)
            cf1 = sb("fa_cf1", [128, D], BF16)
            tg1 = ti1.bitcast(BF16)
            e1 = h_t[:, 0:D]
            k1 = sb("fa_k1", [128, D], BF16)
            k1T = sb("fa_k1T", [128, D], BF16)
            HW4 = INNER // 4
            ti2 = ti1
            cf2 = cf1
            tg2 = tg1
            e2 = e1
            k2h = sb("fa_k2h", [128, INNER], BF16)
            k2T = sb("fa_k2T", [128, INNER], BF16)
            sc1c = sb("fa_sc1", [128, 2], F32)
            sc2c = sb("fa_sc2", [128, 2], F32)
            f1g = sb("fa_f1g", [128, 2], F32)
            f1u = sb("fa_f1u", [128, 2], F32)

            rotp = [(psA[:, 0:512], psA[:, 512:1024]),
                    (psC[:, 0:512], psC[:, 512:1024]),
                    (psD[:, 0:512], psD[:, 512:1024])]
            psBb = psB.bitcast(BF16)
            rot_free = [None, None, None]
            k2_store = [None]
            k2ev_done = [None]
            k2h_v = {}
            xin_free = [None, None]

            def emit_front(b):
                xb = xin[:, (b % 2) * D:(b % 2) * D + D]
                lxs = "lx%d" % (b % 2)
                p.wait("sync", "v", xin_free[b % 2])
                ldt = p.dma("sync", xb[:], src_hbm[b * 128:(b + 1) * 128, :], sig=lxs)
                p.wait("vector", lxs, ldt)
                p.op("vector", lambda e, b=b, xb=xb: e.tensor_scalar(
                    xn[:], xb[:], mu_t[:, b:b + 1], rs_t[:, b:b + 1],
                    ALU.subtract, ALU.mult), sig="v")
                xin_free[b % 2] = p.n("v")
                sc1 = sc1c[:, b % 2:b % 2 + 1]
                fg = f1g[:, b % 2:b % 2 + 1]
                fu = f1u[:, b % 2:b % 2 + 1]
                p.op("vector", lambda e: e.tensor_reduce(stt[:, 0:1], xn[:], AX.X, ALU.max,
                                                         apply_absolute_value=True))
                p.op("vector", lambda e: e.tensor_single_scalar(stt[:, 0:1], stt[:, 0:1],
                                                                1e-5, ALU.max))
                p.op("vector", lambda e, fg=fg: e.scalar_tensor_tensor(
                    fg, stt[:, 0:1], 1.0 / 127.0, wmg[:], ALU.mult, ALU.mult))
                p.op("vector", lambda e, fu=fu: e.scalar_tensor_tensor(
                    fu, stt[:, 0:1], 1.0 / 127.0, wmu[:], ALU.mult, ALU.mult))
                p.op("vector", lambda e: e.reciprocal(stt[:, 1:2], stt[:, 0:1]))
                p.op("vector", lambda e, sc1=sc1: e.tensor_single_scalar(
                    sc1, stt[:, 1:2], 127.0, ALU.mult))
                p.inc("vector", "v", 1)
                p.wait("scalar", "v", p.n("v"))
                vk1 = round_to_ints(xn[:], sc1, ti1, cf1, e1, tg1, k1[:], D)
                p.wait("tensor", "v", vk1)
                for c_ in range(8):
                    fn = lambda e, c_=c_: e.transpose(
                        psBb[:, c_ * 128:(c_ + 1) * 128],
                        k1[:, c_ * 128:(c_ + 1) * 128], idb[:])
                    p.op("tensor", fn, sig="t" if c_ == 7 else None)
                p.wait("vector", "t", p.n("t"))
                p.op("vector", lambda e: e.tensor_copy(k1T[:], psBb[:, 0:1024]), sig="v")
                p.wait("tensor", "v", p.n("v"))

            def emit_gu(b):
                fg = f1g[:, b % 2:b % 2 + 1]
                fu = f1u[:, b % 2:b % 2 + 1]
                for ot in range(8):
                    gbank, ubank = rotp[ot % 3]
                    p.wait("tensor", "v", rot_free[ot % 3])
                    for i in range(8):
                        p.op("tensor", lambda e, gbank=gbank, i=i, ot=ot: e.matmul(
                            gbank, k1T[:, i * 128:(i + 1) * 128],
                            mg[:, i * INNER + ot * 512: i * INNER + ot * 512 + 512],
                            start=(i == 0), stop=(i == 7)))
                        p.op("tensor", lambda e, ubank=ubank, i=i, ot=ot: e.matmul(
                            ubank, k1T[:, i * 128:(i + 1) * 128],
                            mu_[:, i * INNER + ot * 512: i * INNER + ot * 512 + 512],
                            start=(i == 0), stop=(i == 7)),
                            sig="t" if i == 7 else None)
                    tt = p.n("t")
                    p.wait("scalar", "t", tt)
                    p.op("scalar", lambda e, gbank=gbank, fg=fg: e.activation(
                        gsm[:], gbank, ACT.Silu, scale=fg), sig="sc")
                    p.wait("vector", "sc", p.n("sc"))
                    p.op("vector", lambda e, ubank=ubank, fu=fu, ot=ot:
                         e.scalar_tensor_tensor(h_t[:, ot * 512:(ot + 1) * 512], ubank,
                                                fu, gsm[:], ALU.mult, ALU.mult), sig="v")
                    rot_free[ot % 3] = p.n("v")
                    p.wait("scalar", "v", p.n("v"))

            def emit_hq1(b):
                sc2 = sc2c[:, b % 2:b % 2 + 1]
                p.op("vector", lambda e: e.tensor_reduce(stt[:, 2:3], h_t[:], AX.X, ALU.max,
                                                         apply_absolute_value=True))
                p.op("vector", lambda e: e.tensor_single_scalar(stt[:, 2:3], stt[:, 2:3],
                                                                1e-5, ALU.max))
                p.op("vector", lambda e, b=b: e.scalar_tensor_tensor(
                    f2_t[:, b:b + 1], stt[:, 2:3], 0.5 / 127.0, wmd[:],
                    ALU.mult, ALU.mult))
                p.op("vector", lambda e: e.reciprocal(stt[:, 3:4], stt[:, 2:3]))
                p.op("vector", lambda e, sc2=sc2: e.tensor_single_scalar(
                    sc2, stt[:, 3:4], 127.0, ALU.mult))
                p.inc("vector", "v", 1)
                p.wait("scalar", "v", p.n("v"))
                for hh in range(4):
                    hsl = h_t[:, hh * HW4:(hh + 1) * HW4]
                    k2h_v[(b, hh)] = round_to_ints(hsl, sc2, ti2, cf2, e2, tg2,
                                                   k2h[:, hh * HW4:(hh + 1) * HW4], HW4)

            def emit_k2part(b):
                p.wait("tensor", "v", k2ev_done[0])
                for hh in range(4):
                    p.wait("tensor", "v", k2h_v[(b, hh)])
                    for c_ in range(8):
                        fn = lambda e, hh=hh, c_=c_: e.transpose(
                            psBb[:, 1024 + c_ * 128: 1024 + (c_ + 1) * 128],
                            k2h[:, hh * HW4 + c_ * 128: hh * HW4 + (c_ + 1) * 128], idb[:])
                        p.op("tensor", fn, sig="t")
                    tq = p.n("t")
                    p.wait("vector", "t", tq)
                    if hh == 0:
                        p.wait("vector", "stK", k2_store[0])
                    p.op("vector", lambda e, hh=hh: e.tensor_copy(
                        k2T[:, hh * HW4:(hh + 1) * HW4], psBb[:, 1024:2048]), sig="v")
                    if hh < 3:
                        p.wait("tensor", "v", p.n("v"))
                k2ev_done[0] = p.n("v")
                p.wait("scalar", "v", p.n("v"))
                k2_store[0] = p.dma("scalar", K2s[b], k2T[:], sig="stK")

            for b in range(NB):
                emit_front(b)
                emit_gu(b)
                if b >= 1:
                    emit_k2part(b - 1)
                emit_hq1(b)
            emit_k2part(NB - 1)
            p.barrier()
        with contextlib.ExitStack() as st_:
            sb = lambda n, sh, dt: st_.enter_context(nc.sbuf_tensor(_uniq(n), sh, dt)).ap()
            md = sb("fb_md", [128, 32 * D], BF16)
            k2i = sb("fb_k2i", [128, 2 * INNER], BF16)
            xin2 = sb("fb_xin2", [128, 2 * D], F32)
            xo = sb("fb_xo", [128, 2 * D], F32)
            for c_ in range(8):
                p.dma("sync", md[:, c_ * 4 * D:(c_ + 1) * 4 * D],
                      Mds[:, c_ * 4 * D:(c_ + 1) * 4 * D], sig="lw")
            p.wait("tensor", "lw", p.n("lw"))
            mm_done = [None, None]
            xo_free = [None, None]
            k2_free = [None, None]
            xin_free = [None, None]
            for b in range(NB):
                kb = k2i[:, (b % 2) * INNER:(b % 2) * INNER + INNER]
                xb = xin2[:, (b % 2) * D:(b % 2) * D + D]
                xob = xo[:, (b % 2) * D:(b % 2) * D + D]
                lks = "lk%d" % (b % 2)
                lxs = "lx%d" % (b % 2)
                p.wait("sync", "t", k2_free[b % 2])
                l1 = p.dma("sync", kb[:], K2s[b], sig=lks)
                p.wait("sync", "v", xin_free[b % 2])
                l2 = p.dma("sync", xb[:], src_hbm[b * 128:(b + 1) * 128, :], sig=lxs)
                p.wait("tensor", lks, l1)
                p.wait("tensor", "v", mm_done[0])
                for oc in range(32):
                    for d2 in range(2):
                        last = (oc == 31 and d2 == 1)
                        p.op("tensor", lambda e, oc=oc, d2=d2, kb=kb: e.matmul(
                            psA[:, d2 * 512:(d2 + 1) * 512],
                            kb[:, oc * 128:(oc + 1) * 128],
                            md[:, oc * D + d2 * 512: oc * D + d2 * 512 + 512],
                            start=(oc == 0), stop=(oc == 31)),
                            sig="t" if last else None)
                k2_free[b % 2] = p.n("t")
                p.wait("vector", "t", p.n("t"))
                p.wait("vector", lxs, l2)
                p.wait("vector", "stX%d" % (b % 2), xo_free[b % 2])
                p.op("vector", lambda e, b=b, xb=xb, xob=xob: e.scalar_tensor_tensor(
                    xob[:], psA[:], f2_t[:, b:b + 1], xb[:], ALU.mult, ALU.add), sig="v")
                mm_done[0] = p.n("v")
                xin_free[b % 2] = p.n("v")
                p.wait("scalar", "v", p.n("v"))
                if post_stats is not None:
                    mun, rsn = post_stats
                    ln_stats_tile(xob, mun[:, b:b + 1], rsn[:, b:b + 1])
                xo_free[b % 2] = p.dma("scalar", dst_hbm[b * 128:(b + 1) * 128, :], xob[:],
                                       sig="stX%d" % (b % 2))
            p.barrier()

    def attention(src_hbm, dst_hbm, mu_t, rs_t, post_stats):
        with contextlib.ExitStack() as st_:
            sb = lambda n, sh, dt: st_.enter_context(nc.sbuf_tensor(_uniq(n), sh, dt)).ap()
            sintab = sb("at_sin", [128, NB * 32], F32)
            costab = sb("at_cos", [128, NB * 32], F32)
            dmask = sb("at_dmask", [128, H * 128], F32)
            mstab = sb("at_ms", [128, 8 * 128], F32)
            mbc = sb("at_mbc", [128, H], F32)
            scc = sb("at_scc", [128, H], F32)
            lns = sb("at_lns", [128, H], F32)
            lns2 = sb("at_lns2", [128, 8], F32)
            svals = sb("at_sv", [128, H], F32)
            gbc = sb("at_gbc", [128, D], F32)
            bbc = sb("at_bbc", [128, D], F32)
            grow = sb("at_grow", [1, D], F32)
            brow = sb("at_brow", [1, D], F32)
            ssb = sb("at_ssb", [128, H * DH], F32)
            wstg = sb("at_wstg", [128, 8 * D], F32)
            xnT = sb("at_xnT", [128, BPE * D], F32)
            vq_t = sb("at_vq", [128, BPE * D], F32)
            qT = sb("at_qT", [128, BPE * D], F32)
            kT = sb("at_kT", [128, BPE * D], F32)
            kbq = sb("at_kbq", [128, BPE * D], F32)
            xab = sb("at_xab", [128, 2 * D], F32)
            xn2 = sb("at_xn2", [128, D], F32)
            rot = sb("at_rot", [128, D], F32)
            adt = sb("at_adt", [128, 128], F32)
            qst = sb("at_qst", [128, 128], F32)
            rr1 = sb("at_rr1", [128, D], F32)
            rr2 = sb("at_rr2", [128, D], F32)
            rrT = sb("at_rrT", [128, D], F32)
            gnq = sb("at_gnq", [128, H], F32)
            gnm = sb("at_gnm", [128, H], F32)
            gnr = sb("at_gnr", [128, H], F32)
            xo2 = sb("at_xo2", [128, 2 * D], F32)
            ta = sb("at_ta", [128, 1024], F32)
            tb_ = sb("at_tb", [128, 1024], F32)
            ti3 = sb("at_ti3", [128, 1024], I16)

            g1 = p.dma("sync", grow[:], gam_d[:], sig="lw")
            g2 = p.dma("sync", brow[:], bet_d[:], sig="lw")
            p.wait("tensor", "lw", g2)
            for half in range(2):
                p.op("tensor", lambda e, half=half: e.matmul(
                    psA[:, half * 512:(half + 1) * 512], ones1r[:],
                    grow[0:1, half * 512:(half + 1) * 512], start=True, stop=True))
                p.op("tensor", lambda e, half=half: e.matmul(
                    psB[:, half * 512:(half + 1) * 512], ones1r[:],
                    brow[0:1, half * 512:(half + 1) * 512], start=True, stop=True),
                    sig="t" if half == 1 else None)
            p.wait("vector", "t", p.n("t"))
            p.op("vector", lambda e: e.tensor_copy(gbc[:], psA[:]))
            p.op("vector", lambda e: e.tensor_copy(bbc[:], psB[:]), sig="v")
            p.wait("tensor", "v", p.n("v"))

            for hh in range(H):
                p.op("vector", lambda e, hh=hh: e.memset(svals[:, hh:hh + 1], DECAY[hh]))
            p.inc("vector", "v", 1)
            p.wait("scalar", "v", p.n("v"))
            p.op("scalar", lambda e: e.activation(lns[:], svals[:], ACT.Ln), sig="sc")
            p.wait("vector", "sc", p.n("sc"))
            for pc in range(8):
                p.op("vector", lambda e, pc=pc: e.tensor_copy(lns2[0:64, pc:pc + 1],
                                                              lns[0:64, 2 * pc:2 * pc + 1]))
                p.op("vector", lambda e, pc=pc: e.tensor_copy(
                    lns2[64:128, pc:pc + 1], lns[64:128, 2 * pc + 1:2 * pc + 2]))
            p.inc("vector", "v", 1)
            p.wait("scalar", "v", p.n("v"))
            p.wait("gpsimd", "v", p.n("v"))
            p.op("gpsimd", lambda e: e.iota(ta[:, 0:128], [[1, 128]], base=0,
                                            channel_multiplier=-1,
                                            allow_small_or_imprecise_dtypes=True))
            p.op("gpsimd", lambda e: e.iota(ta[:, 128:256], [[1, 128]], base=1,
                                            channel_multiplier=0,
                                            allow_small_or_imprecise_dtypes=True))
            p.op("gpsimd", lambda e: e.iota(ta[:, 256:257], [[1, 1]], base=127,
                                            channel_multiplier=-1,
                                            allow_small_or_imprecise_dtypes=True))
            p.op("gpsimd", lambda e: e.iota(ta[:, 512:544], [[128, 32]], base=0,
                                            channel_multiplier=1,
                                            allow_small_or_imprecise_dtypes=True),
                 sig="g")
            gt_ = p.n("g")
            p.wait("vector", "g", gt_)
            p.wait("scalar", "g", gt_)
            p.op("vector", lambda e: e.tensor_single_scalar(tb_[:, 0:128], ta[:, 0:128],
                                                            0.0, ALU.is_ge), sig="v")
            p.wait("scalar", "v", p.n("v"))
            for hh in range(H):
                p.op("scalar", lambda e, hh=hh: e.activation(
                    dmask[:, hh * 128:(hh + 1) * 128], ta[:, 0:128], ACT.Exp,
                    scale=lns[:, hh:hh + 1]))
                p.op("scalar", lambda e, hh=hh: e.activation(
                    mbc[:, hh:hh + 1], ta[:, 256:257], ACT.Exp, scale=lns[:, hh:hh + 1]))
                p.op("scalar", lambda e, hh=hh: e.activation(
                    scc[:, hh:hh + 1], C_128, ACT.Exp, scale=lns[:, hh:hh + 1]))
            for pc in range(8):
                p.op("scalar", lambda e, pc=pc: e.activation(
                    mstab[:, pc * 128:(pc + 1) * 128], ta[:, 128:256], ACT.Exp,
                    scale=lns2[:, pc:pc + 1]), sig="sc" if pc == 7 else None)
            p.wait("vector", "sc", p.n("sc"))
            for hh in range(H):
                p.op("vector", lambda e, hh=hh: e.tensor_tensor(
                    dmask[:, hh * 128:(hh + 1) * 128], dmask[:, hh * 128:(hh + 1) * 128],
                    tb_[:, 0:128], ALU.mult))
            p.op("vector", lambda e: e.memset(ssb[:], 0.0))
            p.inc("vector", "v", 1)
            p.wait("gpsimd", "v", p.n("v"))
            p.op("gpsimd", lambda e: e.iota(tb_[:, 224:256], [[128, 32]], base=0,
                                            channel_multiplier=1,
                                            allow_small_or_imprecise_dtypes=True))
            p.op("gpsimd", lambda e: e.iota(tb_[:, 160:192], [[1, 32]], base=0,
                                            channel_multiplier=0,
                                            allow_small_or_imprecise_dtypes=True), sig="g")
            p.wait("scalar", "g", p.n("g"))
            p.op("scalar", lambda e: e.activation(tb_[:, 192:224], tb_[:, 160:192], ACT.Exp,
                                                  scale=C_LNA), sig="sc")
            p.wait("vector", "sc", p.n("sc"))
            p.op("vector", lambda e: e.tensor_tensor(
                ta[:].rearrange("p (b u) -> p b u", u=32),
                tb_[:, 192:224].rearrange("p (a u) -> p a u", a=1).broadcast_to([128, 32, 32]),
                tb_[:, 224:256].rearrange("p (b a) -> p b a", a=1).broadcast_to([128, 32, 32]),
                ALU.mult), sig="v")
            p.wait("scalar", "v", p.n("v"))

            def sincos(dst, phase_shift):
                p.op("vector", lambda e: e.tensor_single_scalar(tb_[:], ta[:],
                                                                phase_shift, ALU.add))
                p.op("vector", lambda e: e.tensor_single_scalar(rot[:], tb_[:],
                                                                1.0 / TWO_PI, ALU.mult),
                     sig="v")
                p.wait("scalar", "v", p.n("v"))
                p.op("scalar", lambda e: e.activation(ti3[:], rot[:], ACT.Identity,
                                                      bias=C_HALF))
                p.op("scalar", lambda e: e.copy(xn2[:], ti3[:]), sig="sc")
                p.wait("vector", "sc", p.n("sc"))
                p.op("vector", lambda e: e.tensor_tensor(rot[:], rot[:], xn2[:],
                                                         ALU.subtract))
                p.op("vector", lambda e: e.tensor_single_scalar(rot[:], rot[:], -0.5,
                                                                ALU.is_lt))
                p.op("vector", lambda e: e.tensor_tensor(xn2[:], xn2[:], rot[:],
                                                         ALU.subtract))
                p.op("vector", lambda e: e.scalar_tensor_tensor(tb_[:], xn2[:], -TWO_PI,
                                                                tb_[:], ALU.mult, ALU.add))
                p.op("vector", lambda e: e.tensor_scalar(tb_[:], tb_[:], PI_CLAMP,
                                                         -PI_CLAMP, ALU.min, ALU.max),
                     sig="v")
                p.wait("scalar", "v", p.n("v"))
                p.op("scalar", lambda e, dst=dst: e.activation(dst, tb_[:], ACT.Sin),
                     sig="sc")
                p.wait("vector", "sc", p.n("sc"))

            sincos(sintab, 0.0)
            sincos(costab, np.pi / 2)
            p.inc("scalar", "sc", 1)
            p.wait("tensor", "sc", p.n("sc"))
            p.wait("sync", "sc", p.n("sc"))

            s_dma = [None]
            s_copy = [None]
            wo_evac = [None, None]
            xo_free2 = [None, None]
            xab_free = [None, None]

            def load_w(src, gate_t):
                p.wait("sync", "t", gate_t)
                for i in range(8):
                    tgt = p.dma("sync", wstg[:, i * D:(i + 1) * D],
                                src[i * 128:(i + 1) * 128, :], sig="lw")
                return tgt

            wstg_gate = [None]
            for ei in range(NEI):
                blocks = [ei * BPE + k for k in range(BPE)]
                wl = load_w(wvt, wstg_gate[0])
                p.wait("tensor", "lw", wl)
                for lb, b in enumerate(blocks):
                    xb = xab[:, (lb % 2) * D:(lb % 2) * D + D]
                    lxs = "lx%d" % (lb % 2)
                    p.wait("sync", "v", xab_free[lb % 2])
                    ldt = p.dma("sync", xb[:], src_hbm[b * 128:(b + 1) * 128, :], sig=lxs)
                    p.wait("vector", lxs, ldt)
                    p.op("vector", lambda e, b=b, xb=xb: e.tensor_scalar(
                        xn2[:], xb[:], mu_t[:, b:b + 1], rs_t[:, b:b + 1],
                        ALU.subtract, ALU.mult), sig="v")
                    xab_free[lb % 2] = p.n("v")
                    p.wait("tensor", "v", p.n("v"))
                    for c_ in range(8):
                        p.op("tensor", lambda e, c_=c_: e.transpose(
                            psA[:, c_ * 128:(c_ + 1) * 128],
                            xn2[:, c_ * 128:(c_ + 1) * 128], idf[:]),
                            sig="t" if c_ == 7 else None)
                    p.wait("vector", "t", p.n("t"))
                    p.op("vector", lambda e, lb=lb: e.tensor_copy(
                        xnT[:, lb * D:(lb + 1) * D], psA[:]), sig="v")
                    p.wait("tensor", "v", p.n("v"))
                    for i in range(8):
                        for v2 in range(2):
                            p.op("tensor", lambda e, i=i, v2=v2, lb=lb: e.matmul(
                                psB[:, v2 * 512:(v2 + 1) * 512],
                                xnT[:, lb * D + i * 128: lb * D + (i + 1) * 128],
                                wstg[:, i * D + v2 * 512: i * D + v2 * 512 + 512],
                                start=(i == 0), stop=(i == 7)),
                                sig="t" if (i == 7 and v2 == 1) else None)
                    p.wait("vector", "t", p.n("t"))
                    p.op("vector", lambda e, lb=lb: e.tensor_copy(
                        vq_t[:, lb * D:(lb + 1) * D], psB[:]), sig="v")
                    p.wait("tensor", "v", p.n("v"))
                    wstg_gate[0] = p.n("t")

                for which, wsrc, dstT in (("q", wqt, qT), ("k", wkt, kT)):
                    wl = load_w(wsrc, wstg_gate[0])
                    p.wait("tensor", "lw", wl)
                    for lb, b in enumerate(blocks):
                        for i in range(8):
                            for v2 in range(2):
                                p.op("tensor", lambda e, i=i, v2=v2, lb=lb: e.matmul(
                                    psB[:, v2 * 512:(v2 + 1) * 512],
                                    xnT[:, lb * D + i * 128: lb * D + (i + 1) * 128],
                                    wstg[:, i * D + v2 * 512: i * D + v2 * 512 + 512],
                                    start=(i == 0), stop=(i == 7)),
                                    sig="t" if (i == 7 and v2 == 1) else None)
                        p.wait("vector", "t", p.n("t"))
                        wstg_gate[0] = p.n("t")
                        cb = costab[:, b * 32:(b + 1) * 32]
                        sb_ = sintab[:, b * 32:(b + 1) * 32]
                        bc = lambda t: t.rearrange("p (a u) -> p a u", a=1).broadcast_to(
                            [128, H, 32])
                        x0 = psB[:].rearrange("p (h c u) -> p h c u", h=H, c=2)[:, :, 0, :]
                        x1 = psB[:].rearrange("p (h c u) -> p h c u", h=H, c=2)[:, :, 1, :]
                        r0 = rot[:].rearrange("p (h c u) -> p h c u", h=H, c=2)[:, :, 0, :]
                        r1 = rot[:].rearrange("p (h c u) -> p h c u", h=H, c=2)[:, :, 1, :]
                        ta0 = ta[:, 0:512].rearrange("p (h u) -> p h u", h=H)
                        ta1 = ta[:, 512:1024].rearrange("p (h u) -> p h u", h=H)
                        p.op("vector", lambda e, x0=x0, cb=cb, ta0=ta0: e.tensor_tensor(
                            ta0, x0, bc(cb), ALU.mult))
                        p.op("vector", lambda e, x1=x1, sb_=sb_, ta1=ta1: e.tensor_tensor(
                            ta1, x1, bc(sb_), ALU.mult))
                        p.op("vector", lambda e, r0=r0, ta0=ta0, ta1=ta1: e.tensor_tensor(
                            r0, ta0, ta1, ALU.subtract))
                        p.op("vector", lambda e, x0=x0, sb_=sb_, ta0=ta0: e.tensor_tensor(
                            ta0, x0, bc(sb_), ALU.mult))
                        p.op("vector", lambda e, x1=x1, cb=cb, ta1=ta1: e.tensor_tensor(
                            ta1, x1, bc(cb), ALU.mult))
                        p.op("vector", lambda e, r1=r1, ta0=ta0, ta1=ta1: e.tensor_tensor(
                            r1, ta0, ta1, ALU.add), sig="v")
                        vrot = p.n("v")
                        p.wait("tensor", "v", vrot)
                        if which == "k":
                            p.op("vector", lambda e, lb=lb: e.tensor_tensor(
                                kbq[:, lb * D:(lb + 1) * D].rearrange(
                                    "p (h u) -> p h u", h=H),
                                rot[:].rearrange("p (h u) -> p h u", h=H),
                                mbc[:].rearrange("p (h a) -> p h a", a=1).broadcast_to(
                                    [128, H, DH]), ALU.mult), sig="v")
                        for c_ in range(8):
                            p.op("tensor", lambda e, c_=c_: e.transpose(
                                psA[:, c_ * 128:(c_ + 1) * 128],
                                rot[:, c_ * 128:(c_ + 1) * 128], idf[:]),
                                sig="t" if c_ == 7 else None)
                        p.wait("vector", "t", p.n("t"))
                        p.op("vector", lambda e, lb=lb, dstT=dstT: e.tensor_copy(
                            dstT[:].rearrange("p (c t) -> p c t", c=8)[:, :, lb * 128:
                                                                      (lb + 1) * 128],
                            psA[:].rearrange("p (c t) -> p c t", c=8)), sig="v")
                        p.wait("tensor", "v", p.n("v"))

                wl = load_w(wo_d, wstg_gate[0])

                for lc, b in enumerate(blocks):
                    c_glob = b
                    first = (c_glob == 0)
                    p.wait("tensor", "lw", wl)
                    p.wait("sync", "sd", s_dma[0])
                    p.wait("tensor", "sd", s_dma[0])
                    qstall = rrT
                    if not first and SCAN_QST_BATCH:
                        qv = qT[:].rearrange("p (pc t) -> p pc t", pc=8)[
                            :, :, lc * 128:(lc + 1) * 128]
                        p.op("vector", lambda e, qv=qv: e.tensor_tensor(
                            qstall[:].rearrange("p (pc t) -> p pc t", pc=8), qv,
                            mstab[:].rearrange("p (pc t) -> p pc t", pc=8), ALU.mult),
                            sig="v")
                        p.wait("tensor", "v", p.n("v"))
                    if SCAN_ADT_BATCH == "amm":
                        for hf in range(2):
                            if hf == 1:
                                p.wait("tensor", "v", p.n("v"))
                            for j in range(8):
                                hh = hf * 8 + j
                                pc = hh // 2
                                base = 64 * (hh % 2)
                                ksl = kT[base:base + 64,
                                         pc * 512 + lc * 128: pc * 512 + (lc + 1) * 128]
                                qsl = qT[base:base + 64,
                                         pc * 512 + lc * 128: pc * 512 + (lc + 1) * 128]
                                p.op("tensor", lambda e, j=j, ksl=ksl, qsl=qsl: e.matmul(
                                    psB[:, j * 128:(j + 1) * 128], ksl, qsl,
                                    start=(j % 4 == 0), stop=True,
                                    skip_group_check=(j % 4 != 0)),
                                    sig="t" if j == 7 else None)
                            p.wait("vector", "t", p.n("t"))
                            for j in range(8):
                                hh = hf * 8 + j
                                base = 64 * (hh % 2)
                                p.op("vector", lambda e, j=j, hh=hh: e.tensor_tensor(
                                    adt[:], psB[:, j * 128:(j + 1) * 128],
                                    dmask[:, hh * 128:(hh + 1) * 128], ALU.mult), sig="v")
                                vq_ = p.n("v")
                                p.wait("tensor", "v", vq_)
                                vsl = vq_t[:, lc * D + hh * DH: lc * D + (hh + 1) * DH]
                                p.op("tensor", lambda e, hh=hh, vsl=vsl, first=first:
                                     e.matmul(psC[:, hh * DH:(hh + 1) * DH], adt[:], vsl,
                                              start=True, stop=first),
                                     sig="t" if first else None)
                                if not first:
                                    p.op("tensor", lambda e, hh=hh, base=base: e.matmul(
                                        psC[:, hh * DH:(hh + 1) * DH],
                                        _qst_ap(qstall, hh, lc),
                                        ssb[base:base + 64, hh * DH:(hh + 1) * DH],
                                        start=False, stop=True), sig="t")
                    elif SCAN_ADT_BATCH:
                        adt_t = [rr1, rr2]
                        adt_v = [None, None]
                        for hf in range(2):
                            if hf == 1:
                                p.wait("tensor", "v", adt_v[0])
                            for j in (0, 2, 4, 6, 1, 3, 5, 7):
                                hh = hf * 8 + j
                                pc = hh // 2
                                base = 64 * (hh % 2)
                                ksl = kT[base:base + 64,
                                         pc * 512 + lc * 128: pc * 512 + (lc + 1) * 128]
                                qsl = qT[base:base + 64,
                                         pc * 512 + lc * 128: pc * 512 + (lc + 1) * 128]
                                p.op("tensor", lambda e, j=j, ksl=ksl, qsl=qsl: e.matmul(
                                    psB[:, j * 128:(j + 1) * 128], ksl, qsl,
                                    start=(j % 4 == 0), stop=True,
                                    skip_group_check=(j % 4 != 0)),
                                    sig="t" if j == 7 else None)
                            p.wait("vector", "t", p.n("t"))
                            p.op("vector", lambda e, hf=hf: e.tensor_tensor(
                                adt_t[hf][:], psB[:], dmask[:, hf * 1024:(hf + 1) * 1024],
                                ALU.mult), sig="v")
                            adt_v[hf] = p.n("v")
                        for hf in range(2):
                            p.wait("tensor", "v", adt_v[hf])
                            for j in range(8):
                                hh = hf * 8 + j
                                base = 64 * (hh % 2)
                                vsl = vq_t[:, lc * D + hh * DH: lc * D + (hh + 1) * DH]
                                p.op("tensor", lambda e, hh=hh, j=j, hf=hf, vsl=vsl,
                                     first=first, adt_t=adt_t: e.matmul(
                                         psC[:, hh * DH:(hh + 1) * DH],
                                         adt_t[hf][:, j * 128:(j + 1) * 128], vsl,
                                         start=True, stop=first),
                                     sig="t" if (first and hh == H - 1) else None)
                                if not first:
                                    p.op("tensor", lambda e, hh=hh, base=base: e.matmul(
                                        psC[:, hh * DH:(hh + 1) * DH],
                                        _qst_ap(qstall, hh, lc),
                                        ssb[base:base + 64, hh * DH:(hh + 1) * DH],
                                        start=False, stop=True),
                                        sig="t" if hh == H - 1 else None)
                    else:
                        for hh in range(H):
                            pc = hh // 2
                            base = 64 * (hh % 2)
                            atile = psB[:, (hh % 2) * 512:(hh % 2) * 512 + 128]
                            ksl = kT[base:base + 64,
                                     pc * 512 + lc * 128: pc * 512 + (lc + 1) * 128]
                            qsl = qT[base:base + 64,
                                     pc * 512 + lc * 128: pc * 512 + (lc + 1) * 128]
                            p.op("tensor", lambda e, atile=atile, ksl=ksl, qsl=qsl:
                                 e.matmul(atile, ksl, qsl, start=True, stop=True),
                                 sig="t")
                            p.wait("vector", "t", p.n("t"))
                            p.op("vector", lambda e, atile=atile, hh=hh: e.tensor_tensor(
                                adt[:], atile, dmask[:, hh * 128:(hh + 1) * 128],
                                ALU.mult), sig="v")
                            if not first and not SCAN_QST_BATCH:
                                msl = mstab[base:base + 64, pc * 128:(pc + 1) * 128]
                                p.op("vector", lambda e, qsl=qsl, msl=msl, base=base:
                                     e.tensor_tensor(qst[base:base + 64, :], qsl, msl,
                                                     ALU.mult), sig="v")
                            vq_ = p.n("v")
                            p.wait("tensor", "v", vq_)
                            vsl = vq_t[:, lc * D + hh * DH: lc * D + (hh + 1) * DH]
                            p.op("tensor", lambda e, hh=hh, vsl=vsl, first=first:
                                 e.matmul(psC[:, hh * DH:(hh + 1) * DH], adt[:], vsl,
                                          start=True, stop=first),
                                 sig="t" if first else None)
                            if not first:
                                src_q = (_qst_ap(qstall, hh, lc) if SCAN_QST_BATCH else
                                         qst[base:base + 64, :])
                                p.op("tensor", lambda e, hh=hh, base=base, src_q=src_q:
                                     e.matmul(psC[:, hh * DH:(hh + 1) * DH], src_q,
                                              ssb[base:base + 64, hh * DH:(hh + 1) * DH],
                                              start=False, stop=True), sig="t")
                    t_heads = p.n("t")
                    p.wait("vector", "t", t_heads)
                    if not first:
                        p.op("vector", lambda e: e.tensor_tensor(
                            psD[0:64, :].rearrange("p (h u) -> p h u", h=H),
                            psD[0:64, :].rearrange("p (h u) -> p h u", h=H),
                            scc[0:64, :].rearrange("p (h a) -> p h a", a=1).broadcast_to(
                                [64, H, DH]), ALU.mult), sig="v")
                        p.wait("tensor", "v", p.n("v"))
                    for hh in range(H):
                        ksl2 = kbq[:, lc * D + hh * DH: lc * D + (hh + 1) * DH]
                        vsl2 = vq_t[:, lc * D + hh * DH: lc * D + (hh + 1) * DH]
                        st_flag = first and (hh % 8 == 0)
                        sp_flag = (hh % 8 == 7)
                        p.op("tensor", lambda e, hh=hh, ksl2=ksl2, vsl2=vsl2,
                             st_flag=st_flag, sp_flag=sp_flag, sk=not first: e.matmul(
                                 psD[0:64, hh * DH:(hh + 1) * DH], ksl2, vsl2,
                                 start=st_flag, stop=sp_flag, skip_group_check=sk),
                             sig="t" if hh == H - 1 else None)
                    p.wait("vector", "t", p.n("t"))
                    p.op("vector", lambda e: e.tensor_copy(ssb[0:64, :], psD[0:64, :]),
                         sig="v")
                    s_copy[0] = p.n("v")
                    p.wait("sync", "v", s_copy[0])
                    s_dma[0] = p.dma("sync", ssb[64:128, :], ssb[0:64, :], sig="sd")

                    p.wait("scalar", "t", t_heads)
                    p.op("scalar", lambda e: e.activation(dump[:], psC[:], ACT.Square),
                         sig="sc")
                    p.wait("vector", "sc", p.n("sc"))
                    p.op("vector", lambda e: e.tensor_reduce(
                        gnm[:], psC[:].rearrange("p (h u) -> p h u", h=H), AX.X, ALU.add))
                    p.op("vector", lambda e: e.tensor_reduce(
                        gnq[:], dump[:].rearrange("p (h u) -> p h u", h=H), AX.X, ALU.add))
                    p.drain()
                    p.op("vector", lambda e: e.tensor_single_scalar(gnm[:], gnm[:],
                                                                    1.0 / DH, ALU.mult))
                    p.drain()
                    p.op("vector", lambda e: e.tensor_tensor(ta[:, 0:H], gnm[:], gnm[:],
                                                             ALU.mult))
                    p.drain()
                    p.op("vector", lambda e: e.scalar_tensor_tensor(
                        gnr[:], gnq[:], 1.0 / DH, ta[:, 0:H], ALU.mult, ALU.subtract),
                        sig="v")
                    p.wait("scalar", "v", p.n("v"))
                    p.op("scalar", lambda e: e.activation(gnr[:], gnr[:], ACT.Ln,
                                                          bias=C_EPS))
                    p.op("scalar", lambda e: e.activation(gnr[:], gnr[:], ACT.Exp,
                                                          scale=C_NEGH), sig="sc")
                    p.wait("vector", "sc", p.n("sc"))
                    hview = lambda t: t.rearrange("p (h u) -> p h u", h=H)
                    hbc = lambda t: t.rearrange("p (h a) -> p h a", a=1).broadcast_to(
                        [128, H, DH])
                    p.op("vector", lambda e: e.tensor_tensor(hview(rr1[:]), psC[:].rearrange(
                        "p (h u) -> p h u", h=H), hbc(gnm[:]), ALU.subtract))
                    p.op("vector", lambda e: e.tensor_tensor(hview(rr2[:]), hview(gbc[:]),
                                                             hbc(gnr[:]), ALU.mult))
                    p.op("vector", lambda e: e.tensor_tensor(rr1[:], rr1[:], rr2[:],
                                                             ALU.mult))
                    p.op("vector", lambda e: e.tensor_tensor(rr1[:], rr1[:], bbc[:],
                                                             ALU.add), sig="v")
                    p.wait("tensor", "v", p.n("v"))
                    for c_ in range(8):
                        p.op("tensor", lambda e, c_=c_: e.transpose(
                            psA[:, c_ * 128:(c_ + 1) * 128], rr1[:, c_ * 128:(c_ + 1) * 128],
                            idf[:]), sig="t" if c_ == 7 else None)
                    p.wait("vector", "t", p.n("t"))
                    p.op("vector", lambda e: e.tensor_copy(rrT[:], psA[:]), sig="v")
                    p.wait("tensor", "v", p.n("v"))
                    for i in range(8):
                        for d2 in range(2):
                            p.op("tensor", lambda e, i=i, d2=d2: e.matmul(
                                psA[:, d2 * 512:(d2 + 1) * 512], rrT[:, i * 128:(i + 1) * 128],
                                wstg[:, i * D + d2 * 512: i * D + d2 * 512 + 512],
                                start=(i == 0), stop=(i == 7)),
                                sig="t" if (i == 7 and d2 == 1) else None)
                    wo_t = p.n("t")
                    wstg_gate[0] = wo_t
                    xb2 = xab[:, (lc % 2) * D:(lc % 2) * D + D]
                    lxs2 = "lx%d" % (lc % 2)
                    p.wait("sync", "v", xab_free[lc % 2])
                    l2 = p.dma("sync", xb2[:], src_hbm[b * 128:(b + 1) * 128, :], sig=lxs2)
                    p.wait("vector", "t", wo_t)
                    p.wait("vector", lxs2, l2)
                    xob2 = xo2[:, (lc % 2) * D:(lc % 2) * D + D]
                    p.wait("vector", "stX%d" % (lc % 2), xo_free2[lc % 2])
                    p.op("vector", lambda e, xb2=xb2, xob2=xob2: e.tensor_tensor(
                        xob2[:], psA[:], xb2[:], ALU.add), sig="v")
                    xab_free[lc % 2] = p.n("v")
                    wo_evac[0] = p.n("v")
                    p.wait("scalar", "v", p.n("v"))
                    if post_stats is not None:
                        mun, rsn = post_stats
                        ln_stats_tile(xob2, mun[:, b:b + 1], rsn[:, b:b + 1])
                    xo_free2[lc % 2] = p.dma("scalar", dst_hbm[b * 128:(b + 1) * 128, :],
                                             xob2[:], sig="stX%d" % (lc % 2))
            p.barrier()

    prelude()
    ln1_stats()

    only_ffn1 = (stage == "ffn1")
    ternarize(wgt, 8, INNER, wmg, None, Mgs)
    ternarize(wut, 8, INNER, wmu, None, Mus)
    ternarize(wdt, 32, D, wmd, None, Mds)
    p.barrier()
    ffn_phase(x_in, y_out if only_ffn1 else Xs, mu1, rs1, f2b1, None, None,
              post_stats=None if only_ffn1 else (mu2, rs2))

    if stage in ("attn", "full"):
        attention(Xs, Xs if stage == "full" else y_out, mu2, rs2,
                  post_stats=(mu3, rs3) if stage == "full" else None)

    if stage == "full":
        ffn_phase(Xs, y_out, mu3, rs3, f2b2, None, None, post_stats=None)

    for s_ in ("stK", "stX0", "stX1", "stM0", "stM1"):
        p.wait("sync", s_, p.n(s_))
        p.wait("vector", s_, p.n(s_))

    p.emit()
    return nc


def _host_layouts(WQ, WK, WV, WO, w_gate, w_up, w_down, gamma, beta):
    perm = np.concatenate([np.arange(0, DH, 2), np.arange(1, DH, 2)])
    wq = WQ[:, perm, :]
    wk = WK[:, perm, :]
    return dict(
        wgT=np.ascontiguousarray(w_gate.T),
        wuT=np.ascontiguousarray(w_up.T),
        wdT=np.ascontiguousarray(w_down.T),
        wqT=np.ascontiguousarray(wq.transpose(2, 0, 1).reshape(D, D)),
        wkT=np.ascontiguousarray(wk.transpose(2, 0, 1).reshape(D, D)),
        wvT=np.ascontiguousarray(WV.transpose(2, 0, 1).reshape(D, D)),
        wo=np.ascontiguousarray(WO),
        gam=np.ascontiguousarray(gamma.reshape(1, D)),
        bet=np.ascontiguousarray(beta.reshape(1, D)))


def kernel(L0_attn_x, L0_attn_WQ, L0_attn_WK, L0_attn_WV, L0_attn_WO,
           L0_attn_gamma, L0_attn_beta, L0_ffn_w_gate, L0_ffn_w_up,
           L0_ffn_w_down):
    shared = _host_layouts(L0_attn_WQ, L0_attn_WK, L0_attn_WV, L0_attn_WO,
                           L0_ffn_w_gate, L0_ffn_w_up, L0_ffn_w_down,
                           L0_attn_gamma, L0_attn_beta)
    nc = build_nc("full")
    in_maps = []
    for i in range(8):
        m = dict(shared)
        m["x"] = np.ascontiguousarray(np.asarray(L0_attn_x)[i])
        in_maps.append(m)
    res = run_bass_kernel_spmd(nc, in_maps, core_ids=list(range(8)))
    return np.stack([np.asarray(r["y"]) for r in res.results], axis=0)


# version 05
# speedup vs baseline: 1.3459x; 1.2097x over previous
import contextlib
import math
import numpy as np
import concourse.bass as bass
import concourse.mybir as mybir
from concourse.bass_utils import run_bass_kernel_spmd

F32 = mybir.dt.float32
BF16 = mybir.dt.bfloat16
I16 = mybir.dt.int16
ALU = mybir.AluOpType
ACT = mybir.ActivationFunctionType
AX = mybir.AxisListType

SCAN_QST_BATCH = True
SCAN_ADT_BATCH = True

S = 4096
D = 1024
NB = S // 128
INNER = 4096
H = 16
DH = 64
NEI = 8
BPE = NB // NEI
EPS = 1e-5
TWO_PI = 6.283185307179586
PI_CLAMP = 3.141592
DECAY = [1.0 - 2.0 ** (-5.0 - h) for h in range(H)]


class Prog:
    ENGS = ("sync", "vector", "scalar", "tensor", "gpsimd")

    DRAIN_ENGS = ("vector", "scalar", "gpsimd")

    def __init__(self, nc):
        self.nc = nc
        self.q = {e: [] for e in self.ENGS}
        self.sems = {}
        self._dirty = {e: False for e in self.ENGS}

    def sem(self, name):
        self.sems[name] = [self.nc.alloc_semaphore(name), 0]

    def n(self, name):
        return self.sems[name][1]

    def _pre(self, eng):
        if eng in self.DRAIN_ENGS and self._dirty[eng]:
            self.q[eng].append(lambda e: e.drain())
            self._dirty[eng] = False

    def op(self, eng, fn, sig=None, inc=1):
        self._pre(eng)
        self._dirty[eng] = eng in self.DRAIN_ENGS
        if sig is not None:
            rec = self.sems[sig]
            rec[1] += inc
            h = rec[0]

            def wrapped(e, fn=fn, h=h, inc=inc):
                fn(e).then_inc(h, inc)
            self.q[eng].append(wrapped)
            return rec[1]
        self.q[eng].append(fn)
        return None

    def dma(self, eng, out, in_, sig="ld"):
        rec = self.sems[sig]
        rec[1] += 16
        h = rec[0]
        self.q[eng].append(lambda e, o=out, i=in_, h=h: e.dma_start(out=o, in_=i)
                           .then_inc(h, 16))
        return rec[1]

    def wait(self, eng, sem, val):
        if val is None:
            return
        h = self.sems[sem][0]
        self.q[eng].append(lambda e, h=h, v=val: e.wait_ge(h, v))

    def inc(self, eng, sem, by=1):
        self._pre(eng)
        rec = self.sems[sem]
        rec[1] += by
        h = rec[0]
        self.q[eng].append(lambda e, h=h, b=by: e.sem_inc(h, b))
        return rec[1]

    DMA_SEMS = ("lw", "lx0", "lx1", "lk0", "lk1", "stK", "stX0", "stX1", "stM0",
                "stM1", "sd")

    def barrier(self):
        for s_ in self.DMA_SEMS:
            if s_ in self.sems:
                self.wait("sync", s_, self.sems[s_][1] or None)
        for e in self.ENGS:
            self._pre(e)
            self.inc(e, "bar", 1)
        tgt = self.n("bar")
        for e in self.ENGS:
            self.wait(e, "bar", tgt)

    def drain(self, eng="vector"):
        self.q[eng].append(lambda e: e.drain())
        self._dirty[eng] = False

    def emit(self):
        with self.nc.Block() as block:
            @block.sync
            def _(e):
                for fn in self.q["sync"]:
                    fn(e)

            @block.vector
            def _(e):
                for fn in self.q["vector"]:
                    fn(e)

            @block.scalar
            def _(e):
                for fn in self.q["scalar"]:
                    fn(e)

            @block.tensor
            def _(e):
                for fn in self.q["tensor"]:
                    fn(e)

            @block.gpsimd
            def _(e):
                for fn in self.q["gpsimd"]:
                    fn(e)


def _qst_ap(qstall, hh, lc):
    return qstall[64 * (hh % 2):64 * (hh % 2) + 64,
                  (hh // 2) * 128:(hh // 2) * 128 + 128]


def build_nc(stage="full"):
    nc = bass.Bass("TRN2", target_bir_lowering=False)

    x_in = nc.dram_tensor("x", [S, D], F32, kind="ExternalInput").ap()
    wgt = nc.dram_tensor("wgT", [D, INNER], F32, kind="ExternalInput").ap()
    wut = nc.dram_tensor("wuT", [D, INNER], F32, kind="ExternalInput").ap()
    wdt = nc.dram_tensor("wdT", [INNER, D], F32, kind="ExternalInput").ap()
    wqt = nc.dram_tensor("wqT", [D, D], F32, kind="ExternalInput").ap()
    wkt = nc.dram_tensor("wkT", [D, D], F32, kind="ExternalInput").ap()
    wvt = nc.dram_tensor("wvT", [D, D], F32, kind="ExternalInput").ap()
    wo_d = nc.dram_tensor("wo", [D, D], F32, kind="ExternalInput").ap()
    gam_d = nc.dram_tensor("gam", [1, D], F32, kind="ExternalInput").ap()
    bet_d = nc.dram_tensor("bet", [1, D], F32, kind="ExternalInput").ap()
    y_out = nc.dram_tensor("y", [S, D], F32, kind="ExternalOutput").ap()

    Xs = nc.dram_tensor("Xs", [S, D], F32, kind="Internal").ap()
    K2s = nc.dram_tensor("K2s", [NB, 128, INNER], BF16, kind="Internal").ap()
    Mgs = nc.dram_tensor("Mgs", [128, 8 * INNER], BF16, kind="Internal").ap()
    Mus = nc.dram_tensor("Mus", [128, 8 * INNER], BF16, kind="Internal").ap()
    Mds = nc.dram_tensor("Mds", [128, 32 * D], BF16, kind="Internal").ap()

    p = Prog(nc)
    for s_ in ("lw", "lx0", "lx1", "lk0", "lk1", "stK", "stX0", "stX1", "stM0",
               "stM1", "v", "sc", "t", "g", "bar", "sd"):
        p.sem(s_)

    al = nc.alloc_sbuf_tensor
    alp = nc.alloc_psum_tensor
    _uid = [0]

    def _uniq(n):
        _uid[0] += 1
        return f"{n}_{_uid[0]}"


    psA = alp("psA", [128, 1024], F32).ap()
    psB = alp("psB", [128, 1024], F32).ap()
    psC = alp("psC", [128, 1024], F32).ap()
    psD = alp("psD", [128, 1024], F32).ap()

    cst = al("cst", [128, 8], F32).ap()
    C_HALF = cst[:, 0:1]
    C_NEGH = cst[:, 1:2]
    C_EPS = cst[:, 2:3]
    C_I2PI = cst[:, 3:4]
    C_PIH = cst[:, 4:5]
    C_LNA = cst[:, 5:6]
    C_128 = cst[:, 6:7]
    ones128 = al("ones128", [128, 1], F32).ap()
    ones1r = al("ones1r", [1, 128], F32).ap()
    idf = al("idf", [128, 128], F32).ap()
    idb = al("idb", [128, 128], BF16).ap()
    wmg = al("wmg", [128, 1], F32).ap()
    wmu = al("wmu", [128, 1], F32).ap()
    wmd = al("wmd", [128, 1], F32).ap()
    mu1 = al("mu1", [128, NB], F32).ap()
    rs1 = al("rs1", [128, NB], F32).ap()
    mu2 = al("mu2", [128, NB], F32).ap()
    rs2 = al("rs2", [128, NB], F32).ap()
    mu3 = al("mu3", [128, NB], F32).ap()
    rs3 = al("rs3", [128, NB], F32).ap()
    f2b1 = al("f2b1", [128, NB], F32).ap()
    f2b2 = al("f2b2", [128, NB], F32).ap()
    stt = al("stt", [128, 8], F32).ap()
    dump = al("dump", [128, D], F32).ap()

    def ln_stats_tile(xtile, mu_c, rs_c):
        p.op("vector", lambda e: e.tensor_reduce(stt[:, 0:1], xtile[:], AX.X, ALU.add))
        p.drain()
        p.op("vector", lambda e: e.tensor_single_scalar(mu_c, stt[:, 0:1], 1.0 / D, ALU.mult))
        p.drain()
        p.op("vector", lambda e: e.tensor_tensor(stt[:, 2:3], mu_c, mu_c, ALU.mult), sig="v")
        vt = p.n("v")
        p.op("scalar", lambda e: e.activation(dump[:], xtile[:], ACT.Square,
                                              accum_out=stt[:, 1:2]))
        p.wait("scalar", "v", vt)
        p.op("scalar", lambda e: e.copy(stt[:, 3:4], stt[:, 1:2]), sig="sc")
        p.wait("vector", "sc", p.n("sc"))
        p.drain()
        p.op("vector", lambda e: e.scalar_tensor_tensor(stt[:, 4:5], stt[:, 3:4], 1.0 / D,
                                                        stt[:, 2:3], ALU.mult, ALU.subtract),
             sig="v")
        p.wait("scalar", "v", p.n("v"))
        p.op("scalar", lambda e: e.activation(stt[:, 5:6], stt[:, 4:5], ACT.Ln, bias=C_EPS))
        p.op("scalar", lambda e: e.activation(rs_c, stt[:, 5:6], ACT.Exp, scale=C_NEGH),
             sig="sc")
        p.wait("vector", "sc", p.n("sc"))

    def round_to_ints(src, scale_ap, ti, cf, ei, tg, out, width):
        p.wait("scalar", "v", p.n("v"))
        p.op("scalar", lambda e: e.activation(ti[:, 0:width], src, ACT.Identity,
                                              scale=scale_ap, bias=C_HALF))
        p.op("scalar", lambda e: e.copy(cf[:, 0:width], ti[:, 0:width]), sig="sc")
        p.wait("vector", "sc", p.n("sc"))
        p.op("vector", lambda e: e.scalar_tensor_tensor(ei[:, 0:width], src, scale_ap,
                                                        cf[:, 0:width], ALU.mult, ALU.subtract))
        p.op("vector", lambda e: e.tensor_single_scalar(tg[:, 0:width], ei[:, 0:width],
                                                        -0.5, ALU.is_lt))
        p.op("vector", lambda e: e.tensor_tensor(out, cf[:, 0:width], tg[:, 0:width],
                                                 ALU.subtract), sig="v")
        return p.n("v")

    def prelude():
        for i, v in enumerate([0.5, -0.5, EPS, 1.0 / TWO_PI, np.pi / 2,
                               -math.log(10.0) / 8.0, 128.0]):
            p.op("vector", lambda e, i=i, v=v: e.memset(cst[:, i:i + 1], v))
        p.op("vector", lambda e: e.memset(ones128[:], 1.0))
        p.op("vector", lambda e: e.memset(ones1r[:], 1.0))
        p.op("gpsimd", lambda e: e.iota(idf[:], [[1, 128]], base=0, channel_multiplier=-1,
                                        allow_small_or_imprecise_dtypes=True), sig="g")
        p.wait("vector", "g", p.n("g"))
        p.op("vector", lambda e: e.tensor_single_scalar(idf[:], idf[:], 0.0, ALU.is_equal))
        p.op("vector", lambda e: e.tensor_copy(idb[:], idf[:]), sig="v")
        p.wait("tensor", "v", p.n("v"))
        p.wait("scalar", "v", p.n("v"))

    def ln1_stats():
        with nc.sbuf_tensor(_uniq("p0a_x"), [128, 2 * D], F32) as xg:
            xt = xg.ap()
            done = [None, None]
            for b in range(NB):
                sl = xt[:, (b % 2) * D:(b % 2) * D + D]
                lxs = "lx%d" % (b % 2)
                p.wait("sync", "v", done[b % 2])
                ldt = p.dma("sync", sl, x_in[b * 128:(b + 1) * 128, :], sig=lxs)
                p.wait("vector", lxs, ldt)
                p.wait("scalar", lxs, ldt)
                ln_stats_tile(sl, mu1[:, b:b + 1], rs1[:, b:b + 1])
                p.inc("vector", "v", 1)
                done[b % 2] = p.n("v")
        p.barrier()

    def ternarize(src, n_chunks, cols, wm_tile, dst_sbuf, dst_hbm):
        with contextlib.ExitStack() as st_:
            wstg = st_.enter_context(nc.sbuf_tensor(_uniq("p0_wstg"), [128, 2 * cols], F32)).ap()
            mtmp = st_.enter_context(nc.sbuf_tensor(_uniq("p0_mtmp"), [128, 2 * cols], BF16)).ap()
            acc = st_.enter_context(nc.sbuf_tensor(_uniq("p0_acc"), [128, 1], F32)).ap()
            w11 = st_.enter_context(nc.sbuf_tensor(_uniq("p0_w11"), [1, 1], F32)).ap()
            ti = st_.enter_context(nc.sbuf_tensor(_uniq("p0_ti"), [128, cols], I16)).ap()
            cf = st_.enter_context(nc.sbuf_tensor(_uniq("p0_cf"), [128, cols], BF16)).ap()
            tg = st_.enter_context(nc.sbuf_tensor(_uniq("p0_tg"), [128, cols], BF16)).ap()
            r_ = st_.enter_context(nc.sbuf_tensor(_uniq("p0_r"), [128, cols], F32)).ap()
            isc = st_.enter_context(nc.sbuf_tensor(_uniq("p0_isc"), [128, 1], F32)).ap()

            p.op("vector", lambda e: e.memset(acc[:], 0.0))
            done = [None, None]
            for c_ in range(n_chunks):
                sl = wstg[:, (c_ % 2) * cols:(c_ % 2) * cols + cols]
                lxs = "lx%d" % (c_ % 2)
                p.wait("sync", "v", done[c_ % 2])
                ldt = p.dma("sync", sl, src[c_ * 128:(c_ + 1) * 128, :], sig=lxs)
                p.wait("vector", lxs, ldt)
                p.op("vector", lambda e, sl=sl, c_=c_: e.tensor_reduce(
                    stt[:, c_ % 2:c_ % 2 + 1], sl, AX.X, ALU.add,
                    apply_absolute_value=True))
                p.op("vector", lambda e, c_=c_: e.tensor_tensor(
                    acc[:], acc[:], stt[:, c_ % 2:c_ % 2 + 1], ALU.add), sig="v")
                done[c_ % 2] = p.n("v")
            p.drain()
            p.inc("vector", "v", 1)
            p.wait("tensor", "v", p.n("v"))
            p.op("tensor", lambda e: e.matmul(psA[0:1, 0:1], ones128[:], acc[:],
                                              start=True, stop=True), sig="t")
            p.wait("vector", "t", p.n("t"))
            ntot = float(n_chunks * 128 * cols)
            p.op("vector", lambda e: e.tensor_copy(w11[:], psA[0:1, 0:1]))
            p.drain()
            p.op("vector", lambda e: e.tensor_scalar(w11[:], w11[:], 1.0 / ntot, 1e-5,
                                                     ALU.mult, ALU.max), sig="v")
            p.wait("tensor", "v", p.n("v"))
            p.op("tensor", lambda e: e.matmul(psA[0:128, 1:2], ones1r[:], w11[:],
                                              start=True, stop=True), sig="t")
            p.wait("vector", "t", p.n("t"))
            p.op("vector", lambda e: e.tensor_copy(wm_tile[:], psA[0:128, 1:2]))
            p.drain()
            p.op("vector", lambda e: e.reciprocal(isc[:], wm_tile[:]))
            p.drain()
            p.inc("vector", "v", 1)
            vsc = p.n("v")
            p.wait("scalar", "v", vsc)

            mfree = [None, None]
            for c_ in range(n_chunks):
                sl = wstg[:, (c_ % 2) * cols:(c_ % 2) * cols + cols]
                msl = mtmp[:, (c_ % 2) * cols:(c_ % 2) * cols + cols]
                lxs = "lx%d" % (c_ % 2)
                p.wait("sync", "v", done[c_ % 2])
                ldt = p.dma("sync", sl, src[c_ * 128:(c_ + 1) * 128, :], sig=lxs)
                p.wait("vector", lxs, ldt)
                p.wait("scalar", lxs, ldt)
                p.wait("vector", "stM%d" % (c_ % 2), mfree[c_ % 2])
                vq = round_to_ints(sl, isc[:, 0:1], ti, cf, r_, tg, msl, cols)
                p.wait("vector", "v", vq)
                p.op("vector", lambda e, msl=msl: e.tensor_scalar(
                    msl, msl, 1.0, -1.0, ALU.min, ALU.max), sig="v")
                done[c_ % 2] = p.n("v")
                p.wait("scalar", "v", p.n("v"))
                mfree[c_ % 2] = p.dma("scalar", dst_hbm[:, c_ * cols:(c_ + 1) * cols],
                                      msl, sig="stM%d" % (c_ % 2))
            p.barrier()

    def ffn_phase(src_hbm, dst_hbm, mu_t, rs_t, f2_t, mg, mu_, post_stats):
        HW2 = INNER // 2
        with contextlib.ExitStack() as st_:
            sb = lambda n, sh, dt: st_.enter_context(nc.sbuf_tensor(_uniq(n), sh, dt)).ap()
            if mg is None:
                mg = sb("fa_mg", [128, 8 * INNER], BF16)
                mu_ = sb("fa_mu", [128, 8 * INNER], BF16)
                for c_ in range(8):
                    p.dma("sync", mg[:, c_ * INNER:(c_ + 1) * INNER],
                          Mgs[:, c_ * INNER:(c_ + 1) * INNER], sig="lw")
                for c_ in range(8):
                    p.dma("sync", mu_[:, c_ * INNER:(c_ + 1) * INNER],
                          Mus[:, c_ * INNER:(c_ + 1) * INNER], sig="lw")
                p.wait("tensor", "lw", p.n("lw"))
            h_t = sb("fa_h", [128, INNER], F32)
            gsm = sb("fa_g", [128, 512], F32)
            xin = sb("fa_xin", [128, 2 * D], F32)
            xn = sb("fa_xn", [128, D], F32)
            ti1 = sb("fa_ti1", [128, D], I16)
            cf1 = sb("fa_cf1", [128, D], BF16)
            tg1 = ti1.bitcast(BF16)
            e1 = sb("fa_e1", [128, D], F32)
            k1 = sb("fa_k1", [128, 2 * D], BF16)
            k1T = sb("fa_k1T", [128, D], BF16)
            HW4 = INNER // 4
            ti2 = ti1
            cf2 = cf1
            tg2 = tg1
            e2 = e1
            k2h = sb("fa_k2h", [128, INNER], BF16)
            k2T = sb("fa_k2T", [128, INNER], BF16)
            sc1c = sb("fa_sc1", [128, 2], F32)
            sc2c = sb("fa_sc2", [128, 2], F32)
            f1g = sb("fa_f1g", [128, 2], F32)
            f1u = sb("fa_f1u", [128, 2], F32)

            rotp = [(psA[:, 0:512], psA[:, 512:1024]),
                    (psC[:, 0:512], psC[:, 512:1024]),
                    (psD[:, 0:512], psD[:, 512:1024])]
            psBb = psB.bitcast(BF16)
            rot_free = [None, None, None]
            k2_store = [None]
            k2ev_done = [None]
            k2h_v = {}
            xin_free = [None, None]

            k1_vk = {}

            def front_dve(b):
                xb = xin[:, (b % 2) * D:(b % 2) * D + D]
                k1s = k1[:, (b % 2) * D:(b % 2) * D + D]
                lxs = "lx%d" % (b % 2)
                p.wait("sync", "v", xin_free[b % 2])
                ldt = p.dma("sync", xb[:], src_hbm[b * 128:(b + 1) * 128, :], sig=lxs)
                p.wait("vector", lxs, ldt)
                p.op("vector", lambda e, b=b, xb=xb: e.tensor_scalar(
                    xn[:], xb[:], mu_t[:, b:b + 1], rs_t[:, b:b + 1],
                    ALU.subtract, ALU.mult), sig="v")
                xin_free[b % 2] = p.n("v")
                sc1 = sc1c[:, b % 2:b % 2 + 1]
                fg = f1g[:, b % 2:b % 2 + 1]
                fu = f1u[:, b % 2:b % 2 + 1]
                p.op("vector", lambda e: e.tensor_reduce(stt[:, 0:1], xn[:], AX.X, ALU.max,
                                                         apply_absolute_value=True))
                p.op("vector", lambda e: e.tensor_single_scalar(stt[:, 0:1], stt[:, 0:1],
                                                                1e-5, ALU.max))
                p.op("vector", lambda e, fg=fg: e.scalar_tensor_tensor(
                    fg, stt[:, 0:1], 1.0 / 127.0, wmg[:], ALU.mult, ALU.mult))
                p.op("vector", lambda e, fu=fu: e.scalar_tensor_tensor(
                    fu, stt[:, 0:1], 1.0 / 127.0, wmu[:], ALU.mult, ALU.mult))
                p.op("vector", lambda e: e.reciprocal(stt[:, 1:2], stt[:, 0:1]))
                p.op("vector", lambda e, sc1=sc1: e.tensor_single_scalar(
                    sc1, stt[:, 1:2], 127.0, ALU.mult))
                p.inc("vector", "v", 1)
                p.wait("scalar", "v", p.n("v"))
                k1_vk[b] = round_to_ints(xn[:], sc1, ti1, cf1, e1, tg1, k1s, D)

            def front_pe(b):
                k1s = k1[:, (b % 2) * D:(b % 2) * D + D]
                p.wait("tensor", "v", k1_vk[b])
                for c_ in range(8):
                    fn = lambda e, c_=c_, k1s=k1s: e.transpose(
                        psBb[:, c_ * 128:(c_ + 1) * 128],
                        k1s[:, c_ * 128:(c_ + 1) * 128], idb[:])
                    p.op("tensor", fn, sig="t" if c_ == 7 else None)
                p.wait("scalar", "t", p.n("t"))
                p.op("scalar", lambda e: e.copy(k1T[:], psBb[:, 0:1024]), sig="sc")
                p.wait("tensor", "sc", p.n("sc"))

            def emit_gu(b, ots):
                fg = f1g[:, b % 2:b % 2 + 1]
                fu = f1u[:, b % 2:b % 2 + 1]
                for ot in ots:
                    gbank, ubank = rotp[ot % 3]
                    p.wait("tensor", "v", rot_free[ot % 3])
                    for i in range(8):
                        p.op("tensor", lambda e, gbank=gbank, i=i, ot=ot: e.matmul(
                            gbank, k1T[:, i * 128:(i + 1) * 128],
                            mg[:, i * INNER + ot * 512: i * INNER + ot * 512 + 512],
                            start=(i == 0), stop=(i == 7)))
                        p.op("tensor", lambda e, ubank=ubank, i=i, ot=ot: e.matmul(
                            ubank, k1T[:, i * 128:(i + 1) * 128],
                            mu_[:, i * INNER + ot * 512: i * INNER + ot * 512 + 512],
                            start=(i == 0), stop=(i == 7)),
                            sig="t" if i == 7 else None)
                    tt = p.n("t")
                    p.wait("scalar", "t", tt)
                    p.op("scalar", lambda e, gbank=gbank, fg=fg: e.activation(
                        gsm[:], gbank, ACT.Silu, scale=fg), sig="sc")
                    p.wait("vector", "sc", p.n("sc"))
                    p.op("vector", lambda e, ubank=ubank, fu=fu, ot=ot:
                         e.scalar_tensor_tensor(h_t[:, ot * 512:(ot + 1) * 512], ubank,
                                                fu, gsm[:], ALU.mult, ALU.mult), sig="v")
                    rot_free[ot % 3] = p.n("v")
                    p.wait("scalar", "v", p.n("v"))

            def emit_hq1(b):
                sc2 = sc2c[:, b % 2:b % 2 + 1]
                p.op("vector", lambda e: e.tensor_reduce(stt[:, 2:3], h_t[:], AX.X, ALU.max,
                                                         apply_absolute_value=True))
                p.op("vector", lambda e: e.tensor_single_scalar(stt[:, 2:3], stt[:, 2:3],
                                                                1e-5, ALU.max))
                p.op("vector", lambda e, b=b: e.scalar_tensor_tensor(
                    f2_t[:, b:b + 1], stt[:, 2:3], 0.5 / 127.0, wmd[:],
                    ALU.mult, ALU.mult))
                p.op("vector", lambda e: e.reciprocal(stt[:, 3:4], stt[:, 2:3]))
                p.op("vector", lambda e, sc2=sc2: e.tensor_single_scalar(
                    sc2, stt[:, 3:4], 127.0, ALU.mult))
                p.inc("vector", "v", 1)
                p.wait("scalar", "v", p.n("v"))
                for hh in range(4):
                    hsl = h_t[:, hh * HW4:(hh + 1) * HW4]
                    k2h_v[(b, hh)] = round_to_ints(hsl, sc2, ti2, cf2, e2, tg2,
                                                   k2h[:, hh * HW4:(hh + 1) * HW4], HW4)

            def emit_k2part(b):
                if k2ev_done[0] is not None:
                    p.wait("tensor", k2ev_done[0][0], k2ev_done[0][1])
                for hh in range(4):
                    p.wait("tensor", "v", k2h_v[(b, hh)])
                    for c_ in range(8):
                        fn = lambda e, hh=hh, c_=c_: e.transpose(
                            psBb[:, 1024 + c_ * 128: 1024 + (c_ + 1) * 128],
                            k2h[:, hh * HW4 + c_ * 128: hh * HW4 + (c_ + 1) * 128], idb[:])
                        p.op("tensor", fn, sig="t")
                    tq = p.n("t")
                    p.wait("scalar", "t", tq)
                    if hh == 0:
                        p.wait("scalar", "stK", k2_store[0])
                    p.op("scalar", lambda e, hh=hh: e.copy(
                        k2T[:, hh * HW4:(hh + 1) * HW4], psBb[:, 1024:2048]), sig="sc")
                    if hh < 3:
                        p.wait("tensor", "sc", p.n("sc"))
                k2ev_done[0] = ("sc", p.n("sc"))
                p.wait("scalar", "sc", p.n("sc"))
                k2_store[0] = p.dma("scalar", K2s[b], k2T[:], sig="stK")

            front_dve(0)
            front_pe(0)
            for b in range(NB):
                emit_gu(b, range(0, 4))
                if b + 1 < NB:
                    front_dve(b + 1)
                emit_gu(b, range(4, 8))
                if b >= 1:
                    emit_k2part(b - 1)
                if b + 1 < NB:
                    front_pe(b + 1)
                emit_hq1(b)
            emit_k2part(NB - 1)
            p.barrier()
        with contextlib.ExitStack() as st_:
            sb = lambda n, sh, dt: st_.enter_context(nc.sbuf_tensor(_uniq(n), sh, dt)).ap()
            md = sb("fb_md", [128, 32 * D], BF16)
            k2i = sb("fb_k2i", [128, 2 * INNER], BF16)
            xin2 = sb("fb_xin2", [128, 2 * D], F32)
            xo = sb("fb_xo", [128, 2 * D], F32)
            for c_ in range(8):
                p.dma("sync", md[:, c_ * 4 * D:(c_ + 1) * 4 * D],
                      Mds[:, c_ * 4 * D:(c_ + 1) * 4 * D], sig="lw")
            p.wait("tensor", "lw", p.n("lw"))
            mm_done = [None, None]
            xo_free = [None, None]
            k2_free = [None, None]
            xin_free = [None, None]
            for b in range(NB):
                kb = k2i[:, (b % 2) * INNER:(b % 2) * INNER + INNER]
                xb = xin2[:, (b % 2) * D:(b % 2) * D + D]
                xob = xo[:, (b % 2) * D:(b % 2) * D + D]
                lks = "lk%d" % (b % 2)
                lxs = "lx%d" % (b % 2)
                p.wait("sync", "t", k2_free[b % 2])
                l1 = p.dma("sync", kb[:], K2s[b], sig=lks)
                p.wait("sync", "v", xin_free[b % 2])
                l2 = p.dma("sync", xb[:], src_hbm[b * 128:(b + 1) * 128, :], sig=lxs)
                p.wait("tensor", lks, l1)
                p.wait("tensor", "v", mm_done[0])
                for oc in range(32):
                    for d2 in range(2):
                        last = (oc == 31 and d2 == 1)
                        p.op("tensor", lambda e, oc=oc, d2=d2, kb=kb: e.matmul(
                            psA[:, d2 * 512:(d2 + 1) * 512],
                            kb[:, oc * 128:(oc + 1) * 128],
                            md[:, oc * D + d2 * 512: oc * D + d2 * 512 + 512],
                            start=(oc == 0), stop=(oc == 31)),
                            sig="t" if last else None)
                k2_free[b % 2] = p.n("t")
                p.wait("vector", "t", p.n("t"))
                p.wait("vector", lxs, l2)
                p.wait("vector", "stX%d" % (b % 2), xo_free[b % 2])
                p.op("vector", lambda e, b=b, xb=xb, xob=xob: e.scalar_tensor_tensor(
                    xob[:], psA[:], f2_t[:, b:b + 1], xb[:], ALU.mult, ALU.add), sig="v")
                mm_done[0] = p.n("v")
                xin_free[b % 2] = p.n("v")
                p.wait("scalar", "v", p.n("v"))
                if post_stats is not None:
                    mun, rsn = post_stats
                    ln_stats_tile(xob, mun[:, b:b + 1], rsn[:, b:b + 1])
                xo_free[b % 2] = p.dma("scalar", dst_hbm[b * 128:(b + 1) * 128, :], xob[:],
                                       sig="stX%d" % (b % 2))
            p.barrier()

    def attention(src_hbm, dst_hbm, mu_t, rs_t, post_stats):
        with contextlib.ExitStack() as st_:
            sb = lambda n, sh, dt: st_.enter_context(nc.sbuf_tensor(_uniq(n), sh, dt)).ap()
            sintab = sb("at_sin", [128, NB * 32], F32)
            costab = sb("at_cos", [128, NB * 32], F32)
            dmask = sb("at_dmask", [128, H * 128], F32)
            mstab = sb("at_ms", [128, 8 * 128], F32)
            mbc = sb("at_mbc", [128, H], F32)
            scc = sb("at_scc", [128, H], F32)
            lns = sb("at_lns", [128, H], F32)
            lns2 = sb("at_lns2", [128, 8], F32)
            svals = sb("at_sv", [128, H], F32)
            gbc = sb("at_gbc", [128, D], F32)
            bbc = sb("at_bbc", [128, D], F32)
            grow = sb("at_grow", [1, D], F32)
            brow = sb("at_brow", [1, D], F32)
            ssb = sb("at_ssb", [128, H * DH], F32)
            wstg = sb("at_wstg", [128, 8 * D], F32)
            xnT = sb("at_xnT", [128, BPE * D], F32)
            vq_t = sb("at_vq", [128, BPE * D], F32)
            qT = sb("at_qT", [128, BPE * D], F32)
            kT = sb("at_kT", [128, BPE * D], F32)
            kbq = sb("at_kbq", [128, BPE * D], F32)
            xab = sb("at_xab", [128, 2 * D], F32)
            xn2 = sb("at_xn2", [128, D], F32)
            rot = sb("at_rot", [128, D], F32)
            adt = sb("at_adt", [128, 128], F32)
            qst = sb("at_qst", [128, 128], F32)
            rr1 = sb("at_rr1", [128, D], F32)
            rr2 = sb("at_rr2", [128, D], F32)
            rrT = sb("at_rrT", [128, D], F32)
            gnq = sb("at_gnq", [128, H], F32)
            gnm = sb("at_gnm", [128, H], F32)
            gnr = sb("at_gnr", [128, H], F32)
            xo2 = sb("at_xo2", [128, 2 * D], F32)
            ta = sb("at_ta", [128, 1024], F32)
            tb_ = sb("at_tb", [128, 1024], F32)
            ti3 = sb("at_ti3", [128, 1024], I16)

            g1 = p.dma("sync", grow[:], gam_d[:], sig="lw")
            g2 = p.dma("sync", brow[:], bet_d[:], sig="lw")
            p.wait("tensor", "lw", g2)
            for half in range(2):
                p.op("tensor", lambda e, half=half: e.matmul(
                    psA[:, half * 512:(half + 1) * 512], ones1r[:],
                    grow[0:1, half * 512:(half + 1) * 512], start=True, stop=True))
                p.op("tensor", lambda e, half=half: e.matmul(
                    psB[:, half * 512:(half + 1) * 512], ones1r[:],
                    brow[0:1, half * 512:(half + 1) * 512], start=True, stop=True),
                    sig="t" if half == 1 else None)
            p.wait("vector", "t", p.n("t"))
            p.op("vector", lambda e: e.tensor_copy(gbc[:], psA[:]))
            p.op("vector", lambda e: e.tensor_copy(bbc[:], psB[:]), sig="v")
            p.wait("tensor", "v", p.n("v"))

            for hh in range(H):
                p.op("vector", lambda e, hh=hh: e.memset(svals[:, hh:hh + 1], DECAY[hh]))
            p.inc("vector", "v", 1)
            p.wait("scalar", "v", p.n("v"))
            p.op("scalar", lambda e: e.activation(lns[:], svals[:], ACT.Ln), sig="sc")
            p.wait("vector", "sc", p.n("sc"))
            for pc in range(8):
                p.op("vector", lambda e, pc=pc: e.tensor_copy(lns2[0:64, pc:pc + 1],
                                                              lns[0:64, 2 * pc:2 * pc + 1]))
                p.op("vector", lambda e, pc=pc: e.tensor_copy(
                    lns2[64:128, pc:pc + 1], lns[64:128, 2 * pc + 1:2 * pc + 2]))
            p.inc("vector", "v", 1)
            p.wait("scalar", "v", p.n("v"))
            p.wait("gpsimd", "v", p.n("v"))
            p.op("gpsimd", lambda e: e.iota(ta[:, 0:128], [[1, 128]], base=0,
                                            channel_multiplier=-1,
                                            allow_small_or_imprecise_dtypes=True))
            p.op("gpsimd", lambda e: e.iota(ta[:, 128:256], [[1, 128]], base=1,
                                            channel_multiplier=0,
                                            allow_small_or_imprecise_dtypes=True))
            p.op("gpsimd", lambda e: e.iota(ta[:, 256:257], [[1, 1]], base=127,
                                            channel_multiplier=-1,
                                            allow_small_or_imprecise_dtypes=True))
            p.op("gpsimd", lambda e: e.iota(ta[:, 512:544], [[128, 32]], base=0,
                                            channel_multiplier=1,
                                            allow_small_or_imprecise_dtypes=True),
                 sig="g")
            gt_ = p.n("g")
            p.wait("vector", "g", gt_)
            p.wait("scalar", "g", gt_)
            p.op("vector", lambda e: e.tensor_single_scalar(tb_[:, 0:128], ta[:, 0:128],
                                                            0.0, ALU.is_ge), sig="v")
            p.wait("scalar", "v", p.n("v"))
            for hh in range(H):
                p.op("scalar", lambda e, hh=hh: e.activation(
                    dmask[:, hh * 128:(hh + 1) * 128], ta[:, 0:128], ACT.Exp,
                    scale=lns[:, hh:hh + 1]))
                p.op("scalar", lambda e, hh=hh: e.activation(
                    mbc[:, hh:hh + 1], ta[:, 256:257], ACT.Exp, scale=lns[:, hh:hh + 1]))
                p.op("scalar", lambda e, hh=hh: e.activation(
                    scc[:, hh:hh + 1], C_128, ACT.Exp, scale=lns[:, hh:hh + 1]))
            for pc in range(8):
                p.op("scalar", lambda e, pc=pc: e.activation(
                    mstab[:, pc * 128:(pc + 1) * 128], ta[:, 128:256], ACT.Exp,
                    scale=lns2[:, pc:pc + 1]), sig="sc" if pc == 7 else None)
            p.wait("vector", "sc", p.n("sc"))
            for hh in range(H):
                p.op("vector", lambda e, hh=hh: e.tensor_tensor(
                    dmask[:, hh * 128:(hh + 1) * 128], dmask[:, hh * 128:(hh + 1) * 128],
                    tb_[:, 0:128], ALU.mult))
            p.op("vector", lambda e: e.memset(ssb[:], 0.0))
            p.inc("vector", "v", 1)
            p.wait("gpsimd", "v", p.n("v"))
            p.op("gpsimd", lambda e: e.iota(tb_[:, 224:256], [[128, 32]], base=0,
                                            channel_multiplier=1,
                                            allow_small_or_imprecise_dtypes=True))
            p.op("gpsimd", lambda e: e.iota(tb_[:, 160:192], [[1, 32]], base=0,
                                            channel_multiplier=0,
                                            allow_small_or_imprecise_dtypes=True), sig="g")
            p.wait("scalar", "g", p.n("g"))
            p.op("scalar", lambda e: e.activation(tb_[:, 192:224], tb_[:, 160:192], ACT.Exp,
                                                  scale=C_LNA), sig="sc")
            p.wait("vector", "sc", p.n("sc"))
            p.op("vector", lambda e: e.tensor_tensor(
                ta[:].rearrange("p (b u) -> p b u", u=32),
                tb_[:, 192:224].rearrange("p (a u) -> p a u", a=1).broadcast_to([128, 32, 32]),
                tb_[:, 224:256].rearrange("p (b a) -> p b a", a=1).broadcast_to([128, 32, 32]),
                ALU.mult), sig="v")
            p.wait("scalar", "v", p.n("v"))

            def sincos(dst, phase_shift):
                p.op("vector", lambda e: e.tensor_single_scalar(tb_[:], ta[:],
                                                                phase_shift, ALU.add))
                p.op("vector", lambda e: e.tensor_single_scalar(rot[:], tb_[:],
                                                                1.0 / TWO_PI, ALU.mult),
                     sig="v")
                p.wait("scalar", "v", p.n("v"))
                p.op("scalar", lambda e: e.activation(ti3[:], rot[:], ACT.Identity,
                                                      bias=C_HALF))
                p.op("scalar", lambda e: e.copy(xn2[:], ti3[:]), sig="sc")
                p.wait("vector", "sc", p.n("sc"))
                p.op("vector", lambda e: e.tensor_tensor(rot[:], rot[:], xn2[:],
                                                         ALU.subtract))
                p.op("vector", lambda e: e.tensor_single_scalar(rot[:], rot[:], -0.5,
                                                                ALU.is_lt))
                p.op("vector", lambda e: e.tensor_tensor(xn2[:], xn2[:], rot[:],
                                                         ALU.subtract))
                p.op("vector", lambda e: e.scalar_tensor_tensor(tb_[:], xn2[:], -TWO_PI,
                                                                tb_[:], ALU.mult, ALU.add))
                p.op("vector", lambda e: e.tensor_scalar(tb_[:], tb_[:], PI_CLAMP,
                                                         -PI_CLAMP, ALU.min, ALU.max),
                     sig="v")
                p.wait("scalar", "v", p.n("v"))
                p.op("scalar", lambda e, dst=dst: e.activation(dst, tb_[:], ACT.Sin),
                     sig="sc")
                p.wait("vector", "sc", p.n("sc"))

            sincos(sintab, 0.0)
            sincos(costab, np.pi / 2)
            p.inc("scalar", "sc", 1)
            p.wait("tensor", "sc", p.n("sc"))
            p.wait("sync", "sc", p.n("sc"))

            s_dma = [None]
            s_copy = [None]
            wo_evac = [None, None]
            xo_free2 = [None, None]
            xab_free = [None, None]

            def load_w(src, gate_t):
                p.wait("sync", "t", gate_t)
                for i in range(8):
                    tgt = p.dma("sync", wstg[:, i * D:(i + 1) * D],
                                src[i * 128:(i + 1) * 128, :], sig="lw")
                return tgt

            wstg_gate = [None]
            t_heads = [None]
            gn_psC = [None]
            for ei in range(NEI):
                blocks = [ei * BPE + k for k in range(BPE)]
                wl = load_w(wvt, wstg_gate[0])
                p.wait("tensor", "lw", wl)
                psPP = [psB, psC]
                pp_free = [("v", gn_psC[0]) if gn_psC[0] else None,
                           ("v", gn_psC[0]) if gn_psC[0] else None]
                trT = [None]
                def a1_xnv(lb, b):
                    xb = xab[:, (lb % 2) * D:(lb % 2) * D + D]
                    lxs = "lx%d" % (lb % 2)
                    p.wait("sync", "v", xab_free[lb % 2])
                    ldt = p.dma("sync", xb[:], src_hbm[b * 128:(b + 1) * 128, :], sig=lxs)
                    p.wait("vector", lxs, ldt)
                    p.wait("vector", "t", trT[0])
                    p.op("vector", lambda e, b=b, xb=xb: e.tensor_scalar(
                        xn2[:], xb[:], mu_t[:, b:b + 1], rs_t[:, b:b + 1],
                        ALU.subtract, ALU.mult), sig="v")
                    xab_free[lb % 2] = p.n("v")
                    p.wait("tensor", "v", p.n("v"))
                    for c_ in range(8):
                        p.op("tensor", lambda e, c_=c_: e.transpose(
                            psA[:, c_ * 128:(c_ + 1) * 128],
                            xn2[:, c_ * 128:(c_ + 1) * 128], idf[:]),
                            sig="t" if c_ == 7 else None)
                    trT[0] = p.n("t")
                    p.wait("scalar", "t", p.n("t"))
                    p.op("scalar", lambda e, lb=lb: e.copy(
                        xnT[:, lb * D:(lb + 1) * D], psA[:]), sig="sc")
                    p.wait("tensor", "sc", p.n("sc"))

                def a1_proj(lb):
                    ps = psPP[lb % 2]
                    if pp_free[lb % 2] is not None:
                        p.wait("tensor", pp_free[lb % 2][0], pp_free[lb % 2][1])
                    for i in range(8):
                        for v2 in range(2):
                            p.op("tensor", lambda e, ps=ps, i=i, v2=v2, lb=lb: e.matmul(
                                ps[:, v2 * 512:(v2 + 1) * 512],
                                xnT[:, lb * D + i * 128: lb * D + (i + 1) * 128],
                                wstg[:, i * D + v2 * 512: i * D + v2 * 512 + 512],
                                start=(i == 0), stop=(i == 7)),
                                sig="t" if (i == 7 and v2 == 1) else None)
                    wstg_gate[0] = p.n("t")
                    return p.n("t")

                def a1_vevac(lb, tproj):
                    ps = psPP[lb % 2]
                    p.wait("scalar", "t", tproj)
                    p.op("scalar", lambda e, ps=ps, lb=lb: e.copy(
                        vq_t[:, lb * D:(lb + 1) * D], ps[:]), sig="sc")
                    pp_free[lb % 2] = ("sc", p.n("sc"))

                pend = None
                for lb, b in enumerate(blocks):
                    a1_xnv(lb, b)
                    tp = a1_proj(lb)
                    if pend is not None:
                        a1_vevac(*pend)
                    pend = (lb, tp)
                a1_vevac(*pend)

                for which, wsrc, dstT in (("q", wqt, qT), ("k", wkt, kT)):
                    wl = load_w(wsrc, wstg_gate[0])
                    p.wait("tensor", "lw", wl)

                    def qk_rot(lb, b, tproj, which=which, dstT=dstT):
                        ps = psPP[lb % 2]
                        cb = costab[:, b * 32:(b + 1) * 32]
                        sb_ = sintab[:, b * 32:(b + 1) * 32]
                        bc = lambda t: t.rearrange("p (a u) -> p a u", a=1).broadcast_to(
                            [128, H, 32])
                        x0 = ps[:].rearrange("p (h c u) -> p h c u", h=H, c=2)[:, :, 0, :]
                        x1 = ps[:].rearrange("p (h c u) -> p h c u", h=H, c=2)[:, :, 1, :]
                        r0 = rot[:].rearrange("p (h c u) -> p h c u", h=H, c=2)[:, :, 0, :]
                        r1 = rot[:].rearrange("p (h c u) -> p h c u", h=H, c=2)[:, :, 1, :]
                        ta0 = ta[:, 0:512].rearrange("p (h u) -> p h u", h=H)
                        ta1 = ta[:, 512:1024].rearrange("p (h u) -> p h u", h=H)
                        p.wait("vector", "t", tproj)
                        p.wait("vector", "t", trT[0])
                        p.op("vector", lambda e, x0=x0, cb=cb, ta0=ta0: e.tensor_tensor(
                            ta0, x0, bc(cb), ALU.mult))
                        p.op("vector", lambda e, x1=x1, sb_=sb_, ta1=ta1: e.tensor_tensor(
                            ta1, x1, bc(sb_), ALU.mult))
                        p.op("vector", lambda e, r0=r0, ta0=ta0, ta1=ta1: e.tensor_tensor(
                            r0, ta0, ta1, ALU.subtract))
                        p.op("vector", lambda e, x0=x0, sb_=sb_, ta0=ta0: e.tensor_tensor(
                            ta0, x0, bc(sb_), ALU.mult))
                        p.op("vector", lambda e, x1=x1, cb=cb, ta1=ta1: e.tensor_tensor(
                            ta1, x1, bc(cb), ALU.mult))
                        p.op("vector", lambda e, r1=r1, ta0=ta0, ta1=ta1: e.tensor_tensor(
                            r1, ta0, ta1, ALU.add), sig="v")
                        pp_free[lb % 2] = ("v", p.n("v"))
                        if which == "k":
                            p.op("vector", lambda e, lb=lb: e.tensor_tensor(
                                kbq[:, lb * D:(lb + 1) * D].rearrange(
                                    "p (h u) -> p h u", h=H),
                                rot[:].rearrange("p (h u) -> p h u", h=H),
                                mbc[:].rearrange("p (h a) -> p h a", a=1).broadcast_to(
                                    [128, H, DH]), ALU.mult), sig="v")
                        vrot = p.n("v")
                        p.wait("tensor", "v", vrot)
                        for c_ in range(8):
                            p.op("tensor", lambda e, c_=c_: e.transpose(
                                psA[:, c_ * 128:(c_ + 1) * 128],
                                rot[:, c_ * 128:(c_ + 1) * 128], idf[:]),
                                sig="t" if c_ == 7 else None)
                        trT[0] = p.n("t")
                        p.wait("scalar", "t", p.n("t"))
                        p.op("scalar", lambda e, lb=lb, dstT=dstT: e.copy(
                            dstT[:].rearrange("p (c t) -> p c t", c=8)[:, :, lb * 128:
                                                                      (lb + 1) * 128],
                            psA[:].rearrange("p (c t) -> p c t", c=8)), sig="sc")
                        p.wait("tensor", "sc", p.n("sc"))

                    pend = None
                    for lb, b in enumerate(blocks):
                        tp = a1_proj(lb)
                        if pend is not None:
                            qk_rot(*pend)
                        pend = (lb, b, tp)
                    qk_rot(*pend)

                wl = load_w(wo_d, wstg_gate[0])

                gn_a = xnT[:, 0:D]
                gn_b = xnT[:, D:2 * D]
                woT = xnT[:, 2 * D:3 * D]

                def emit_A(b, lc, first):
                    p.wait("tensor", "lw", wl)
                    qstall = rrT
                    if not first:
                        p.wait("vector", "t", t_heads[0])
                        qv = qT[:].rearrange("p (pc t) -> p pc t", pc=8)[
                            :, :, lc * 128:(lc + 1) * 128]
                        p.op("vector", lambda e, qv=qv: e.tensor_tensor(
                            qstall[:].rearrange("p (pc t) -> p pc t", pc=8), qv,
                            mstab[:].rearrange("p (pc t) -> p pc t", pc=8), ALU.mult),
                            sig="v")
                        p.wait("tensor", "v", p.n("v"))
                    adt_v = [None, None]
                    for hf in range(2):
                        if hf == 1:
                            p.wait("tensor", "v", adt_v[0])
                        for j in (0, 2, 4, 6, 1, 3, 5, 7):
                            hh = hf * 8 + j
                            pc = hh // 2
                            base = 64 * (hh % 2)
                            ksl = kT[base:base + 64,
                                     pc * 512 + lc * 128: pc * 512 + (lc + 1) * 128]
                            qsl = qT[base:base + 64,
                                     pc * 512 + lc * 128: pc * 512 + (lc + 1) * 128]
                            p.op("tensor", lambda e, j=j, ksl=ksl, qsl=qsl: e.matmul(
                                psB[:, j * 128:(j + 1) * 128], ksl, qsl,
                                start=(j % 4 == 0), stop=True,
                                skip_group_check=(j % 4 != 0)),
                                sig="t" if j == 7 else None)
                        p.wait("vector", "t", p.n("t"))
                        p.op("vector", lambda e, hf=hf: e.tensor_tensor(
                            [rr1, rr2][hf][:], psB[:], dmask[:, hf * 1024:(hf + 1) * 1024],
                            ALU.mult), sig="v")
                        adt_v[hf] = p.n("v")
                    return adt_v

                def emit_R(b, lc, first, adt_v):
                    p.wait("tensor", "sd", s_dma[0])
                    p.wait("tensor", "v", gn_psC[0])
                    for hf in range(2):
                        p.wait("tensor", "v", adt_v[hf])
                        for j in range(8):
                            hh = hf * 8 + j
                            base = 64 * (hh % 2)
                            vsl = vq_t[:, lc * D + hh * DH: lc * D + (hh + 1) * DH]
                            p.op("tensor", lambda e, hh=hh, j=j, hf=hf, vsl=vsl,
                                 first=first: e.matmul(
                                     psC[:, hh * DH:(hh + 1) * DH],
                                     [rr1, rr2][hf][:, j * 128:(j + 1) * 128], vsl,
                                     start=True, stop=first),
                                 sig="t" if (first and hh == H - 1) else None)
                            if not first:
                                p.op("tensor", lambda e, hh=hh, base=base: e.matmul(
                                    psC[:, hh * DH:(hh + 1) * DH],
                                    _qst_ap(rrT, hh, lc),
                                    ssb[base:base + 64, hh * DH:(hh + 1) * DH],
                                    start=False, stop=True),
                                    sig="t" if hh == H - 1 else None)
                    t_heads[0] = p.n("t")

                gn_c = xnT[:, 3 * D:4 * D]

                def emit_S(b, lc, first):
                    th = t_heads[0]
                    p.wait("vector", "t", th)
                    if not first:
                        p.op("vector", lambda e: e.tensor_tensor(
                            psD[0:64, :].rearrange("p (h u) -> p h u", h=H),
                            psD[0:64, :].rearrange("p (h u) -> p h u", h=H),
                            scc[0:64, :].rearrange("p (h a) -> p h a", a=1).broadcast_to(
                                [64, H, DH]), ALU.mult), sig="v")
                        p.wait("tensor", "v", p.n("v"))
                    for hh in range(H):
                        ksl2 = kbq[:, lc * D + hh * DH: lc * D + (hh + 1) * DH]
                        vsl2 = vq_t[:, lc * D + hh * DH: lc * D + (hh + 1) * DH]
                        st_flag = first and (hh % 8 == 0)
                        sp_flag = (hh % 8 == 7)
                        p.op("tensor", lambda e, hh=hh, ksl2=ksl2, vsl2=vsl2,
                             st_flag=st_flag, sp_flag=sp_flag, sk=not first: e.matmul(
                                 psD[0:64, hh * DH:(hh + 1) * DH], ksl2, vsl2,
                                 start=st_flag, stop=sp_flag, skip_group_check=sk),
                             sig="t" if hh == H - 1 else None)
                    p.wait("vector", "t", p.n("t"))
                    p.op("vector", lambda e: e.tensor_copy(ssb[0:64, :], psD[0:64, :]),
                         sig="v")
                    p.wait("sync", "v", p.n("v"))
                    s_dma[0] = p.dma("sync", ssb[64:128, :], ssb[0:64, :], sig="sd")
                    p.op("vector", lambda e: e.tensor_copy(gn_c, psC[:]), sig="v")
                    gn_psC[0] = p.n("v")

                def emit_gnWO(b, lc, first):
                    p.wait("scalar", "v", gn_psC[0])
                    p.op("scalar", lambda e: e.activation(dump[:], gn_c, ACT.Square),
                         sig="sc")
                    p.wait("vector", "sc", p.n("sc"))
                    p.op("vector", lambda e: e.tensor_reduce(
                        gnm[:], gn_c.rearrange("p (h u) -> p h u", h=H), AX.X, ALU.add))
                    p.op("vector", lambda e: e.tensor_reduce(
                        gnq[:], dump[:].rearrange("p (h u) -> p h u", h=H), AX.X, ALU.add))
                    p.op("vector", lambda e: e.tensor_single_scalar(gnm[:], gnm[:],
                                                                    1.0 / DH, ALU.mult))
                    p.op("vector", lambda e: e.tensor_tensor(ta[:, 0:H], gnm[:], gnm[:],
                                                             ALU.mult))
                    p.op("vector", lambda e: e.scalar_tensor_tensor(
                        gnr[:], gnq[:], 1.0 / DH, ta[:, 0:H], ALU.mult, ALU.subtract),
                        sig="v")
                    p.wait("scalar", "v", p.n("v"))
                    p.op("scalar", lambda e: e.activation(gnr[:], gnr[:], ACT.Ln,
                                                          bias=C_EPS))
                    p.op("scalar", lambda e: e.activation(gnr[:], gnr[:], ACT.Exp,
                                                          scale=C_NEGH), sig="sc")
                    p.wait("vector", "sc", p.n("sc"))
                    hview = lambda t: t.rearrange("p (h u) -> p h u", h=H)
                    hbc = lambda t: t.rearrange("p (h a) -> p h a", a=1).broadcast_to(
                        [128, H, DH])
                    p.op("vector", lambda e: e.tensor_tensor(
                        hview(gn_a), gn_c.rearrange("p (h u) -> p h u", h=H),
                        hbc(gnm[:]), ALU.subtract))
                    p.op("vector", lambda e: e.tensor_tensor(hview(gn_b), hview(gbc[:]),
                                                             hbc(gnr[:]), ALU.mult))
                    p.op("vector", lambda e: e.tensor_tensor(gn_a, gn_a, gn_b, ALU.mult))
                    p.op("vector", lambda e: e.tensor_tensor(gn_a, gn_a, bbc[:],
                                                             ALU.add), sig="v")
                    p.wait("tensor", "v", p.n("v"))
                    for c_ in range(8):
                        p.op("tensor", lambda e, c_=c_: e.transpose(
                            psA[:, c_ * 128:(c_ + 1) * 128], gn_a[:, c_ * 128:(c_ + 1) * 128],
                            idf[:]), sig="t" if c_ == 7 else None)
                    p.wait("vector", "t", p.n("t"))
                    p.op("vector", lambda e: e.tensor_copy(woT[:], psA[:]), sig="v")
                    p.wait("tensor", "v", p.n("v"))
                    for i in range(8):
                        for d2 in range(2):
                            p.op("tensor", lambda e, i=i, d2=d2: e.matmul(
                                psA[:, d2 * 512:(d2 + 1) * 512], woT[:, i * 128:(i + 1) * 128],
                                wstg[:, i * D + d2 * 512: i * D + d2 * 512 + 512],
                                start=(i == 0), stop=(i == 7)),
                                sig="t" if (i == 7 and d2 == 1) else None)
                    wo_t = p.n("t")
                    wstg_gate[0] = wo_t
                    xb2 = xab[:, (lc % 2) * D:(lc % 2) * D + D]
                    lxs2 = "lx%d" % (lc % 2)
                    p.wait("sync", "v", xab_free[lc % 2])
                    l2 = p.dma("sync", xb2[:], src_hbm[b * 128:(b + 1) * 128, :], sig=lxs2)
                    p.wait("vector", "t", wo_t)
                    p.wait("vector", lxs2, l2)
                    xob2 = xo2[:, (lc % 2) * D:(lc % 2) * D + D]
                    p.wait("vector", "stX%d" % (lc % 2), xo_free2[lc % 2])
                    p.op("vector", lambda e, xb2=xb2, xob2=xob2: e.tensor_tensor(
                        xob2[:], psA[:], xb2[:], ALU.add), sig="v")
                    xab_free[lc % 2] = p.n("v")
                    p.wait("scalar", "v", p.n("v"))
                    if post_stats is not None:
                        mun, rsn = post_stats
                        ln_stats_tile(xob2, mun[:, b:b + 1], rsn[:, b:b + 1])
                    xo_free2[lc % 2] = p.dma("scalar", dst_hbm[b * 128:(b + 1) * 128, :],
                                             xob2[:], sig="stX%d" % (lc % 2))

                pending = None
                for lc, b in enumerate(blocks):
                    first = (b == 0)
                    av = emit_A(b, lc, first)
                    if pending is not None:
                        emit_S(*pending)
                    emit_R(b, lc, first, av)
                    if pending is not None:
                        emit_gnWO(*pending)
                    pending = (b, lc, first)
                emit_S(*pending)
                emit_gnWO(*pending)
            p.barrier()

    prelude()
    ln1_stats()

    only_ffn1 = (stage == "ffn1")
    ternarize(wgt, 8, INNER, wmg, None, Mgs)
    ternarize(wut, 8, INNER, wmu, None, Mus)
    ternarize(wdt, 32, D, wmd, None, Mds)
    p.barrier()
    ffn_phase(x_in, y_out if only_ffn1 else Xs, mu1, rs1, f2b1, None, None,
              post_stats=None if only_ffn1 else (mu2, rs2))

    if stage in ("attn", "full"):
        attention(Xs, Xs if stage == "full" else y_out, mu2, rs2,
                  post_stats=(mu3, rs3) if stage == "full" else None)

    if stage == "full":
        ffn_phase(Xs, y_out, mu3, rs3, f2b2, None, None, post_stats=None)

    for s_ in ("stK", "stX0", "stX1", "stM0", "stM1"):
        p.wait("sync", s_, p.n(s_))
        p.wait("vector", s_, p.n(s_))

    p.emit()
    return nc


def _host_layouts(WQ, WK, WV, WO, w_gate, w_up, w_down, gamma, beta):
    perm = np.concatenate([np.arange(0, DH, 2), np.arange(1, DH, 2)])
    wq = WQ[:, perm, :]
    wk = WK[:, perm, :]
    return dict(
        wgT=np.ascontiguousarray(w_gate.T),
        wuT=np.ascontiguousarray(w_up.T),
        wdT=np.ascontiguousarray(w_down.T),
        wqT=np.ascontiguousarray(wq.transpose(2, 0, 1).reshape(D, D)),
        wkT=np.ascontiguousarray(wk.transpose(2, 0, 1).reshape(D, D)),
        wvT=np.ascontiguousarray(WV.transpose(2, 0, 1).reshape(D, D)),
        wo=np.ascontiguousarray(WO),
        gam=np.ascontiguousarray(gamma.reshape(1, D)),
        bet=np.ascontiguousarray(beta.reshape(1, D)))


def kernel(L0_attn_x, L0_attn_WQ, L0_attn_WK, L0_attn_WV, L0_attn_WO,
           L0_attn_gamma, L0_attn_beta, L0_ffn_w_gate, L0_ffn_w_up,
           L0_ffn_w_down):
    shared = _host_layouts(L0_attn_WQ, L0_attn_WK, L0_attn_WV, L0_attn_WO,
                           L0_ffn_w_gate, L0_ffn_w_up, L0_ffn_w_down,
                           L0_attn_gamma, L0_attn_beta)
    nc = build_nc("full")
    in_maps = []
    for i in range(8):
        m = dict(shared)
        m["x"] = np.ascontiguousarray(np.asarray(L0_attn_x)[i])
        in_maps.append(m)
    res = run_bass_kernel_spmd(nc, in_maps, core_ids=list(range(8)))
    return np.stack([np.asarray(r["y"]) for r in res.results], axis=0)


# version 06
# speedup vs baseline: 1.3729x; 1.0201x over previous
import contextlib
import math
import numpy as np
import concourse.bass as bass
import concourse.mybir as mybir
from concourse.bass_utils import run_bass_kernel_spmd

F32 = mybir.dt.float32
BF16 = mybir.dt.bfloat16
I16 = mybir.dt.int16
ALU = mybir.AluOpType
ACT = mybir.ActivationFunctionType
AX = mybir.AxisListType

SCAN_QST_BATCH = True
SCAN_ADT_BATCH = True

S = 4096
D = 1024
NB = S // 128
INNER = 4096
H = 16
DH = 64
NEI = 8
BPE = NB // NEI
EPS = 1e-5
TWO_PI = 6.283185307179586
PI_CLAMP = 3.141592
DECAY = [1.0 - 2.0 ** (-5.0 - h) for h in range(H)]


class Prog:
    ENGS = ("sync", "vector", "scalar", "tensor", "gpsimd")

    DRAIN_ENGS = ("vector", "scalar", "gpsimd")

    def __init__(self, nc):
        self.nc = nc
        self.q = {e: [] for e in self.ENGS}
        self.sems = {}
        self._dirty = {e: False for e in self.ENGS}

    def sem(self, name):
        self.sems[name] = [self.nc.alloc_semaphore(name), 0]

    def n(self, name):
        return self.sems[name][1]

    def _pre(self, eng):
        if eng in self.DRAIN_ENGS and self._dirty[eng]:
            self.q[eng].append(lambda e: e.drain())
            self._dirty[eng] = False

    def op(self, eng, fn, sig=None, inc=1):
        self._pre(eng)
        self._dirty[eng] = eng in self.DRAIN_ENGS
        if sig is not None:
            rec = self.sems[sig]
            rec[1] += inc
            h = rec[0]

            def wrapped(e, fn=fn, h=h, inc=inc):
                fn(e).then_inc(h, inc)
            self.q[eng].append(wrapped)
            return rec[1]
        self.q[eng].append(fn)
        return None

    def dma(self, eng, out, in_, sig="ld"):
        rec = self.sems[sig]
        rec[1] += 16
        h = rec[0]
        self.q[eng].append(lambda e, o=out, i=in_, h=h: e.dma_start(out=o, in_=i)
                           .then_inc(h, 16))
        return rec[1]

    def wait(self, eng, sem, val):
        if val is None:
            return
        h = self.sems[sem][0]
        self.q[eng].append(lambda e, h=h, v=val: e.wait_ge(h, v))

    def inc(self, eng, sem, by=1):
        self._pre(eng)
        rec = self.sems[sem]
        rec[1] += by
        h = rec[0]
        self.q[eng].append(lambda e, h=h, b=by: e.sem_inc(h, b))
        return rec[1]

    DMA_SEMS = ("lw", "lx0", "lx1", "lk0", "lk1", "stK", "stX0", "stX1", "stM0",
                "stM1", "sd")

    def barrier(self):
        for s_ in self.DMA_SEMS:
            if s_ in self.sems:
                self.wait("sync", s_, self.sems[s_][1] or None)
        for e in self.ENGS:
            self._pre(e)
            self.inc(e, "bar", 1)
        tgt = self.n("bar")
        for e in self.ENGS:
            self.wait(e, "bar", tgt)

    def drain(self, eng="vector"):
        self.q[eng].append(lambda e: e.drain())
        self._dirty[eng] = False

    def emit(self):
        with self.nc.Block() as block:
            @block.sync
            def _(e):
                for fn in self.q["sync"]:
                    fn(e)

            @block.vector
            def _(e):
                for fn in self.q["vector"]:
                    fn(e)

            @block.scalar
            def _(e):
                for fn in self.q["scalar"]:
                    fn(e)

            @block.tensor
            def _(e):
                for fn in self.q["tensor"]:
                    fn(e)

            @block.gpsimd
            def _(e):
                for fn in self.q["gpsimd"]:
                    fn(e)


def _qst_ap(qstall, hh, lc):
    return qstall[64 * (hh % 2):64 * (hh % 2) + 64,
                  (hh // 2) * 128:(hh // 2) * 128 + 128]


def build_nc(stage="full"):
    nc = bass.Bass("TRN2", target_bir_lowering=False)

    x_in = nc.dram_tensor("x", [S, D], F32, kind="ExternalInput").ap()
    wgt = nc.dram_tensor("wgT", [D, INNER], F32, kind="ExternalInput").ap()
    wut = nc.dram_tensor("wuT", [D, INNER], F32, kind="ExternalInput").ap()
    wdt = nc.dram_tensor("wdT", [INNER, D], F32, kind="ExternalInput").ap()
    wqt = nc.dram_tensor("wqT", [D, D], F32, kind="ExternalInput").ap()
    wkt = nc.dram_tensor("wkT", [D, D], F32, kind="ExternalInput").ap()
    wvt = nc.dram_tensor("wvT", [D, D], F32, kind="ExternalInput").ap()
    wo_d = nc.dram_tensor("wo", [D, D], F32, kind="ExternalInput").ap()
    gam_d = nc.dram_tensor("gam", [1, D], F32, kind="ExternalInput").ap()
    bet_d = nc.dram_tensor("bet", [1, D], F32, kind="ExternalInput").ap()
    y_out = nc.dram_tensor("y", [S, D], F32, kind="ExternalOutput").ap()

    Xs = nc.dram_tensor("Xs", [S, D], F32, kind="Internal").ap()
    K2s = nc.dram_tensor("K2s", [NB, 128, INNER], BF16, kind="Internal").ap()
    Mgs = nc.dram_tensor("Mgs", [128, 8 * INNER], BF16, kind="Internal").ap()
    Mus = nc.dram_tensor("Mus", [128, 8 * INNER], BF16, kind="Internal").ap()
    Mds = nc.dram_tensor("Mds", [128, 32 * D], BF16, kind="Internal").ap()

    p = Prog(nc)
    for s_ in ("lw", "lx0", "lx1", "lk0", "lk1", "stK", "stX0", "stX1", "stM0",
               "stM1", "v", "sc", "t", "g", "bar", "sd"):
        p.sem(s_)

    al = nc.alloc_sbuf_tensor
    alp = nc.alloc_psum_tensor
    _uid = [0]

    def _uniq(n):
        _uid[0] += 1
        return f"{n}_{_uid[0]}"


    psA = alp("psA", [128, 1024], F32).ap()
    psB = alp("psB", [128, 1024], F32).ap()
    psC = alp("psC", [128, 1024], F32).ap()
    psD = alp("psD", [128, 1024], F32).ap()

    cst = al("cst", [128, 8], F32).ap()
    C_HALF = cst[:, 0:1]
    C_NEGH = cst[:, 1:2]
    C_EPS = cst[:, 2:3]
    C_I2PI = cst[:, 3:4]
    C_PIH = cst[:, 4:5]
    C_LNA = cst[:, 5:6]
    C_128 = cst[:, 6:7]
    ones128 = al("ones128", [128, 1], F32).ap()
    ones1r = al("ones1r", [1, 128], F32).ap()
    idf = al("idf", [128, 128], F32).ap()
    idb = al("idb", [128, 128], BF16).ap()
    wmg = al("wmg", [128, 1], F32).ap()
    wmu = al("wmu", [128, 1], F32).ap()
    wmd = al("wmd", [128, 1], F32).ap()
    mu1 = al("mu1", [128, NB], F32).ap()
    rs1 = al("rs1", [128, NB], F32).ap()
    mu2 = al("mu2", [128, NB], F32).ap()
    rs2 = al("rs2", [128, NB], F32).ap()
    mu3 = al("mu3", [128, NB], F32).ap()
    rs3 = al("rs3", [128, NB], F32).ap()
    f2b1 = al("f2b1", [128, NB], F32).ap()
    f2b2 = al("f2b2", [128, NB], F32).ap()
    stt = al("stt", [128, 8], F32).ap()
    dump = al("dump", [128, D], F32).ap()

    def ln_stats_tile(xtile, mu_c, rs_c):
        p.op("vector", lambda e: e.tensor_reduce(stt[:, 0:1], xtile[:], AX.X, ALU.add))
        p.drain()
        p.op("vector", lambda e: e.tensor_single_scalar(mu_c, stt[:, 0:1], 1.0 / D, ALU.mult))
        p.drain()
        p.op("vector", lambda e: e.tensor_tensor(stt[:, 2:3], mu_c, mu_c, ALU.mult), sig="v")
        vt = p.n("v")
        p.op("scalar", lambda e: e.activation(dump[:], xtile[:], ACT.Square,
                                              accum_out=stt[:, 1:2]))
        p.wait("scalar", "v", vt)
        p.op("scalar", lambda e: e.copy(stt[:, 3:4], stt[:, 1:2]), sig="sc")
        p.wait("vector", "sc", p.n("sc"))
        p.drain()
        p.op("vector", lambda e: e.scalar_tensor_tensor(stt[:, 4:5], stt[:, 3:4], 1.0 / D,
                                                        stt[:, 2:3], ALU.mult, ALU.subtract),
             sig="v")
        p.wait("scalar", "v", p.n("v"))
        p.op("scalar", lambda e: e.activation(stt[:, 5:6], stt[:, 4:5], ACT.Ln, bias=C_EPS))
        p.op("scalar", lambda e: e.activation(rs_c, stt[:, 5:6], ACT.Exp, scale=C_NEGH),
             sig="sc")
        p.wait("vector", "sc", p.n("sc"))

    def round_to_ints(src, scale_ap, ti, cf, ei, tg, out, width):
        p.wait("scalar", "v", p.n("v"))
        p.op("scalar", lambda e: e.activation(ti[:, 0:width], src, ACT.Identity,
                                              scale=scale_ap, bias=C_HALF))
        p.op("scalar", lambda e: e.copy(cf[:, 0:width], ti[:, 0:width]), sig="sc")
        p.wait("vector", "sc", p.n("sc"))
        p.op("vector", lambda e: e.scalar_tensor_tensor(ei[:, 0:width], src, scale_ap,
                                                        cf[:, 0:width], ALU.mult, ALU.subtract))
        p.op("vector", lambda e: e.tensor_single_scalar(tg[:, 0:width], ei[:, 0:width],
                                                        -0.5, ALU.is_lt))
        p.op("vector", lambda e: e.tensor_tensor(out, cf[:, 0:width], tg[:, 0:width],
                                                 ALU.subtract), sig="v")
        return p.n("v")

    def prelude():
        for i, v in enumerate([0.5, -0.5, EPS, 1.0 / TWO_PI, np.pi / 2,
                               -math.log(10.0) / 8.0, 128.0]):
            p.op("vector", lambda e, i=i, v=v: e.memset(cst[:, i:i + 1], v))
        p.op("vector", lambda e: e.memset(ones128[:], 1.0))
        p.op("vector", lambda e: e.memset(ones1r[:], 1.0))
        p.op("gpsimd", lambda e: e.iota(idf[:], [[1, 128]], base=0, channel_multiplier=-1,
                                        allow_small_or_imprecise_dtypes=True), sig="g")
        p.wait("vector", "g", p.n("g"))
        p.op("vector", lambda e: e.tensor_single_scalar(idf[:], idf[:], 0.0, ALU.is_equal))
        p.op("vector", lambda e: e.tensor_copy(idb[:], idf[:]), sig="v")
        p.wait("tensor", "v", p.n("v"))
        p.wait("scalar", "v", p.n("v"))

    def ln1_stats():
        with nc.sbuf_tensor(_uniq("p0a_x"), [128, 2 * D], F32) as xg:
            xt = xg.ap()
            done = [None, None]
            for b in range(NB):
                sl = xt[:, (b % 2) * D:(b % 2) * D + D]
                lxs = "lx%d" % (b % 2)
                p.wait("sync", "v", done[b % 2])
                ldt = p.dma("sync", sl, x_in[b * 128:(b + 1) * 128, :], sig=lxs)
                p.wait("vector", lxs, ldt)
                p.wait("scalar", lxs, ldt)
                ln_stats_tile(sl, mu1[:, b:b + 1], rs1[:, b:b + 1])
                p.inc("vector", "v", 1)
                done[b % 2] = p.n("v")
        p.barrier()

    def ternarize(src, n_chunks, cols, wm_tile, dst_sbuf, dst_hbm):
        with contextlib.ExitStack() as st_:
            wstg = st_.enter_context(nc.sbuf_tensor(_uniq("p0_wstg"), [128, 2 * cols], F32)).ap()
            mtmp = st_.enter_context(nc.sbuf_tensor(_uniq("p0_mtmp"), [128, 2 * cols], BF16)).ap()
            acc = st_.enter_context(nc.sbuf_tensor(_uniq("p0_acc"), [128, 1], F32)).ap()
            w11 = st_.enter_context(nc.sbuf_tensor(_uniq("p0_w11"), [1, 1], F32)).ap()
            ti = st_.enter_context(nc.sbuf_tensor(_uniq("p0_ti"), [128, cols], I16)).ap()
            cf = st_.enter_context(nc.sbuf_tensor(_uniq("p0_cf"), [128, cols], BF16)).ap()
            tg = st_.enter_context(nc.sbuf_tensor(_uniq("p0_tg"), [128, cols], BF16)).ap()
            r_ = st_.enter_context(nc.sbuf_tensor(_uniq("p0_r"), [128, cols], F32)).ap()
            isc = st_.enter_context(nc.sbuf_tensor(_uniq("p0_isc"), [128, 1], F32)).ap()

            p.op("vector", lambda e: e.memset(acc[:], 0.0))
            done = [None, None]
            for c_ in range(n_chunks):
                sl = wstg[:, (c_ % 2) * cols:(c_ % 2) * cols + cols]
                lxs = "lx%d" % (c_ % 2)
                p.wait("sync", "v", done[c_ % 2])
                ldt = p.dma("sync", sl, src[c_ * 128:(c_ + 1) * 128, :], sig=lxs)
                p.wait("vector", lxs, ldt)
                p.op("vector", lambda e, sl=sl, c_=c_: e.tensor_reduce(
                    stt[:, c_ % 2:c_ % 2 + 1], sl, AX.X, ALU.add,
                    apply_absolute_value=True))
                p.op("vector", lambda e, c_=c_: e.tensor_tensor(
                    acc[:], acc[:], stt[:, c_ % 2:c_ % 2 + 1], ALU.add), sig="v")
                done[c_ % 2] = p.n("v")
            p.drain()
            p.inc("vector", "v", 1)
            p.wait("tensor", "v", p.n("v"))
            p.op("tensor", lambda e: e.matmul(psA[0:1, 0:1], ones128[:], acc[:],
                                              start=True, stop=True), sig="t")
            p.wait("vector", "t", p.n("t"))
            ntot = float(n_chunks * 128 * cols)
            p.op("vector", lambda e: e.tensor_copy(w11[:], psA[0:1, 0:1]))
            p.drain()
            p.op("vector", lambda e: e.tensor_scalar(w11[:], w11[:], 1.0 / ntot, 1e-5,
                                                     ALU.mult, ALU.max), sig="v")
            p.wait("tensor", "v", p.n("v"))
            p.op("tensor", lambda e: e.matmul(psA[0:128, 1:2], ones1r[:], w11[:],
                                              start=True, stop=True), sig="t")
            p.wait("vector", "t", p.n("t"))
            p.op("vector", lambda e: e.tensor_copy(wm_tile[:], psA[0:128, 1:2]))
            p.drain()
            p.op("vector", lambda e: e.reciprocal(isc[:], wm_tile[:]))
            p.drain()
            p.inc("vector", "v", 1)
            vsc = p.n("v")
            p.wait("scalar", "v", vsc)

            mfree = [None, None]
            for c_ in range(n_chunks):
                sl = wstg[:, (c_ % 2) * cols:(c_ % 2) * cols + cols]
                msl = mtmp[:, (c_ % 2) * cols:(c_ % 2) * cols + cols]
                lxs = "lx%d" % (c_ % 2)
                p.wait("sync", "v", done[c_ % 2])
                ldt = p.dma("sync", sl, src[c_ * 128:(c_ + 1) * 128, :], sig=lxs)
                p.wait("vector", lxs, ldt)
                p.wait("scalar", lxs, ldt)
                p.wait("vector", "stM%d" % (c_ % 2), mfree[c_ % 2])
                vq = round_to_ints(sl, isc[:, 0:1], ti, cf, r_, tg, msl, cols)
                p.wait("vector", "v", vq)
                p.op("vector", lambda e, msl=msl: e.tensor_scalar(
                    msl, msl, 1.0, -1.0, ALU.min, ALU.max), sig="v")
                done[c_ % 2] = p.n("v")
                p.wait("scalar", "v", p.n("v"))
                mfree[c_ % 2] = p.dma("scalar", dst_hbm[:, c_ * cols:(c_ + 1) * cols],
                                      msl, sig="stM%d" % (c_ % 2))
            p.barrier()

    def ffn_phase(src_hbm, dst_hbm, mu_t, rs_t, f2_t, mg, mu_, post_stats,
                  adopt_guards=None):
        HW2 = INNER // 2
        with contextlib.ExitStack() as st_:
            sb = lambda n, sh, dt: st_.enter_context(nc.sbuf_tensor(_uniq(n), sh, dt)).ap()
            for g_ in (adopt_guards or ()):
                st_.push(g_)
            if mg is None:
                mg = sb("fa_mg", [128, 8 * INNER], BF16)
                mu_ = sb("fa_mu", [128, 8 * INNER], BF16)
                for c_ in range(8):
                    p.dma("sync", mg[:, c_ * INNER:(c_ + 1) * INNER],
                          Mgs[:, c_ * INNER:(c_ + 1) * INNER], sig="lw")
                for c_ in range(8):
                    p.dma("sync", mu_[:, c_ * INNER:(c_ + 1) * INNER],
                          Mus[:, c_ * INNER:(c_ + 1) * INNER], sig="lw")
                p.wait("tensor", "lw", p.n("lw"))
            h_t = sb("fa_h", [128, INNER], F32)
            gsm = sb("fa_g", [128, 512], F32)
            xin = sb("fa_xin", [128, 2 * D], F32)
            xn = sb("fa_xn", [128, D], F32)
            ti1 = sb("fa_ti1", [128, D], I16)
            cf1 = sb("fa_cf1", [128, D], BF16)
            tg1 = ti1.bitcast(BF16)
            e1 = sb("fa_e1", [128, D], F32)
            k1 = sb("fa_k1", [128, 2 * D], BF16)
            k1T = sb("fa_k1T", [128, D], BF16)
            HW4 = INNER // 4
            ti2 = ti1
            cf2 = cf1
            tg2 = tg1
            e2 = e1
            k2h = sb("fa_k2h", [128, INNER], BF16)
            k2T = sb("fa_k2T", [128, INNER], BF16)
            sc1c = sb("fa_sc1", [128, 2], F32)
            sc2c = sb("fa_sc2", [128, 2], F32)
            f1g = sb("fa_f1g", [128, 2], F32)
            f1u = sb("fa_f1u", [128, 2], F32)

            rotp = [(psA[:, 0:512], psA[:, 512:1024]),
                    (psC[:, 0:512], psC[:, 512:1024]),
                    (psD[:, 0:512], psD[:, 512:1024])]
            psBb = psB.bitcast(BF16)
            rot_free = [None, None, None]
            k2_store = [None]
            k2ev_done = [None]
            k2h_v = {}
            xin_free = [None, None]

            k1_vk = {}

            def front_dve(b):
                xb = xin[:, (b % 2) * D:(b % 2) * D + D]
                k1s = k1[:, (b % 2) * D:(b % 2) * D + D]
                lxs = "lx%d" % (b % 2)
                p.wait("sync", "v", xin_free[b % 2])
                ldt = p.dma("sync", xb[:], src_hbm[b * 128:(b + 1) * 128, :], sig=lxs)
                p.wait("vector", lxs, ldt)
                p.op("vector", lambda e, b=b, xb=xb: e.tensor_scalar(
                    xn[:], xb[:], mu_t[:, b:b + 1], rs_t[:, b:b + 1],
                    ALU.subtract, ALU.mult), sig="v")
                xin_free[b % 2] = p.n("v")
                sc1 = sc1c[:, b % 2:b % 2 + 1]
                fg = f1g[:, b % 2:b % 2 + 1]
                fu = f1u[:, b % 2:b % 2 + 1]
                p.op("vector", lambda e: e.tensor_reduce(stt[:, 0:1], xn[:], AX.X, ALU.max,
                                                         apply_absolute_value=True))
                p.op("vector", lambda e: e.tensor_single_scalar(stt[:, 0:1], stt[:, 0:1],
                                                                1e-5, ALU.max))
                p.op("vector", lambda e, fg=fg: e.scalar_tensor_tensor(
                    fg, stt[:, 0:1], 1.0 / 127.0, wmg[:], ALU.mult, ALU.mult))
                p.op("vector", lambda e, fu=fu: e.scalar_tensor_tensor(
                    fu, stt[:, 0:1], 1.0 / 127.0, wmu[:], ALU.mult, ALU.mult))
                p.op("vector", lambda e: e.reciprocal(stt[:, 1:2], stt[:, 0:1]))
                p.op("vector", lambda e, sc1=sc1: e.tensor_single_scalar(
                    sc1, stt[:, 1:2], 127.0, ALU.mult))
                p.inc("vector", "v", 1)
                p.wait("scalar", "v", p.n("v"))
                k1_vk[b] = round_to_ints(xn[:], sc1, ti1, cf1, e1, tg1, k1s, D)

            def front_pe(b):
                k1s = k1[:, (b % 2) * D:(b % 2) * D + D]
                p.wait("tensor", "v", k1_vk[b])
                p.wait("tensor", "sc", p.n("sc"))
                for c_ in range(8):
                    fn = lambda e, c_=c_, k1s=k1s: e.transpose(
                        psBb[:, c_ * 128:(c_ + 1) * 128],
                        k1s[:, c_ * 128:(c_ + 1) * 128], idb[:])
                    p.op("tensor", fn, sig="t" if c_ == 7 else None)
                p.wait("scalar", "t", p.n("t"))
                p.op("scalar", lambda e: e.copy(k1T[:], psBb[:, 0:1024]), sig="sc")
                p.wait("tensor", "sc", p.n("sc"))

            def emit_gu(b, ots):
                fg = f1g[:, b % 2:b % 2 + 1]
                fu = f1u[:, b % 2:b % 2 + 1]
                for ot in ots:
                    gbank, ubank = rotp[ot % 3]
                    p.wait("tensor", "v", rot_free[ot % 3])
                    for i in range(8):
                        p.op("tensor", lambda e, gbank=gbank, i=i, ot=ot: e.matmul(
                            gbank, k1T[:, i * 128:(i + 1) * 128],
                            mg[:, i * INNER + ot * 512: i * INNER + ot * 512 + 512],
                            start=(i == 0), stop=(i == 7)))
                        p.op("tensor", lambda e, ubank=ubank, i=i, ot=ot: e.matmul(
                            ubank, k1T[:, i * 128:(i + 1) * 128],
                            mu_[:, i * INNER + ot * 512: i * INNER + ot * 512 + 512],
                            start=(i == 0), stop=(i == 7)),
                            sig="t" if i == 7 else None)
                    tt = p.n("t")
                    p.wait("scalar", "t", tt)
                    p.op("scalar", lambda e, gbank=gbank, fg=fg: e.activation(
                        gsm[:], gbank, ACT.Silu, scale=fg), sig="sc")
                    p.wait("vector", "sc", p.n("sc"))
                    p.op("vector", lambda e, ubank=ubank, fu=fu, ot=ot:
                         e.scalar_tensor_tensor(h_t[:, ot * 512:(ot + 1) * 512], ubank,
                                                fu, gsm[:], ALU.mult, ALU.mult), sig="v")
                    rot_free[ot % 3] = p.n("v")
                    p.wait("scalar", "v", p.n("v"))

            def emit_hq1(b):
                sc2 = sc2c[:, b % 2:b % 2 + 1]
                p.op("vector", lambda e: e.tensor_reduce(stt[:, 2:3], h_t[:], AX.X, ALU.max,
                                                         apply_absolute_value=True))
                p.op("vector", lambda e: e.tensor_single_scalar(stt[:, 2:3], stt[:, 2:3],
                                                                1e-5, ALU.max))
                p.op("vector", lambda e, b=b: e.scalar_tensor_tensor(
                    f2_t[:, b:b + 1], stt[:, 2:3], 0.5 / 127.0, wmd[:],
                    ALU.mult, ALU.mult))
                p.op("vector", lambda e: e.reciprocal(stt[:, 3:4], stt[:, 2:3]))
                p.op("vector", lambda e, sc2=sc2: e.tensor_single_scalar(
                    sc2, stt[:, 3:4], 127.0, ALU.mult))
                p.inc("vector", "v", 1)
                p.wait("scalar", "v", p.n("v"))
                for hh in range(4):
                    hsl = h_t[:, hh * HW4:(hh + 1) * HW4]
                    k2h_v[(b, hh)] = round_to_ints(hsl, sc2, ti2, cf2, e2, tg2,
                                                   k2h[:, hh * HW4:(hh + 1) * HW4], HW4)

            def emit_k2part(b):
                if k2ev_done[0] is not None:
                    p.wait("tensor", k2ev_done[0][0], k2ev_done[0][1])
                qevac = [None, None]
                for hh in range(4):
                    off = 1024 * (hh % 2)
                    p.wait("tensor", "v", k2h_v[(b, hh)])
                    if qevac[hh % 2] is not None:
                        p.wait("tensor", "sc", qevac[hh % 2])
                    for c_ in range(8):
                        fn = lambda e, off=off, hh=hh, c_=c_: e.transpose(
                            psBb[:, off + c_ * 128: off + (c_ + 1) * 128],
                            k2h[:, hh * HW4 + c_ * 128: hh * HW4 + (c_ + 1) * 128], idb[:])
                        p.op("tensor", fn, sig="t")
                    tq = p.n("t")
                    p.wait("scalar", "t", tq)
                    if hh == 0:
                        p.wait("scalar", "stK", k2_store[0])
                    p.op("scalar", lambda e, hh=hh, off=off: e.copy(
                        k2T[:, hh * HW4:(hh + 1) * HW4], psBb[:, off:off + 1024]),
                        sig="sc")
                    qevac[hh % 2] = p.n("sc")
                k2ev_done[0] = ("sc", p.n("sc"))
                p.wait("scalar", "sc", p.n("sc"))
                k2_store[0] = p.dma("scalar", K2s[b], k2T[:], sig="stK")

            front_dve(0)
            front_pe(0)
            for b in range(NB):
                emit_gu(b, range(0, 4))
                if b + 1 < NB:
                    front_dve(b + 1)
                emit_gu(b, range(4, 8))
                if b >= 1:
                    emit_k2part(b - 1)
                if b + 1 < NB:
                    front_pe(b + 1)
                emit_hq1(b)
            emit_k2part(NB - 1)
            p.barrier()
        with contextlib.ExitStack() as st_:
            sb = lambda n, sh, dt: st_.enter_context(nc.sbuf_tensor(_uniq(n), sh, dt)).ap()
            md = sb("fb_md", [128, 32 * D], BF16)
            k2i = sb("fb_k2i", [128, 2 * INNER], BF16)
            xin2 = sb("fb_xin2", [128, 2 * D], F32)
            xo = sb("fb_xo", [128, 2 * D], F32)
            for c_ in range(8):
                p.dma("sync", md[:, c_ * 4 * D:(c_ + 1) * 4 * D],
                      Mds[:, c_ * 4 * D:(c_ + 1) * 4 * D], sig="lw")
            p.wait("tensor", "lw", p.n("lw"))
            mm_done = [None, None]
            xo_free = [None, None]
            k2_free = [None, None]
            xin_free = [None, None]
            for b in range(NB):
                kb = k2i[:, (b % 2) * INNER:(b % 2) * INNER + INNER]
                xb = xin2[:, (b % 2) * D:(b % 2) * D + D]
                xob = xo[:, (b % 2) * D:(b % 2) * D + D]
                lks = "lk%d" % (b % 2)
                lxs = "lx%d" % (b % 2)
                p.wait("sync", "t", k2_free[b % 2])
                l1 = p.dma("sync", kb[:], K2s[b], sig=lks)
                p.wait("sync", "v", xin_free[b % 2])
                l2 = p.dma("sync", xb[:], src_hbm[b * 128:(b + 1) * 128, :], sig=lxs)
                p.wait("tensor", lks, l1)
                p.wait("tensor", "v", mm_done[0])
                for oc in range(32):
                    for d2 in range(2):
                        last = (oc == 31 and d2 == 1)
                        p.op("tensor", lambda e, oc=oc, d2=d2, kb=kb: e.matmul(
                            psA[:, d2 * 512:(d2 + 1) * 512],
                            kb[:, oc * 128:(oc + 1) * 128],
                            md[:, oc * D + d2 * 512: oc * D + d2 * 512 + 512],
                            start=(oc == 0), stop=(oc == 31)),
                            sig="t" if last else None)
                k2_free[b % 2] = p.n("t")
                p.wait("vector", "t", p.n("t"))
                p.wait("vector", lxs, l2)
                p.wait("vector", "stX%d" % (b % 2), xo_free[b % 2])
                p.op("vector", lambda e, b=b, xb=xb, xob=xob: e.scalar_tensor_tensor(
                    xob[:], psA[:], f2_t[:, b:b + 1], xb[:], ALU.mult, ALU.add), sig="v")
                mm_done[0] = p.n("v")
                xin_free[b % 2] = p.n("v")
                p.wait("scalar", "v", p.n("v"))
                if post_stats is not None:
                    mun, rsn = post_stats
                    ln_stats_tile(xob, mun[:, b:b + 1], rsn[:, b:b + 1])
                xo_free[b % 2] = p.dma("scalar", dst_hbm[b * 128:(b + 1) * 128, :], xob[:],
                                       sig="stX%d" % (b % 2))
            p.barrier()

    def attention(src_hbm, dst_hbm, mu_t, rs_t, post_stats):
        with contextlib.ExitStack() as st_:
            sb = lambda n, sh, dt: st_.enter_context(nc.sbuf_tensor(_uniq(n), sh, dt)).ap()
            sintab = sb("at_sin", [128, NB * 32], F32)
            costab = sb("at_cos", [128, NB * 32], F32)
            dmask = sb("at_dmask", [128, H * 128], F32)
            mstab = sb("at_ms", [128, 8 * 128], F32)
            mbc = sb("at_mbc", [128, H], F32)
            scc = sb("at_scc", [128, H], F32)
            lns = sb("at_lns", [128, H], F32)
            lns2 = sb("at_lns2", [128, 8], F32)
            svals = sb("at_sv", [128, H], F32)
            gbc = sb("at_gbc", [128, D], F32)
            bbc = sb("at_bbc", [128, D], F32)
            grow = sb("at_grow", [1, D], F32)
            brow = sb("at_brow", [1, D], F32)
            ssb = sb("at_ssb", [128, H * DH], F32)
            wstg = sb("at_wstg", [128, 8 * D], F32)
            xnT = sb("at_xnT", [128, BPE * D], F32)
            vq_t = sb("at_vq", [128, BPE * D], F32)
            qT = sb("at_qT", [128, BPE * D], F32)
            kT = sb("at_kT", [128, BPE * D], F32)
            kbq = sb("at_kbq", [128, BPE * D], F32)
            xab = sb("at_xab", [128, 2 * D], F32)
            xn2 = sb("at_xn2", [128, D], F32)
            rot = sb("at_rot", [128, D], F32)
            adt = sb("at_adt", [128, 128], F32)
            qst = sb("at_qst", [128, 128], F32)
            rr1 = sb("at_rr1", [128, D], F32)
            rr2 = sb("at_rr2", [128, D], F32)
            rrT = sb("at_rrT", [128, D], F32)
            gnq = sb("at_gnq", [128, H], F32)
            gnm = sb("at_gnm", [128, H], F32)
            gnr = sb("at_gnr", [128, H], F32)
            xo2 = sb("at_xo2", [128, 2 * D], F32)
            ta = sb("at_ta", [128, 1024], F32)
            tb_ = sb("at_tb", [128, 1024], F32)
            ti3 = sb("at_ti3", [128, 1024], I16)

            g1 = p.dma("sync", grow[:], gam_d[:], sig="lw")
            g2 = p.dma("sync", brow[:], bet_d[:], sig="lw")
            p.wait("tensor", "lw", g2)
            for half in range(2):
                p.op("tensor", lambda e, half=half: e.matmul(
                    psA[:, half * 512:(half + 1) * 512], ones1r[:],
                    grow[0:1, half * 512:(half + 1) * 512], start=True, stop=True))
                p.op("tensor", lambda e, half=half: e.matmul(
                    psB[:, half * 512:(half + 1) * 512], ones1r[:],
                    brow[0:1, half * 512:(half + 1) * 512], start=True, stop=True),
                    sig="t" if half == 1 else None)
            p.wait("vector", "t", p.n("t"))
            p.op("vector", lambda e: e.tensor_copy(gbc[:], psA[:]))
            p.op("vector", lambda e: e.tensor_copy(bbc[:], psB[:]), sig="v")
            p.wait("tensor", "v", p.n("v"))

            for hh in range(H):
                p.op("vector", lambda e, hh=hh: e.memset(svals[:, hh:hh + 1], DECAY[hh]))
            p.inc("vector", "v", 1)
            p.wait("scalar", "v", p.n("v"))
            p.op("scalar", lambda e: e.activation(lns[:], svals[:], ACT.Ln), sig="sc")
            p.wait("vector", "sc", p.n("sc"))
            for pc in range(8):
                p.op("vector", lambda e, pc=pc: e.tensor_copy(lns2[0:64, pc:pc + 1],
                                                              lns[0:64, 2 * pc:2 * pc + 1]))
                p.op("vector", lambda e, pc=pc: e.tensor_copy(
                    lns2[64:128, pc:pc + 1], lns[64:128, 2 * pc + 1:2 * pc + 2]))
            p.inc("vector", "v", 1)
            p.wait("scalar", "v", p.n("v"))
            p.wait("gpsimd", "v", p.n("v"))
            p.op("gpsimd", lambda e: e.iota(ta[:, 0:128], [[1, 128]], base=0,
                                            channel_multiplier=-1,
                                            allow_small_or_imprecise_dtypes=True))
            p.op("gpsimd", lambda e: e.iota(ta[:, 128:256], [[1, 128]], base=1,
                                            channel_multiplier=0,
                                            allow_small_or_imprecise_dtypes=True))
            p.op("gpsimd", lambda e: e.iota(ta[:, 256:257], [[1, 1]], base=127,
                                            channel_multiplier=-1,
                                            allow_small_or_imprecise_dtypes=True))
            p.op("gpsimd", lambda e: e.iota(ta[:, 512:544], [[128, 32]], base=0,
                                            channel_multiplier=1,
                                            allow_small_or_imprecise_dtypes=True),
                 sig="g")
            gt_ = p.n("g")
            p.wait("vector", "g", gt_)
            p.wait("scalar", "g", gt_)
            p.op("vector", lambda e: e.tensor_single_scalar(tb_[:, 0:128], ta[:, 0:128],
                                                            0.0, ALU.is_ge), sig="v")
            p.wait("scalar", "v", p.n("v"))
            for hh in range(H):
                p.op("scalar", lambda e, hh=hh: e.activation(
                    dmask[:, hh * 128:(hh + 1) * 128], ta[:, 0:128], ACT.Exp,
                    scale=lns[:, hh:hh + 1]))
                p.op("scalar", lambda e, hh=hh: e.activation(
                    mbc[:, hh:hh + 1], ta[:, 256:257], ACT.Exp, scale=lns[:, hh:hh + 1]))
                p.op("scalar", lambda e, hh=hh: e.activation(
                    scc[:, hh:hh + 1], C_128, ACT.Exp, scale=lns[:, hh:hh + 1]))
            for pc in range(8):
                p.op("scalar", lambda e, pc=pc: e.activation(
                    mstab[:, pc * 128:(pc + 1) * 128], ta[:, 128:256], ACT.Exp,
                    scale=lns2[:, pc:pc + 1]), sig="sc" if pc == 7 else None)
            p.wait("vector", "sc", p.n("sc"))
            for hh in range(H):
                p.op("vector", lambda e, hh=hh: e.tensor_tensor(
                    dmask[:, hh * 128:(hh + 1) * 128], dmask[:, hh * 128:(hh + 1) * 128],
                    tb_[:, 0:128], ALU.mult))
            p.op("vector", lambda e: e.memset(ssb[:], 0.0))
            p.inc("vector", "v", 1)
            p.wait("gpsimd", "v", p.n("v"))
            p.op("gpsimd", lambda e: e.iota(tb_[:, 224:256], [[128, 32]], base=0,
                                            channel_multiplier=1,
                                            allow_small_or_imprecise_dtypes=True))
            p.op("gpsimd", lambda e: e.iota(tb_[:, 160:192], [[1, 32]], base=0,
                                            channel_multiplier=0,
                                            allow_small_or_imprecise_dtypes=True), sig="g")
            p.wait("scalar", "g", p.n("g"))
            p.op("scalar", lambda e: e.activation(tb_[:, 192:224], tb_[:, 160:192], ACT.Exp,
                                                  scale=C_LNA), sig="sc")
            p.wait("vector", "sc", p.n("sc"))
            p.op("vector", lambda e: e.tensor_tensor(
                ta[:].rearrange("p (b u) -> p b u", u=32),
                tb_[:, 192:224].rearrange("p (a u) -> p a u", a=1).broadcast_to([128, 32, 32]),
                tb_[:, 224:256].rearrange("p (b a) -> p b a", a=1).broadcast_to([128, 32, 32]),
                ALU.mult), sig="v")
            p.wait("scalar", "v", p.n("v"))

            def sincos(dst, phase_shift):
                p.op("vector", lambda e: e.tensor_single_scalar(tb_[:], ta[:],
                                                                phase_shift, ALU.add))
                p.op("vector", lambda e: e.tensor_single_scalar(rot[:], tb_[:],
                                                                1.0 / TWO_PI, ALU.mult),
                     sig="v")
                p.wait("scalar", "v", p.n("v"))
                p.op("scalar", lambda e: e.activation(ti3[:], rot[:], ACT.Identity,
                                                      bias=C_HALF))
                p.op("scalar", lambda e: e.copy(xn2[:], ti3[:]), sig="sc")
                p.wait("vector", "sc", p.n("sc"))
                p.op("vector", lambda e: e.tensor_tensor(rot[:], rot[:], xn2[:],
                                                         ALU.subtract))
                p.op("vector", lambda e: e.tensor_single_scalar(rot[:], rot[:], -0.5,
                                                                ALU.is_lt))
                p.op("vector", lambda e: e.tensor_tensor(xn2[:], xn2[:], rot[:],
                                                         ALU.subtract))
                p.op("vector", lambda e: e.scalar_tensor_tensor(tb_[:], xn2[:], -TWO_PI,
                                                                tb_[:], ALU.mult, ALU.add))
                p.op("vector", lambda e: e.tensor_scalar(tb_[:], tb_[:], PI_CLAMP,
                                                         -PI_CLAMP, ALU.min, ALU.max),
                     sig="v")
                p.wait("scalar", "v", p.n("v"))
                p.op("scalar", lambda e, dst=dst: e.activation(dst, tb_[:], ACT.Sin),
                     sig="sc")
                p.wait("vector", "sc", p.n("sc"))

            sincos(sintab, 0.0)
            sincos(costab, np.pi / 2)
            p.inc("scalar", "sc", 1)
            p.wait("tensor", "sc", p.n("sc"))
            p.wait("sync", "sc", p.n("sc"))

            s_dma = [None]
            s_copy = [None]
            wo_evac = [None, None]
            xo_free2 = [None, None]
            xab_free = [None, None]

            def load_w(src, gate_t):
                p.wait("sync", "t", gate_t)
                for i in range(8):
                    tgt = p.dma("sync", wstg[:, i * D:(i + 1) * D],
                                src[i * 128:(i + 1) * 128, :], sig="lw")
                return tgt

            wstg_gate = [None]
            t_heads = [None]
            gn_psC = [None]
            for ei in range(NEI):
                blocks = [ei * BPE + k for k in range(BPE)]
                wl = load_w(wvt, wstg_gate[0])
                p.wait("tensor", "lw", wl)
                psPP = [psB, psC]
                pp_free = [("v", gn_psC[0]) if gn_psC[0] else None,
                           ("v", gn_psC[0]) if gn_psC[0] else None]
                trT = [None]
                def a1_xnv(lb, b):
                    xb = xab[:, (lb % 2) * D:(lb % 2) * D + D]
                    lxs = "lx%d" % (lb % 2)
                    p.wait("sync", "v", xab_free[lb % 2])
                    ldt = p.dma("sync", xb[:], src_hbm[b * 128:(b + 1) * 128, :], sig=lxs)
                    p.wait("vector", lxs, ldt)
                    p.wait("vector", "t", trT[0])
                    p.op("vector", lambda e, b=b, xb=xb: e.tensor_scalar(
                        xn2[:], xb[:], mu_t[:, b:b + 1], rs_t[:, b:b + 1],
                        ALU.subtract, ALU.mult), sig="v")
                    xab_free[lb % 2] = p.n("v")
                    p.wait("tensor", "v", p.n("v"))
                    for c_ in range(8):
                        p.op("tensor", lambda e, c_=c_: e.transpose(
                            psA[:, c_ * 128:(c_ + 1) * 128],
                            xn2[:, c_ * 128:(c_ + 1) * 128], idf[:]),
                            sig="t" if c_ == 7 else None)
                    trT[0] = p.n("t")
                    p.wait("scalar", "t", p.n("t"))
                    p.op("scalar", lambda e, lb=lb: e.copy(
                        xnT[:, lb * D:(lb + 1) * D], psA[:]), sig="sc")
                    p.wait("tensor", "sc", p.n("sc"))

                def a1_proj(lb):
                    ps = psPP[lb % 2]
                    if pp_free[lb % 2] is not None:
                        p.wait("tensor", pp_free[lb % 2][0], pp_free[lb % 2][1])
                    for i in range(8):
                        for v2 in range(2):
                            p.op("tensor", lambda e, ps=ps, i=i, v2=v2, lb=lb: e.matmul(
                                ps[:, v2 * 512:(v2 + 1) * 512],
                                xnT[:, lb * D + i * 128: lb * D + (i + 1) * 128],
                                wstg[:, i * D + v2 * 512: i * D + v2 * 512 + 512],
                                start=(i == 0), stop=(i == 7)),
                                sig="t" if (i == 7 and v2 == 1) else None)
                    wstg_gate[0] = p.n("t")
                    return p.n("t")

                def a1_vevac(lb, tproj):
                    ps = psPP[lb % 2]
                    p.wait("scalar", "t", tproj)
                    p.op("scalar", lambda e, ps=ps, lb=lb: e.copy(
                        vq_t[:, lb * D:(lb + 1) * D], ps[:]), sig="sc")
                    pp_free[lb % 2] = ("sc", p.n("sc"))

                pend = None
                for lb, b in enumerate(blocks):
                    a1_xnv(lb, b)
                    tp = a1_proj(lb)
                    if pend is not None:
                        a1_vevac(*pend)
                    pend = (lb, tp)
                a1_vevac(*pend)

                for which, wsrc, dstT in (("q", wqt, qT), ("k", wkt, kT)):
                    wl = load_w(wsrc, wstg_gate[0])
                    p.wait("tensor", "lw", wl)

                    def qk_rot(lb, b, tproj, which=which, dstT=dstT):
                        ps = psPP[lb % 2]
                        cb = costab[:, b * 32:(b + 1) * 32]
                        sb_ = sintab[:, b * 32:(b + 1) * 32]
                        bc = lambda t: t.rearrange("p (a u) -> p a u", a=1).broadcast_to(
                            [128, H, 32])
                        x0 = ps[:].rearrange("p (h c u) -> p h c u", h=H, c=2)[:, :, 0, :]
                        x1 = ps[:].rearrange("p (h c u) -> p h c u", h=H, c=2)[:, :, 1, :]
                        r0 = rot[:].rearrange("p (h c u) -> p h c u", h=H, c=2)[:, :, 0, :]
                        r1 = rot[:].rearrange("p (h c u) -> p h c u", h=H, c=2)[:, :, 1, :]
                        ta0 = ta[:, 0:512].rearrange("p (h u) -> p h u", h=H)
                        ta1 = ta[:, 512:1024].rearrange("p (h u) -> p h u", h=H)
                        p.wait("vector", "t", tproj)
                        p.wait("vector", "t", trT[0])
                        p.op("vector", lambda e, x0=x0, cb=cb, ta0=ta0: e.tensor_tensor(
                            ta0, x0, bc(cb), ALU.mult))
                        p.op("vector", lambda e, x1=x1, sb_=sb_, ta1=ta1: e.tensor_tensor(
                            ta1, x1, bc(sb_), ALU.mult))
                        p.op("vector", lambda e, r0=r0, ta0=ta0, ta1=ta1: e.tensor_tensor(
                            r0, ta0, ta1, ALU.subtract))
                        p.op("vector", lambda e, x0=x0, sb_=sb_, ta0=ta0: e.tensor_tensor(
                            ta0, x0, bc(sb_), ALU.mult))
                        p.op("vector", lambda e, x1=x1, cb=cb, ta1=ta1: e.tensor_tensor(
                            ta1, x1, bc(cb), ALU.mult))
                        p.op("vector", lambda e, r1=r1, ta0=ta0, ta1=ta1: e.tensor_tensor(
                            r1, ta0, ta1, ALU.add), sig="v")
                        pp_free[lb % 2] = ("v", p.n("v"))
                        if which == "k":
                            p.op("vector", lambda e, lb=lb: e.tensor_tensor(
                                kbq[:, lb * D:(lb + 1) * D].rearrange(
                                    "p (h u) -> p h u", h=H),
                                rot[:].rearrange("p (h u) -> p h u", h=H),
                                mbc[:].rearrange("p (h a) -> p h a", a=1).broadcast_to(
                                    [128, H, DH]), ALU.mult), sig="v")
                        vrot = p.n("v")
                        p.wait("tensor", "v", vrot)
                        for c_ in range(8):
                            p.op("tensor", lambda e, c_=c_: e.transpose(
                                psA[:, c_ * 128:(c_ + 1) * 128],
                                rot[:, c_ * 128:(c_ + 1) * 128], idf[:]),
                                sig="t" if c_ == 7 else None)
                        trT[0] = p.n("t")
                        p.wait("scalar", "t", p.n("t"))
                        p.op("scalar", lambda e, lb=lb, dstT=dstT: e.copy(
                            dstT[:].rearrange("p (c t) -> p c t", c=8)[:, :, lb * 128:
                                                                      (lb + 1) * 128],
                            psA[:].rearrange("p (c t) -> p c t", c=8)), sig="sc")
                        p.wait("tensor", "sc", p.n("sc"))

                    pend = None
                    for lb, b in enumerate(blocks):
                        tp = a1_proj(lb)
                        if pend is not None:
                            qk_rot(*pend)
                        pend = (lb, b, tp)
                    qk_rot(*pend)

                wl = load_w(wo_d, wstg_gate[0])

                gn_a = xnT[:, 0:D]
                gn_b = xnT[:, D:2 * D]
                woT = xnT[:, 2 * D:3 * D]

                def emit_A(b, lc, first):
                    p.wait("tensor", "lw", wl)
                    qstall = rrT
                    if not first:
                        p.wait("vector", "t", t_heads[0])
                        qv = qT[:].rearrange("p (pc t) -> p pc t", pc=8)[
                            :, :, lc * 128:(lc + 1) * 128]
                        p.op("vector", lambda e, qv=qv: e.tensor_tensor(
                            qstall[:].rearrange("p (pc t) -> p pc t", pc=8), qv,
                            mstab[:].rearrange("p (pc t) -> p pc t", pc=8), ALU.mult),
                            sig="v")
                        p.wait("tensor", "v", p.n("v"))
                    adt_v = [None, None]
                    for hf in range(2):
                        if hf == 1:
                            p.wait("tensor", "v", adt_v[0])
                        for j in (0, 2, 4, 6, 1, 3, 5, 7):
                            hh = hf * 8 + j
                            pc = hh // 2
                            base = 64 * (hh % 2)
                            ksl = kT[base:base + 64,
                                     pc * 512 + lc * 128: pc * 512 + (lc + 1) * 128]
                            qsl = qT[base:base + 64,
                                     pc * 512 + lc * 128: pc * 512 + (lc + 1) * 128]
                            p.op("tensor", lambda e, j=j, ksl=ksl, qsl=qsl: e.matmul(
                                psB[:, j * 128:(j + 1) * 128], ksl, qsl,
                                start=(j % 4 == 0), stop=True,
                                skip_group_check=(j % 4 != 0)),
                                sig="t" if j == 7 else None)
                        p.wait("vector", "t", p.n("t"))
                        p.op("vector", lambda e, hf=hf: e.tensor_tensor(
                            [rr1, rr2][hf][:], psB[:], dmask[:, hf * 1024:(hf + 1) * 1024],
                            ALU.mult), sig="v")
                        adt_v[hf] = p.n("v")
                    return adt_v

                def emit_R(b, lc, first, adt_v):
                    p.wait("tensor", "sd", s_dma[0])
                    p.wait("tensor", "v", gn_psC[0])
                    for hf in range(2):
                        p.wait("tensor", "v", adt_v[hf])
                        for j in range(8):
                            hh = hf * 8 + j
                            base = 64 * (hh % 2)
                            vsl = vq_t[:, lc * D + hh * DH: lc * D + (hh + 1) * DH]
                            p.op("tensor", lambda e, hh=hh, j=j, hf=hf, vsl=vsl,
                                 first=first: e.matmul(
                                     psC[:, hh * DH:(hh + 1) * DH],
                                     [rr1, rr2][hf][:, j * 128:(j + 1) * 128], vsl,
                                     start=True, stop=first),
                                 sig="t" if (first and hh == H - 1) else None)
                            if not first:
                                p.op("tensor", lambda e, hh=hh, base=base: e.matmul(
                                    psC[:, hh * DH:(hh + 1) * DH],
                                    _qst_ap(rrT, hh, lc),
                                    ssb[base:base + 64, hh * DH:(hh + 1) * DH],
                                    start=False, stop=True),
                                    sig="t" if hh == H - 1 else None)
                    t_heads[0] = p.n("t")

                gn_c = xnT[:, 3 * D:4 * D]

                def emit_S(b, lc, first):
                    th = t_heads[0]
                    p.wait("vector", "t", th)
                    if not first:
                        p.op("vector", lambda e: e.tensor_tensor(
                            psD[0:64, :].rearrange("p (h u) -> p h u", h=H),
                            psD[0:64, :].rearrange("p (h u) -> p h u", h=H),
                            scc[0:64, :].rearrange("p (h a) -> p h a", a=1).broadcast_to(
                                [64, H, DH]), ALU.mult), sig="v")
                        p.wait("tensor", "v", p.n("v"))
                    for hh in range(H):
                        ksl2 = kbq[:, lc * D + hh * DH: lc * D + (hh + 1) * DH]
                        vsl2 = vq_t[:, lc * D + hh * DH: lc * D + (hh + 1) * DH]
                        st_flag = first and (hh % 8 == 0)
                        sp_flag = (hh % 8 == 7)
                        p.op("tensor", lambda e, hh=hh, ksl2=ksl2, vsl2=vsl2,
                             st_flag=st_flag, sp_flag=sp_flag, sk=not first: e.matmul(
                                 psD[0:64, hh * DH:(hh + 1) * DH], ksl2, vsl2,
                                 start=st_flag, stop=sp_flag, skip_group_check=sk),
                             sig="t" if hh == H - 1 else None)
                    p.wait("vector", "t", p.n("t"))
                    p.op("vector", lambda e: e.tensor_copy(ssb[0:64, :], psD[0:64, :]),
                         sig="v")
                    p.wait("sync", "v", p.n("v"))
                    s_dma[0] = p.dma("sync", ssb[64:128, :], ssb[0:64, :], sig="sd")
                    p.op("vector", lambda e: e.tensor_copy(gn_c, psC[:]), sig="v")
                    gn_psC[0] = p.n("v")

                def emit_gnWO(b, lc, first):
                    p.wait("scalar", "v", gn_psC[0])
                    p.op("scalar", lambda e: e.activation(dump[:], gn_c, ACT.Square),
                         sig="sc")
                    p.wait("vector", "sc", p.n("sc"))
                    p.op("vector", lambda e: e.tensor_reduce(
                        gnm[:], gn_c.rearrange("p (h u) -> p h u", h=H), AX.X, ALU.add))
                    p.op("vector", lambda e: e.tensor_reduce(
                        gnq[:], dump[:].rearrange("p (h u) -> p h u", h=H), AX.X, ALU.add))
                    p.op("vector", lambda e: e.tensor_single_scalar(gnm[:], gnm[:],
                                                                    1.0 / DH, ALU.mult))
                    p.op("vector", lambda e: e.tensor_tensor(ta[:, 0:H], gnm[:], gnm[:],
                                                             ALU.mult))
                    p.op("vector", lambda e: e.scalar_tensor_tensor(
                        gnr[:], gnq[:], 1.0 / DH, ta[:, 0:H], ALU.mult, ALU.subtract),
                        sig="v")
                    p.wait("scalar", "v", p.n("v"))
                    p.op("scalar", lambda e: e.activation(gnr[:], gnr[:], ACT.Ln,
                                                          bias=C_EPS))
                    p.op("scalar", lambda e: e.activation(gnr[:], gnr[:], ACT.Exp,
                                                          scale=C_NEGH), sig="sc")
                    p.wait("vector", "sc", p.n("sc"))
                    hview = lambda t: t.rearrange("p (h u) -> p h u", h=H)
                    hbc = lambda t: t.rearrange("p (h a) -> p h a", a=1).broadcast_to(
                        [128, H, DH])
                    p.op("vector", lambda e: e.tensor_tensor(
                        hview(gn_a), gn_c.rearrange("p (h u) -> p h u", h=H),
                        hbc(gnm[:]), ALU.subtract))
                    p.op("vector", lambda e: e.tensor_tensor(hview(gn_b), hview(gbc[:]),
                                                             hbc(gnr[:]), ALU.mult))
                    p.op("vector", lambda e: e.tensor_tensor(gn_a, gn_a, gn_b, ALU.mult))
                    p.op("vector", lambda e: e.tensor_tensor(gn_a, gn_a, bbc[:],
                                                             ALU.add), sig="v")
                    p.wait("tensor", "v", p.n("v"))
                    for c_ in range(8):
                        p.op("tensor", lambda e, c_=c_: e.transpose(
                            psA[:, c_ * 128:(c_ + 1) * 128], gn_a[:, c_ * 128:(c_ + 1) * 128],
                            idf[:]), sig="t" if c_ == 7 else None)
                    p.wait("vector", "t", p.n("t"))
                    p.op("vector", lambda e: e.tensor_copy(woT[:], psA[:]), sig="v")
                    p.wait("tensor", "v", p.n("v"))
                    for i in range(8):
                        for d2 in range(2):
                            p.op("tensor", lambda e, i=i, d2=d2: e.matmul(
                                psA[:, d2 * 512:(d2 + 1) * 512], woT[:, i * 128:(i + 1) * 128],
                                wstg[:, i * D + d2 * 512: i * D + d2 * 512 + 512],
                                start=(i == 0), stop=(i == 7)),
                                sig="t" if (i == 7 and d2 == 1) else None)
                    wo_t = p.n("t")
                    wstg_gate[0] = wo_t
                    xb2 = xab[:, (lc % 2) * D:(lc % 2) * D + D]
                    lxs2 = "lx%d" % (lc % 2)
                    p.wait("sync", "v", xab_free[lc % 2])
                    l2 = p.dma("sync", xb2[:], src_hbm[b * 128:(b + 1) * 128, :], sig=lxs2)
                    p.wait("vector", "t", wo_t)
                    p.wait("vector", lxs2, l2)
                    xob2 = xo2[:, (lc % 2) * D:(lc % 2) * D + D]
                    p.wait("vector", "stX%d" % (lc % 2), xo_free2[lc % 2])
                    p.op("vector", lambda e, xb2=xb2, xob2=xob2: e.tensor_tensor(
                        xob2[:], psA[:], xb2[:], ALU.add), sig="v")
                    xab_free[lc % 2] = p.n("v")
                    p.wait("scalar", "v", p.n("v"))
                    if post_stats is not None:
                        mun, rsn = post_stats
                        ln_stats_tile(xob2, mun[:, b:b + 1], rsn[:, b:b + 1])
                    xo_free2[lc % 2] = p.dma("scalar", dst_hbm[b * 128:(b + 1) * 128, :],
                                             xob2[:], sig="stX%d" % (lc % 2))

                pending = None
                for lc, b in enumerate(blocks):
                    first = (b == 0)
                    av = emit_A(b, lc, first)
                    if pending is not None:
                        emit_S(*pending)
                    emit_R(b, lc, first, av)
                    if pending is not None:
                        emit_gnWO(*pending)
                    pending = (b, lc, first)
                emit_S(*pending)
                emit_gnWO(*pending)
            p.barrier()

    prelude()
    ln1_stats()

    only_ffn1 = (stage == "ffn1")
    ternarize(wgt, 8, INNER, wmg, None, Mgs)
    ternarize(wut, 8, INNER, wmu, None, Mus)
    mgg = nc.sbuf_tensor(_uniq("mg_pre"), [128, 8 * INNER], BF16)
    mug = nc.sbuf_tensor(_uniq("mu_pre"), [128, 8 * INNER], BF16)
    mg_pre = mgg.__enter__().ap()
    mu_pre = mug.__enter__().ap()
    for c_ in range(8):
        p.dma("sync", mg_pre[:, c_ * INNER:(c_ + 1) * INNER],
              Mgs[:, c_ * INNER:(c_ + 1) * INNER], sig="lw")
    for c_ in range(8):
        p.dma("sync", mu_pre[:, c_ * INNER:(c_ + 1) * INNER],
              Mus[:, c_ * INNER:(c_ + 1) * INNER], sig="lw")
    mpre_t = p.n("lw")
    ternarize(wdt, 32, D, wmd, None, Mds)
    p.wait("tensor", "lw", mpre_t)
    ffn_phase(x_in, y_out if only_ffn1 else Xs, mu1, rs1, f2b1, mg_pre, mu_pre,
              post_stats=None if only_ffn1 else (mu2, rs2),
              adopt_guards=(mgg, mug))

    if stage in ("attn", "full"):
        attention(Xs, Xs if stage == "full" else y_out, mu2, rs2,
                  post_stats=(mu3, rs3) if stage == "full" else None)

    if stage == "full":
        ffn_phase(Xs, y_out, mu3, rs3, f2b2, None, None, post_stats=None)

    for s_ in ("stK", "stX0", "stX1", "stM0", "stM1"):
        p.wait("sync", s_, p.n(s_))
        p.wait("vector", s_, p.n(s_))

    p.emit()
    return nc


def _host_layouts(WQ, WK, WV, WO, w_gate, w_up, w_down, gamma, beta):
    perm = np.concatenate([np.arange(0, DH, 2), np.arange(1, DH, 2)])
    wq = WQ[:, perm, :]
    wk = WK[:, perm, :]
    return dict(
        wgT=np.ascontiguousarray(w_gate.T),
        wuT=np.ascontiguousarray(w_up.T),
        wdT=np.ascontiguousarray(w_down.T),
        wqT=np.ascontiguousarray(wq.transpose(2, 0, 1).reshape(D, D)),
        wkT=np.ascontiguousarray(wk.transpose(2, 0, 1).reshape(D, D)),
        wvT=np.ascontiguousarray(WV.transpose(2, 0, 1).reshape(D, D)),
        wo=np.ascontiguousarray(WO),
        gam=np.ascontiguousarray(gamma.reshape(1, D)),
        bet=np.ascontiguousarray(beta.reshape(1, D)))


def kernel(L0_attn_x, L0_attn_WQ, L0_attn_WK, L0_attn_WV, L0_attn_WO,
           L0_attn_gamma, L0_attn_beta, L0_ffn_w_gate, L0_ffn_w_up,
           L0_ffn_w_down):
    shared = _host_layouts(L0_attn_WQ, L0_attn_WK, L0_attn_WV, L0_attn_WO,
                           L0_ffn_w_gate, L0_ffn_w_up, L0_ffn_w_down,
                           L0_attn_gamma, L0_attn_beta)
    nc = build_nc("full")
    in_maps = []
    for i in range(8):
        m = dict(shared)
        m["x"] = np.ascontiguousarray(np.asarray(L0_attn_x)[i])
        in_maps.append(m)
    res = run_bass_kernel_spmd(nc, in_maps, core_ids=list(range(8)))
    return np.stack([np.asarray(r["y"]) for r in res.results], axis=0)


# version 07 (final)
# speedup vs baseline: 1.3741x; 1.0008x over previous
import contextlib
import math
import numpy as np
import concourse.bass as bass
import concourse.mybir as mybir
from concourse.bass_utils import run_bass_kernel_spmd

F32 = mybir.dt.float32
BF16 = mybir.dt.bfloat16
I16 = mybir.dt.int16
ALU = mybir.AluOpType
ACT = mybir.ActivationFunctionType
AX = mybir.AxisListType

SCAN_QST_BATCH = True
SCAN_ADT_BATCH = True

S = 4096
D = 1024
NB = S // 128
INNER = 4096
H = 16
DH = 64
NEI = 8
BPE = NB // NEI
EPS = 1e-5
TWO_PI = 6.283185307179586
PI_CLAMP = 3.141592
DECAY = [1.0 - 2.0 ** (-5.0 - h) for h in range(H)]


class Prog:
    ENGS = ("sync", "vector", "scalar", "tensor", "gpsimd")

    DRAIN_ENGS = ("vector", "scalar", "gpsimd")

    def __init__(self, nc):
        self.nc = nc
        self.q = {e: [] for e in self.ENGS}
        self.sems = {}
        self._dirty = {e: False for e in self.ENGS}

    def sem(self, name):
        self.sems[name] = [self.nc.alloc_semaphore(name), 0]

    def n(self, name):
        return self.sems[name][1]

    def _pre(self, eng):
        if eng in self.DRAIN_ENGS and self._dirty[eng]:
            self.q[eng].append(lambda e: e.drain())
            self._dirty[eng] = False

    def op(self, eng, fn, sig=None, inc=1):
        self._pre(eng)
        self._dirty[eng] = eng in self.DRAIN_ENGS
        if sig is not None:
            rec = self.sems[sig]
            rec[1] += inc
            h = rec[0]

            def wrapped(e, fn=fn, h=h, inc=inc):
                fn(e).then_inc(h, inc)
            self.q[eng].append(wrapped)
            return rec[1]
        self.q[eng].append(fn)
        return None

    def dma(self, eng, out, in_, sig="ld"):
        rec = self.sems[sig]
        rec[1] += 16
        h = rec[0]
        self.q[eng].append(lambda e, o=out, i=in_, h=h: e.dma_start(out=o, in_=i)
                           .then_inc(h, 16))
        return rec[1]

    def wait(self, eng, sem, val):
        if val is None:
            return
        h = self.sems[sem][0]
        self.q[eng].append(lambda e, h=h, v=val: e.wait_ge(h, v))

    def inc(self, eng, sem, by=1):
        self._pre(eng)
        rec = self.sems[sem]
        rec[1] += by
        h = rec[0]
        self.q[eng].append(lambda e, h=h, b=by: e.sem_inc(h, b))
        return rec[1]

    DMA_SEMS = ("lw", "lx0", "lx1", "lx2", "lx3", "lk0", "lk1", "stK", "stX0",
                "stX1", "stM0", "stM1", "sd")

    def barrier(self):
        for s_ in self.DMA_SEMS:
            if s_ in self.sems:
                self.wait("sync", s_, self.sems[s_][1] or None)
        for e in self.ENGS:
            self._pre(e)
            self.inc(e, "bar", 1)
        tgt = self.n("bar")
        for e in self.ENGS:
            self.wait(e, "bar", tgt)

    def drain(self, eng="vector"):
        self.q[eng].append(lambda e: e.drain())
        self._dirty[eng] = False

    def emit(self):
        with self.nc.Block() as block:
            @block.sync
            def _(e):
                for fn in self.q["sync"]:
                    fn(e)

            @block.vector
            def _(e):
                for fn in self.q["vector"]:
                    fn(e)

            @block.scalar
            def _(e):
                for fn in self.q["scalar"]:
                    fn(e)

            @block.tensor
            def _(e):
                for fn in self.q["tensor"]:
                    fn(e)

            @block.gpsimd
            def _(e):
                for fn in self.q["gpsimd"]:
                    fn(e)


def _qst_ap(qstall, hh, lc):
    return qstall[64 * (hh % 2):64 * (hh % 2) + 64,
                  (hh // 2) * 128:(hh // 2) * 128 + 128]


def build_nc(stage="full"):
    nc = bass.Bass("TRN2", target_bir_lowering=False)

    x_in = nc.dram_tensor("x", [S, D], F32, kind="ExternalInput").ap()
    wgt = nc.dram_tensor("wgT", [D, INNER], F32, kind="ExternalInput").ap()
    wut = nc.dram_tensor("wuT", [D, INNER], F32, kind="ExternalInput").ap()
    wdt = nc.dram_tensor("wdT", [INNER, D], F32, kind="ExternalInput").ap()
    wqt = nc.dram_tensor("wqT", [D, D], F32, kind="ExternalInput").ap()
    wkt = nc.dram_tensor("wkT", [D, D], F32, kind="ExternalInput").ap()
    wvt = nc.dram_tensor("wvT", [D, D], F32, kind="ExternalInput").ap()
    wo_d = nc.dram_tensor("wo", [D, D], F32, kind="ExternalInput").ap()
    gam_d = nc.dram_tensor("gam", [1, D], F32, kind="ExternalInput").ap()
    bet_d = nc.dram_tensor("bet", [1, D], F32, kind="ExternalInput").ap()
    y_out = nc.dram_tensor("y", [S, D], F32, kind="ExternalOutput").ap()

    Xs = nc.dram_tensor("Xs", [S, D], F32, kind="Internal").ap()
    K2s = nc.dram_tensor("K2s", [NB, 128, INNER], BF16, kind="Internal").ap()
    Mgs = nc.dram_tensor("Mgs", [128, 8 * INNER], BF16, kind="Internal").ap()
    Mus = nc.dram_tensor("Mus", [128, 8 * INNER], BF16, kind="Internal").ap()
    Mds = nc.dram_tensor("Mds", [128, 32 * D], BF16, kind="Internal").ap()

    p = Prog(nc)
    for s_ in ("lw", "lx0", "lx1", "lx2", "lx3", "lk0", "lk1", "stK", "stX0",
               "stX1", "stM0", "stM1", "v", "sc", "t", "g", "bar", "sd"):
        p.sem(s_)

    al = nc.alloc_sbuf_tensor
    alp = nc.alloc_psum_tensor
    _uid = [0]

    def _uniq(n):
        _uid[0] += 1
        return f"{n}_{_uid[0]}"


    psA = alp("psA", [128, 1024], F32).ap()
    psB = alp("psB", [128, 1024], F32).ap()
    psC = alp("psC", [128, 1024], F32).ap()
    psD = alp("psD", [128, 1024], F32).ap()

    cst = al("cst", [128, 8], F32).ap()
    C_HALF = cst[:, 0:1]
    C_NEGH = cst[:, 1:2]
    C_EPS = cst[:, 2:3]
    C_I2PI = cst[:, 3:4]
    C_PIH = cst[:, 4:5]
    C_LNA = cst[:, 5:6]
    C_128 = cst[:, 6:7]
    ones128 = al("ones128", [128, 1], F32).ap()
    ones1r = al("ones1r", [1, 128], F32).ap()
    idf = al("idf", [128, 128], F32).ap()
    idb = al("idb", [128, 128], BF16).ap()
    wmg = al("wmg", [128, 1], F32).ap()
    wmu = al("wmu", [128, 1], F32).ap()
    wmd = al("wmd", [128, 1], F32).ap()
    mu1 = al("mu1", [128, NB], F32).ap()
    rs1 = al("rs1", [128, NB], F32).ap()
    mu2 = al("mu2", [128, NB], F32).ap()
    rs2 = al("rs2", [128, NB], F32).ap()
    mu3 = al("mu3", [128, NB], F32).ap()
    rs3 = al("rs3", [128, NB], F32).ap()
    f2b1 = al("f2b1", [128, NB], F32).ap()
    f2b2 = al("f2b2", [128, NB], F32).ap()
    stt = al("stt", [128, 8], F32).ap()
    dump = al("dump", [128, D], F32).ap()

    def ln_stats_tile(xtile, mu_c, rs_c):
        p.op("vector", lambda e: e.tensor_reduce(stt[:, 0:1], xtile[:], AX.X, ALU.add))
        p.drain()
        p.op("vector", lambda e: e.tensor_single_scalar(mu_c, stt[:, 0:1], 1.0 / D, ALU.mult))
        p.drain()
        p.op("vector", lambda e: e.tensor_tensor(stt[:, 2:3], mu_c, mu_c, ALU.mult), sig="v")
        vt = p.n("v")
        p.op("scalar", lambda e: e.activation(dump[:], xtile[:], ACT.Square,
                                              accum_out=stt[:, 1:2]))
        p.wait("scalar", "v", vt)
        p.op("scalar", lambda e: e.copy(stt[:, 3:4], stt[:, 1:2]), sig="sc")
        p.wait("vector", "sc", p.n("sc"))
        p.drain()
        p.op("vector", lambda e: e.scalar_tensor_tensor(stt[:, 4:5], stt[:, 3:4], 1.0 / D,
                                                        stt[:, 2:3], ALU.mult, ALU.subtract),
             sig="v")
        p.wait("scalar", "v", p.n("v"))
        p.op("scalar", lambda e: e.activation(stt[:, 5:6], stt[:, 4:5], ACT.Ln, bias=C_EPS))
        p.op("scalar", lambda e: e.activation(rs_c, stt[:, 5:6], ACT.Exp, scale=C_NEGH),
             sig="sc")
        p.wait("vector", "sc", p.n("sc"))

    def round_to_ints(src, scale_ap, ti, cf, ei, tg, out, width):
        p.wait("scalar", "v", p.n("v"))
        p.op("scalar", lambda e: e.activation(ti[:, 0:width], src, ACT.Identity,
                                              scale=scale_ap, bias=C_HALF))
        p.op("scalar", lambda e: e.copy(cf[:, 0:width], ti[:, 0:width]), sig="sc")
        p.wait("vector", "sc", p.n("sc"))
        p.op("vector", lambda e: e.scalar_tensor_tensor(ei[:, 0:width], src, scale_ap,
                                                        cf[:, 0:width], ALU.mult, ALU.subtract))
        p.op("vector", lambda e: e.tensor_single_scalar(tg[:, 0:width], ei[:, 0:width],
                                                        -0.5, ALU.is_lt))
        p.op("vector", lambda e: e.tensor_tensor(out, cf[:, 0:width], tg[:, 0:width],
                                                 ALU.subtract), sig="v")
        return p.n("v")

    def prelude():
        for i, v in enumerate([0.5, -0.5, EPS, 1.0 / TWO_PI, np.pi / 2,
                               -math.log(10.0) / 8.0, 128.0]):
            p.op("vector", lambda e, i=i, v=v: e.memset(cst[:, i:i + 1], v))
        p.op("vector", lambda e: e.memset(ones128[:], 1.0))
        p.op("vector", lambda e: e.memset(ones1r[:], 1.0))
        p.op("gpsimd", lambda e: e.iota(idf[:], [[1, 128]], base=0, channel_multiplier=-1,
                                        allow_small_or_imprecise_dtypes=True), sig="g")
        p.wait("vector", "g", p.n("g"))
        p.op("vector", lambda e: e.tensor_single_scalar(idf[:], idf[:], 0.0, ALU.is_equal))
        p.op("vector", lambda e: e.tensor_copy(idb[:], idf[:]), sig="v")
        p.wait("tensor", "v", p.n("v"))
        p.wait("scalar", "v", p.n("v"))

    def ln1_stats():
        with nc.sbuf_tensor(_uniq("p0a_x"), [128, 2 * D], F32) as xg:
            xt = xg.ap()
            done = [None, None]
            for b in range(NB):
                sl = xt[:, (b % 2) * D:(b % 2) * D + D]
                lxs = "lx%d" % (b % 2)
                p.wait("sync", "v", done[b % 2])
                ldt = p.dma("sync", sl, x_in[b * 128:(b + 1) * 128, :], sig=lxs)
                p.wait("vector", lxs, ldt)
                p.wait("scalar", lxs, ldt)
                ln_stats_tile(sl, mu1[:, b:b + 1], rs1[:, b:b + 1])
                p.inc("vector", "v", 1)
                done[b % 2] = p.n("v")
        p.barrier()

    def ternarize(src, n_chunks, cols, wm_tile, dst_sbuf, dst_hbm):
        with contextlib.ExitStack() as st_:
            wstg = st_.enter_context(nc.sbuf_tensor(_uniq("p0_wstg"), [128, 4 * cols], F32)).ap()
            mtmp = st_.enter_context(nc.sbuf_tensor(_uniq("p0_mtmp"), [128, 2 * cols], BF16)).ap()
            acc = st_.enter_context(nc.sbuf_tensor(_uniq("p0_acc"), [128, 1], F32)).ap()
            w11 = st_.enter_context(nc.sbuf_tensor(_uniq("p0_w11"), [1, 1], F32)).ap()
            ti = st_.enter_context(nc.sbuf_tensor(_uniq("p0_ti"), [128, cols], I16)).ap()
            cf = st_.enter_context(nc.sbuf_tensor(_uniq("p0_cf"), [128, cols], BF16)).ap()
            tg = st_.enter_context(nc.sbuf_tensor(_uniq("p0_tg"), [128, cols], BF16)).ap()
            r_ = st_.enter_context(nc.sbuf_tensor(_uniq("p0_r"), [128, cols], F32)).ap()
            isc = st_.enter_context(nc.sbuf_tensor(_uniq("p0_isc"), [128, 1], F32)).ap()

            p.op("vector", lambda e: e.memset(acc[:], 0.0))
            done = [None, None, None, None]
            for c_ in range(n_chunks):
                sl = wstg[:, (c_ % 4) * cols:(c_ % 4) * cols + cols]
                lxs = "lx%d" % (c_ % 4)
                p.wait("sync", "v", done[c_ % 4])
                ldt = p.dma("sync", sl, src[c_ * 128:(c_ + 1) * 128, :], sig=lxs)
                p.wait("vector", lxs, ldt)
                p.op("vector", lambda e, sl=sl, c_=c_: e.tensor_reduce(
                    stt[:, c_ % 2:c_ % 2 + 1], sl, AX.X, ALU.add,
                    apply_absolute_value=True))
                p.op("vector", lambda e, c_=c_: e.tensor_tensor(
                    acc[:], acc[:], stt[:, c_ % 2:c_ % 2 + 1], ALU.add), sig="v")
                done[c_ % 4] = p.n("v")
            p.drain()
            p.inc("vector", "v", 1)
            p.wait("tensor", "v", p.n("v"))
            p.op("tensor", lambda e: e.matmul(psA[0:1, 0:1], ones128[:], acc[:],
                                              start=True, stop=True), sig="t")
            p.wait("vector", "t", p.n("t"))
            ntot = float(n_chunks * 128 * cols)
            p.op("vector", lambda e: e.tensor_copy(w11[:], psA[0:1, 0:1]))
            p.drain()
            p.op("vector", lambda e: e.tensor_scalar(w11[:], w11[:], 1.0 / ntot, 1e-5,
                                                     ALU.mult, ALU.max), sig="v")
            p.wait("tensor", "v", p.n("v"))
            p.op("tensor", lambda e: e.matmul(psA[0:128, 1:2], ones1r[:], w11[:],
                                              start=True, stop=True), sig="t")
            p.wait("vector", "t", p.n("t"))
            p.op("vector", lambda e: e.tensor_copy(wm_tile[:], psA[0:128, 1:2]))
            p.drain()
            p.op("vector", lambda e: e.reciprocal(isc[:], wm_tile[:]))
            p.drain()
            p.inc("vector", "v", 1)
            vsc = p.n("v")
            p.wait("scalar", "v", vsc)

            mfree = [None, None]
            for c_ in range(n_chunks):
                sl = wstg[:, (c_ % 2) * cols:(c_ % 2) * cols + cols]
                msl = mtmp[:, (c_ % 2) * cols:(c_ % 2) * cols + cols]
                lxs = "lx%d" % (c_ % 2)
                p.wait("sync", "v", done[c_ % 2])
                ldt = p.dma("sync", sl, src[c_ * 128:(c_ + 1) * 128, :], sig=lxs)
                p.wait("vector", lxs, ldt)
                p.wait("scalar", lxs, ldt)
                p.wait("vector", "stM%d" % (c_ % 2), mfree[c_ % 2])
                vq = round_to_ints(sl, isc[:, 0:1], ti, cf, r_, tg, msl, cols)
                p.wait("vector", "v", vq)
                p.op("vector", lambda e, msl=msl: e.tensor_scalar(
                    msl, msl, 1.0, -1.0, ALU.min, ALU.max), sig="v")
                done[c_ % 2] = p.n("v")
                p.wait("scalar", "v", p.n("v"))
                mfree[c_ % 2] = p.dma("scalar", dst_hbm[:, c_ * cols:(c_ + 1) * cols],
                                      msl, sig="stM%d" % (c_ % 2))
            p.barrier()

    def ffn_phase(src_hbm, dst_hbm, mu_t, rs_t, f2_t, mg, mu_, post_stats,
                  adopt_guards=None):
        HW2 = INNER // 2
        with contextlib.ExitStack() as st_:
            sb = lambda n, sh, dt: st_.enter_context(nc.sbuf_tensor(_uniq(n), sh, dt)).ap()
            for g_ in (adopt_guards or ()):
                st_.push(g_)
            if mg is None:
                mg = sb("fa_mg", [128, 8 * INNER], BF16)
                mu_ = sb("fa_mu", [128, 8 * INNER], BF16)
                for c_ in range(8):
                    p.dma("sync", mg[:, c_ * INNER:(c_ + 1) * INNER],
                          Mgs[:, c_ * INNER:(c_ + 1) * INNER], sig="lw")
                for c_ in range(8):
                    p.dma("sync", mu_[:, c_ * INNER:(c_ + 1) * INNER],
                          Mus[:, c_ * INNER:(c_ + 1) * INNER], sig="lw")
                p.wait("tensor", "lw", p.n("lw"))
            h_t = sb("fa_h", [128, INNER], F32)
            gsm = sb("fa_g", [128, 512], F32)
            xin = sb("fa_xin", [128, 2 * D], F32)
            xn = sb("fa_xn", [128, D], F32)
            ti1 = sb("fa_ti1", [128, D], I16)
            cf1 = sb("fa_cf1", [128, D], BF16)
            tg1 = ti1.bitcast(BF16)
            e1 = sb("fa_e1", [128, D], F32)
            k1 = sb("fa_k1", [128, 2 * D], BF16)
            k1T = sb("fa_k1T", [128, D], BF16)
            HW4 = INNER // 4
            ti2 = ti1
            cf2 = cf1
            tg2 = tg1
            e2 = e1
            k2h = sb("fa_k2h", [128, INNER], BF16)
            k2T = sb("fa_k2T", [128, INNER], BF16)
            sc1c = sb("fa_sc1", [128, 2], F32)
            sc2c = sb("fa_sc2", [128, 2], F32)
            f1g = sb("fa_f1g", [128, 2], F32)
            f1u = sb("fa_f1u", [128, 2], F32)

            rotp = [(psA[:, 0:512], psA[:, 512:1024]),
                    (psC[:, 0:512], psC[:, 512:1024]),
                    (psD[:, 0:512], psD[:, 512:1024])]
            psBb = psB.bitcast(BF16)
            rot_free = [None, None, None]
            k2_store = [None]
            k2ev_done = [None]
            k2b2_free = [None]
            k2h_v = {}
            xin_free = [None, None]

            k1_vk = {}

            def front_dve(b):
                xb = xin[:, (b % 2) * D:(b % 2) * D + D]
                k1s = k1[:, (b % 2) * D:(b % 2) * D + D]
                lxs = "lx%d" % (b % 2)
                p.wait("sync", "v", xin_free[b % 2])
                ldt = p.dma("sync", xb[:], src_hbm[b * 128:(b + 1) * 128, :], sig=lxs)
                p.wait("vector", lxs, ldt)
                p.op("vector", lambda e, b=b, xb=xb: e.tensor_scalar(
                    xn[:], xb[:], mu_t[:, b:b + 1], rs_t[:, b:b + 1],
                    ALU.subtract, ALU.mult), sig="v")
                xin_free[b % 2] = p.n("v")
                sc1 = sc1c[:, b % 2:b % 2 + 1]
                fg = f1g[:, b % 2:b % 2 + 1]
                fu = f1u[:, b % 2:b % 2 + 1]
                p.op("vector", lambda e: e.tensor_reduce(stt[:, 0:1], xn[:], AX.X, ALU.max,
                                                         apply_absolute_value=True))
                p.op("vector", lambda e: e.tensor_single_scalar(stt[:, 0:1], stt[:, 0:1],
                                                                1e-5, ALU.max))
                p.op("vector", lambda e, fg=fg: e.scalar_tensor_tensor(
                    fg, stt[:, 0:1], 1.0 / 127.0, wmg[:], ALU.mult, ALU.mult))
                p.op("vector", lambda e, fu=fu: e.scalar_tensor_tensor(
                    fu, stt[:, 0:1], 1.0 / 127.0, wmu[:], ALU.mult, ALU.mult))
                p.op("vector", lambda e: e.reciprocal(stt[:, 1:2], stt[:, 0:1]))
                p.op("vector", lambda e, sc1=sc1: e.tensor_single_scalar(
                    sc1, stt[:, 1:2], 127.0, ALU.mult))
                p.inc("vector", "v", 1)
                p.wait("scalar", "v", p.n("v"))
                k1_vk[b] = round_to_ints(xn[:], sc1, ti1, cf1, e1, tg1, k1s, D)

            def front_pe(b):
                k1s = k1[:, (b % 2) * D:(b % 2) * D + D]
                p.wait("tensor", "v", k1_vk[b])
                if k2b2_free[0] is not None:
                    p.wait("tensor", "sc", k2b2_free[0])
                for c_ in range(8):
                    fn = lambda e, c_=c_, k1s=k1s: e.transpose(
                        psBb[:, c_ * 128:(c_ + 1) * 128],
                        k1s[:, c_ * 128:(c_ + 1) * 128], idb[:])
                    p.op("tensor", fn, sig="t" if c_ == 7 else None)
                p.wait("scalar", "t", p.n("t"))
                p.op("scalar", lambda e: e.copy(k1T[:], psBb[:, 0:1024]), sig="sc")
                p.wait("tensor", "sc", p.n("sc"))

            def emit_gu(b, ots):
                fg = f1g[:, b % 2:b % 2 + 1]
                fu = f1u[:, b % 2:b % 2 + 1]
                for ot in ots:
                    gbank, ubank = rotp[ot % 3]
                    p.wait("tensor", "v", rot_free[ot % 3])
                    for i in range(8):
                        p.op("tensor", lambda e, gbank=gbank, i=i, ot=ot: e.matmul(
                            gbank, k1T[:, i * 128:(i + 1) * 128],
                            mg[:, i * INNER + ot * 512: i * INNER + ot * 512 + 512],
                            start=(i == 0), stop=(i == 7)))
                        p.op("tensor", lambda e, ubank=ubank, i=i, ot=ot: e.matmul(
                            ubank, k1T[:, i * 128:(i + 1) * 128],
                            mu_[:, i * INNER + ot * 512: i * INNER + ot * 512 + 512],
                            start=(i == 0), stop=(i == 7)),
                            sig="t" if i == 7 else None)
                    tt = p.n("t")
                    p.wait("scalar", "t", tt)
                    p.op("scalar", lambda e, gbank=gbank, fg=fg: e.activation(
                        gsm[:], gbank, ACT.Silu, scale=fg), sig="sc")
                    p.wait("vector", "sc", p.n("sc"))
                    p.op("vector", lambda e, ubank=ubank, fu=fu, ot=ot:
                         e.scalar_tensor_tensor(h_t[:, ot * 512:(ot + 1) * 512], ubank,
                                                fu, gsm[:], ALU.mult, ALU.mult), sig="v")
                    rot_free[ot % 3] = p.n("v")
                    p.wait("scalar", "v", p.n("v"))

            def emit_hq1(b):
                sc2 = sc2c[:, b % 2:b % 2 + 1]
                p.op("vector", lambda e: e.tensor_reduce(stt[:, 2:3], h_t[:], AX.X, ALU.max,
                                                         apply_absolute_value=True))
                p.op("vector", lambda e: e.tensor_single_scalar(stt[:, 2:3], stt[:, 2:3],
                                                                1e-5, ALU.max))
                p.op("vector", lambda e, b=b: e.scalar_tensor_tensor(
                    f2_t[:, b:b + 1], stt[:, 2:3], 0.5 / 127.0, wmd[:],
                    ALU.mult, ALU.mult))
                p.op("vector", lambda e: e.reciprocal(stt[:, 3:4], stt[:, 2:3]))
                p.op("vector", lambda e, sc2=sc2: e.tensor_single_scalar(
                    sc2, stt[:, 3:4], 127.0, ALU.mult))
                p.inc("vector", "v", 1)
                p.wait("scalar", "v", p.n("v"))
                for hh in range(4):
                    hsl = h_t[:, hh * HW4:(hh + 1) * HW4]
                    k2h_v[(b, hh)] = round_to_ints(hsl, sc2, ti2, cf2, e2, tg2,
                                                   k2h[:, hh * HW4:(hh + 1) * HW4], HW4)

            def emit_k2part(b):
                if k2ev_done[0] is not None:
                    p.wait("tensor", k2ev_done[0][0], k2ev_done[0][1])
                qevac = [None, None]
                for hh in range(4):
                    off = 1024 * (hh % 2)
                    p.wait("tensor", "v", k2h_v[(b, hh)])
                    if qevac[hh % 2] is not None:
                        p.wait("tensor", "sc", qevac[hh % 2])
                    for c_ in range(8):
                        fn = lambda e, off=off, hh=hh, c_=c_: e.transpose(
                            psBb[:, off + c_ * 128: off + (c_ + 1) * 128],
                            k2h[:, hh * HW4 + c_ * 128: hh * HW4 + (c_ + 1) * 128], idb[:])
                        p.op("tensor", fn, sig="t")
                    tq = p.n("t")
                    p.wait("scalar", "t", tq)
                    if hh == 0:
                        p.wait("scalar", "stK", k2_store[0])
                    p.op("scalar", lambda e, hh=hh, off=off: e.copy(
                        k2T[:, hh * HW4:(hh + 1) * HW4], psBb[:, off:off + 1024]),
                        sig="sc")
                    qevac[hh % 2] = p.n("sc")
                k2b2_free[0] = qevac[1]
                k2ev_done[0] = ("sc", p.n("sc"))
                p.wait("scalar", "sc", p.n("sc"))
                k2_store[0] = p.dma("scalar", K2s[b], k2T[:], sig="stK")

            front_dve(0)
            front_pe(0)
            for b in range(NB):
                emit_gu(b, range(0, 4))
                if b + 1 < NB:
                    front_dve(b + 1)
                emit_gu(b, range(4, 8))
                if b >= 1:
                    emit_k2part(b - 1)
                if b + 1 < NB:
                    front_pe(b + 1)
                emit_hq1(b)
            emit_k2part(NB - 1)
            p.barrier()
        with contextlib.ExitStack() as st_:
            sb = lambda n, sh, dt: st_.enter_context(nc.sbuf_tensor(_uniq(n), sh, dt)).ap()
            md = sb("fb_md", [128, 32 * D], BF16)
            k2i = sb("fb_k2i", [128, 2 * INNER], BF16)
            xin2 = sb("fb_xin2", [128, 2 * D], F32)
            xo = sb("fb_xo", [128, 2 * D], F32)
            for c_ in range(8):
                p.dma("sync", md[:, c_ * 4 * D:(c_ + 1) * 4 * D],
                      Mds[:, c_ * 4 * D:(c_ + 1) * 4 * D], sig="lw")
            p.wait("tensor", "lw", p.n("lw"))
            mm_done = [None, None]
            xo_free = [None, None]
            k2_free = [None, None]
            xin_free = [None, None]
            for b in range(NB):
                kb = k2i[:, (b % 2) * INNER:(b % 2) * INNER + INNER]
                xb = xin2[:, (b % 2) * D:(b % 2) * D + D]
                xob = xo[:, (b % 2) * D:(b % 2) * D + D]
                lks = "lk%d" % (b % 2)
                lxs = "lx%d" % (b % 2)
                p.wait("sync", "t", k2_free[b % 2])
                l1 = p.dma("sync", kb[:], K2s[b], sig=lks)
                p.wait("sync", "v", xin_free[b % 2])
                l2 = p.dma("sync", xb[:], src_hbm[b * 128:(b + 1) * 128, :], sig=lxs)
                p.wait("tensor", lks, l1)
                p.wait("tensor", "v", mm_done[0])
                for oc in range(32):
                    for d2 in range(2):
                        last = (oc == 31 and d2 == 1)
                        p.op("tensor", lambda e, oc=oc, d2=d2, kb=kb: e.matmul(
                            psA[:, d2 * 512:(d2 + 1) * 512],
                            kb[:, oc * 128:(oc + 1) * 128],
                            md[:, oc * D + d2 * 512: oc * D + d2 * 512 + 512],
                            start=(oc == 0), stop=(oc == 31)),
                            sig="t" if last else None)
                k2_free[b % 2] = p.n("t")
                p.wait("vector", "t", p.n("t"))
                p.wait("vector", lxs, l2)
                p.wait("vector", "stX%d" % (b % 2), xo_free[b % 2])
                p.op("vector", lambda e, b=b, xb=xb, xob=xob: e.scalar_tensor_tensor(
                    xob[:], psA[:], f2_t[:, b:b + 1], xb[:], ALU.mult, ALU.add), sig="v")
                mm_done[0] = p.n("v")
                xin_free[b % 2] = p.n("v")
                p.wait("scalar", "v", p.n("v"))
                if post_stats is not None:
                    mun, rsn = post_stats
                    ln_stats_tile(xob, mun[:, b:b + 1], rsn[:, b:b + 1])
                xo_free[b % 2] = p.dma("scalar", dst_hbm[b * 128:(b + 1) * 128, :], xob[:],
                                       sig="stX%d" % (b % 2))
            p.barrier()

    def attention(src_hbm, dst_hbm, mu_t, rs_t, post_stats):
        with contextlib.ExitStack() as st_:
            sb = lambda n, sh, dt: st_.enter_context(nc.sbuf_tensor(_uniq(n), sh, dt)).ap()
            sintab = sb("at_sin", [128, NB * 32], F32)
            costab = sb("at_cos", [128, NB * 32], F32)
            dmask = sb("at_dmask", [128, H * 128], F32)
            mstab = sb("at_ms", [128, 8 * 128], F32)
            mbc = sb("at_mbc", [128, H], F32)
            scc = sb("at_scc", [128, H], F32)
            lns = sb("at_lns", [128, H], F32)
            lns2 = sb("at_lns2", [128, 8], F32)
            svals = sb("at_sv", [128, H], F32)
            gbc = sb("at_gbc", [128, D], F32)
            bbc = sb("at_bbc", [128, D], F32)
            grow = sb("at_grow", [1, D], F32)
            brow = sb("at_brow", [1, D], F32)
            ssb = sb("at_ssb", [128, H * DH], F32)
            wstg = sb("at_wstg", [128, 8 * D], F32)
            xnT = sb("at_xnT", [128, BPE * D], F32)
            vq_t = sb("at_vq", [128, BPE * D], F32)
            qT = sb("at_qT", [128, BPE * D], F32)
            kT = sb("at_kT", [128, BPE * D], F32)
            kbq = sb("at_kbq", [128, BPE * D], F32)
            xab = sb("at_xab", [128, 2 * D], F32)
            xn2 = sb("at_xn2", [128, D], F32)
            rot = sb("at_rot", [128, D], F32)
            adt = sb("at_adt", [128, 128], F32)
            qst = sb("at_qst", [128, 128], F32)
            rr1 = sb("at_rr1", [128, D], F32)
            rr2 = sb("at_rr2", [128, D], F32)
            rrT = sb("at_rrT", [128, D], F32)
            gnq = sb("at_gnq", [128, H], F32)
            gnm = sb("at_gnm", [128, H], F32)
            gnr = sb("at_gnr", [128, H], F32)
            xo2 = sb("at_xo2", [128, 2 * D], F32)
            ta = sb("at_ta", [128, 1024], F32)
            tb_ = sb("at_tb", [128, 1024], F32)
            ti3 = sb("at_ti3", [128, 1024], I16)

            g1 = p.dma("sync", grow[:], gam_d[:], sig="lw")
            g2 = p.dma("sync", brow[:], bet_d[:], sig="lw")
            p.wait("tensor", "lw", g2)
            for half in range(2):
                p.op("tensor", lambda e, half=half: e.matmul(
                    psA[:, half * 512:(half + 1) * 512], ones1r[:],
                    grow[0:1, half * 512:(half + 1) * 512], start=True, stop=True))
                p.op("tensor", lambda e, half=half: e.matmul(
                    psB[:, half * 512:(half + 1) * 512], ones1r[:],
                    brow[0:1, half * 512:(half + 1) * 512], start=True, stop=True),
                    sig="t" if half == 1 else None)
            p.wait("vector", "t", p.n("t"))
            p.op("vector", lambda e: e.tensor_copy(gbc[:], psA[:]))
            p.op("vector", lambda e: e.tensor_copy(bbc[:], psB[:]), sig="v")
            p.wait("tensor", "v", p.n("v"))

            for hh in range(H):
                p.op("vector", lambda e, hh=hh: e.memset(svals[:, hh:hh + 1], DECAY[hh]))
            p.inc("vector", "v", 1)
            p.wait("scalar", "v", p.n("v"))
            p.op("scalar", lambda e: e.activation(lns[:], svals[:], ACT.Ln), sig="sc")
            p.wait("vector", "sc", p.n("sc"))
            for pc in range(8):
                p.op("vector", lambda e, pc=pc: e.tensor_copy(lns2[0:64, pc:pc + 1],
                                                              lns[0:64, 2 * pc:2 * pc + 1]))
                p.op("vector", lambda e, pc=pc: e.tensor_copy(
                    lns2[64:128, pc:pc + 1], lns[64:128, 2 * pc + 1:2 * pc + 2]))
            p.inc("vector", "v", 1)
            p.wait("scalar", "v", p.n("v"))
            p.wait("gpsimd", "v", p.n("v"))
            p.op("gpsimd", lambda e: e.iota(ta[:, 0:128], [[1, 128]], base=0,
                                            channel_multiplier=-1,
                                            allow_small_or_imprecise_dtypes=True))
            p.op("gpsimd", lambda e: e.iota(ta[:, 128:256], [[1, 128]], base=1,
                                            channel_multiplier=0,
                                            allow_small_or_imprecise_dtypes=True))
            p.op("gpsimd", lambda e: e.iota(ta[:, 256:257], [[1, 1]], base=127,
                                            channel_multiplier=-1,
                                            allow_small_or_imprecise_dtypes=True))
            p.op("gpsimd", lambda e: e.iota(ta[:, 512:544], [[128, 32]], base=0,
                                            channel_multiplier=1,
                                            allow_small_or_imprecise_dtypes=True),
                 sig="g")
            gt_ = p.n("g")
            p.wait("vector", "g", gt_)
            p.wait("scalar", "g", gt_)
            p.op("vector", lambda e: e.tensor_single_scalar(tb_[:, 0:128], ta[:, 0:128],
                                                            0.0, ALU.is_ge), sig="v")
            p.wait("scalar", "v", p.n("v"))
            for hh in range(H):
                p.op("scalar", lambda e, hh=hh: e.activation(
                    dmask[:, hh * 128:(hh + 1) * 128], ta[:, 0:128], ACT.Exp,
                    scale=lns[:, hh:hh + 1]))
                p.op("scalar", lambda e, hh=hh: e.activation(
                    mbc[:, hh:hh + 1], ta[:, 256:257], ACT.Exp, scale=lns[:, hh:hh + 1]))
                p.op("scalar", lambda e, hh=hh: e.activation(
                    scc[:, hh:hh + 1], C_128, ACT.Exp, scale=lns[:, hh:hh + 1]))
            for pc in range(8):
                p.op("scalar", lambda e, pc=pc: e.activation(
                    mstab[:, pc * 128:(pc + 1) * 128], ta[:, 128:256], ACT.Exp,
                    scale=lns2[:, pc:pc + 1]), sig="sc" if pc == 7 else None)
            p.wait("vector", "sc", p.n("sc"))
            for hh in range(H):
                p.op("vector", lambda e, hh=hh: e.tensor_tensor(
                    dmask[:, hh * 128:(hh + 1) * 128], dmask[:, hh * 128:(hh + 1) * 128],
                    tb_[:, 0:128], ALU.mult))
            p.op("vector", lambda e: e.memset(ssb[:], 0.0))
            p.inc("vector", "v", 1)
            p.wait("gpsimd", "v", p.n("v"))
            p.op("gpsimd", lambda e: e.iota(tb_[:, 224:256], [[128, 32]], base=0,
                                            channel_multiplier=1,
                                            allow_small_or_imprecise_dtypes=True))
            p.op("gpsimd", lambda e: e.iota(tb_[:, 160:192], [[1, 32]], base=0,
                                            channel_multiplier=0,
                                            allow_small_or_imprecise_dtypes=True), sig="g")
            p.wait("scalar", "g", p.n("g"))
            p.op("scalar", lambda e: e.activation(tb_[:, 192:224], tb_[:, 160:192], ACT.Exp,
                                                  scale=C_LNA), sig="sc")
            p.wait("vector", "sc", p.n("sc"))
            p.op("vector", lambda e: e.tensor_tensor(
                ta[:].rearrange("p (b u) -> p b u", u=32),
                tb_[:, 192:224].rearrange("p (a u) -> p a u", a=1).broadcast_to([128, 32, 32]),
                tb_[:, 224:256].rearrange("p (b a) -> p b a", a=1).broadcast_to([128, 32, 32]),
                ALU.mult), sig="v")
            p.wait("scalar", "v", p.n("v"))

            def sincos(dst, phase_shift):
                p.op("vector", lambda e: e.tensor_single_scalar(tb_[:], ta[:],
                                                                phase_shift, ALU.add))
                p.op("vector", lambda e: e.tensor_single_scalar(rot[:], tb_[:],
                                                                1.0 / TWO_PI, ALU.mult),
                     sig="v")
                p.wait("scalar", "v", p.n("v"))
                p.op("scalar", lambda e: e.activation(ti3[:], rot[:], ACT.Identity,
                                                      bias=C_HALF))
                p.op("scalar", lambda e: e.copy(xn2[:], ti3[:]), sig="sc")
                p.wait("vector", "sc", p.n("sc"))
                p.op("vector", lambda e: e.tensor_tensor(rot[:], rot[:], xn2[:],
                                                         ALU.subtract))
                p.op("vector", lambda e: e.tensor_single_scalar(rot[:], rot[:], -0.5,
                                                                ALU.is_lt))
                p.op("vector", lambda e: e.tensor_tensor(xn2[:], xn2[:], rot[:],
                                                         ALU.subtract))
                p.op("vector", lambda e: e.scalar_tensor_tensor(tb_[:], xn2[:], -TWO_PI,
                                                                tb_[:], ALU.mult, ALU.add))
                p.op("vector", lambda e: e.tensor_scalar(tb_[:], tb_[:], PI_CLAMP,
                                                         -PI_CLAMP, ALU.min, ALU.max),
                     sig="v")
                p.wait("scalar", "v", p.n("v"))
                p.op("scalar", lambda e, dst=dst: e.activation(dst, tb_[:], ACT.Sin),
                     sig="sc")
                p.wait("vector", "sc", p.n("sc"))

            sincos(sintab, 0.0)
            sincos(costab, np.pi / 2)
            p.inc("scalar", "sc", 1)
            p.wait("tensor", "sc", p.n("sc"))
            p.wait("sync", "sc", p.n("sc"))

            s_dma = [None]
            s_copy = [None]
            wo_evac = [None, None]
            xo_free2 = [None, None]
            xab_free = [None, None]

            def load_w(src, gate_t):
                p.wait("sync", "t", gate_t)
                for i in range(8):
                    tgt = p.dma("sync", wstg[:, i * D:(i + 1) * D],
                                src[i * 128:(i + 1) * 128, :], sig="lw")
                return tgt

            wstg_gate = [None]
            t_heads = [None]
            gn_psC = [None]
            for ei in range(NEI):
                blocks = [ei * BPE + k for k in range(BPE)]
                wl = load_w(wvt, wstg_gate[0])
                p.wait("tensor", "lw", wl)
                psPP = [psB, psC]
                pp_free = [("v", gn_psC[0]) if gn_psC[0] else None,
                           ("v", gn_psC[0]) if gn_psC[0] else None]
                trT = [None]
                def a1_xnv(lb, b):
                    xb = xab[:, (lb % 2) * D:(lb % 2) * D + D]
                    lxs = "lx%d" % (lb % 2)
                    p.wait("sync", "v", xab_free[lb % 2])
                    ldt = p.dma("sync", xb[:], src_hbm[b * 128:(b + 1) * 128, :], sig=lxs)
                    p.wait("vector", lxs, ldt)
                    p.wait("vector", "t", trT[0])
                    p.op("vector", lambda e, b=b, xb=xb: e.tensor_scalar(
                        xn2[:], xb[:], mu_t[:, b:b + 1], rs_t[:, b:b + 1],
                        ALU.subtract, ALU.mult), sig="v")
                    xab_free[lb % 2] = p.n("v")
                    p.wait("tensor", "v", p.n("v"))
                    for c_ in range(8):
                        p.op("tensor", lambda e, c_=c_: e.transpose(
                            psA[:, c_ * 128:(c_ + 1) * 128],
                            xn2[:, c_ * 128:(c_ + 1) * 128], idf[:]),
                            sig="t" if c_ == 7 else None)
                    trT[0] = p.n("t")
                    p.wait("scalar", "t", p.n("t"))
                    p.op("scalar", lambda e, lb=lb: e.copy(
                        xnT[:, lb * D:(lb + 1) * D], psA[:]), sig="sc")
                    p.wait("tensor", "sc", p.n("sc"))

                def a1_proj(lb):
                    ps = psPP[lb % 2]
                    if pp_free[lb % 2] is not None:
                        p.wait("tensor", pp_free[lb % 2][0], pp_free[lb % 2][1])
                    for i in range(8):
                        for v2 in range(2):
                            p.op("tensor", lambda e, ps=ps, i=i, v2=v2, lb=lb: e.matmul(
                                ps[:, v2 * 512:(v2 + 1) * 512],
                                xnT[:, lb * D + i * 128: lb * D + (i + 1) * 128],
                                wstg[:, i * D + v2 * 512: i * D + v2 * 512 + 512],
                                start=(i == 0), stop=(i == 7)),
                                sig="t" if (i == 7 and v2 == 1) else None)
                    wstg_gate[0] = p.n("t")
                    return p.n("t")

                def a1_vevac(lb, tproj):
                    ps = psPP[lb % 2]
                    p.wait("scalar", "t", tproj)
                    p.op("scalar", lambda e, ps=ps, lb=lb: e.copy(
                        vq_t[:, lb * D:(lb + 1) * D], ps[:]), sig="sc")
                    pp_free[lb % 2] = ("sc", p.n("sc"))

                pend = None
                for lb, b in enumerate(blocks):
                    a1_xnv(lb, b)
                    tp = a1_proj(lb)
                    if pend is not None:
                        a1_vevac(*pend)
                    pend = (lb, tp)
                a1_vevac(*pend)

                for which, wsrc, dstT in (("q", wqt, qT), ("k", wkt, kT)):
                    wl = load_w(wsrc, wstg_gate[0])
                    p.wait("tensor", "lw", wl)

                    def qk_rot(lb, b, tproj, which=which, dstT=dstT):
                        ps = psPP[lb % 2]
                        cb = costab[:, b * 32:(b + 1) * 32]
                        sb_ = sintab[:, b * 32:(b + 1) * 32]
                        bc = lambda t: t.rearrange("p (a u) -> p a u", a=1).broadcast_to(
                            [128, H, 32])
                        x0 = ps[:].rearrange("p (h c u) -> p h c u", h=H, c=2)[:, :, 0, :]
                        x1 = ps[:].rearrange("p (h c u) -> p h c u", h=H, c=2)[:, :, 1, :]
                        r0 = rot[:].rearrange("p (h c u) -> p h c u", h=H, c=2)[:, :, 0, :]
                        r1 = rot[:].rearrange("p (h c u) -> p h c u", h=H, c=2)[:, :, 1, :]
                        ta0 = ta[:, 0:512].rearrange("p (h u) -> p h u", h=H)
                        ta1 = ta[:, 512:1024].rearrange("p (h u) -> p h u", h=H)
                        p.wait("vector", "t", tproj)
                        p.wait("vector", "t", trT[0])
                        p.op("vector", lambda e, x0=x0, cb=cb, ta0=ta0: e.tensor_tensor(
                            ta0, x0, bc(cb), ALU.mult))
                        p.op("vector", lambda e, x1=x1, sb_=sb_, ta1=ta1: e.tensor_tensor(
                            ta1, x1, bc(sb_), ALU.mult))
                        p.op("vector", lambda e, r0=r0, ta0=ta0, ta1=ta1: e.tensor_tensor(
                            r0, ta0, ta1, ALU.subtract))
                        p.op("vector", lambda e, x0=x0, sb_=sb_, ta0=ta0: e.tensor_tensor(
                            ta0, x0, bc(sb_), ALU.mult))
                        p.op("vector", lambda e, x1=x1, cb=cb, ta1=ta1: e.tensor_tensor(
                            ta1, x1, bc(cb), ALU.mult))
                        p.op("vector", lambda e, r1=r1, ta0=ta0, ta1=ta1: e.tensor_tensor(
                            r1, ta0, ta1, ALU.add), sig="v")
                        pp_free[lb % 2] = ("v", p.n("v"))
                        if which == "k":
                            p.op("vector", lambda e, lb=lb: e.tensor_tensor(
                                kbq[:, lb * D:(lb + 1) * D].rearrange(
                                    "p (h u) -> p h u", h=H),
                                rot[:].rearrange("p (h u) -> p h u", h=H),
                                mbc[:].rearrange("p (h a) -> p h a", a=1).broadcast_to(
                                    [128, H, DH]), ALU.mult), sig="v")
                        vrot = p.n("v")
                        p.wait("tensor", "v", vrot)
                        for c_ in range(8):
                            p.op("tensor", lambda e, c_=c_: e.transpose(
                                psA[:, c_ * 128:(c_ + 1) * 128],
                                rot[:, c_ * 128:(c_ + 1) * 128], idf[:]),
                                sig="t" if c_ == 7 else None)
                        trT[0] = p.n("t")
                        p.wait("scalar", "t", p.n("t"))
                        p.op("scalar", lambda e, lb=lb, dstT=dstT: e.copy(
                            dstT[:].rearrange("p (c t) -> p c t", c=8)[:, :, lb * 128:
                                                                      (lb + 1) * 128],
                            psA[:].rearrange("p (c t) -> p c t", c=8)), sig="sc")
                        p.wait("tensor", "sc", p.n("sc"))

                    pend = None
                    for lb, b in enumerate(blocks):
                        tp = a1_proj(lb)
                        if pend is not None:
                            qk_rot(*pend)
                        pend = (lb, b, tp)
                    qk_rot(*pend)

                wl = load_w(wo_d, wstg_gate[0])

                gn_a = xnT[:, 0:D]
                gn_b = xnT[:, D:2 * D]
                woT = xnT[:, 2 * D:3 * D]

                def emit_A(b, lc, first):
                    p.wait("tensor", "lw", wl)
                    qstall = rrT
                    if not first:
                        p.wait("vector", "t", t_heads[0])
                        qv = qT[:].rearrange("p (pc t) -> p pc t", pc=8)[
                            :, :, lc * 128:(lc + 1) * 128]
                        p.op("vector", lambda e, qv=qv: e.tensor_tensor(
                            qstall[:].rearrange("p (pc t) -> p pc t", pc=8), qv,
                            mstab[:].rearrange("p (pc t) -> p pc t", pc=8), ALU.mult),
                            sig="v")
                        p.wait("tensor", "v", p.n("v"))
                    adt_v = [None, None]
                    for hf in range(2):
                        if hf == 1:
                            p.wait("tensor", "v", adt_v[0])
                        for j in (0, 2, 4, 6, 1, 3, 5, 7):
                            hh = hf * 8 + j
                            pc = hh // 2
                            base = 64 * (hh % 2)
                            ksl = kT[base:base + 64,
                                     pc * 512 + lc * 128: pc * 512 + (lc + 1) * 128]
                            qsl = qT[base:base + 64,
                                     pc * 512 + lc * 128: pc * 512 + (lc + 1) * 128]
                            p.op("tensor", lambda e, j=j, ksl=ksl, qsl=qsl: e.matmul(
                                psB[:, j * 128:(j + 1) * 128], ksl, qsl,
                                start=(j % 4 == 0), stop=True,
                                skip_group_check=(j % 4 != 0)),
                                sig="t" if j == 7 else None)
                        p.wait("vector", "t", p.n("t"))
                        p.op("vector", lambda e, hf=hf: e.tensor_tensor(
                            [rr1, rr2][hf][:], psB[:], dmask[:, hf * 1024:(hf + 1) * 1024],
                            ALU.mult), sig="v")
                        adt_v[hf] = p.n("v")
                    return adt_v

                def emit_R(b, lc, first, adt_v):
                    p.wait("tensor", "sd", s_dma[0])
                    p.wait("tensor", "v", gn_psC[0])
                    for hf in range(2):
                        p.wait("tensor", "v", adt_v[hf])
                        for j in range(8):
                            hh = hf * 8 + j
                            base = 64 * (hh % 2)
                            vsl = vq_t[:, lc * D + hh * DH: lc * D + (hh + 1) * DH]
                            p.op("tensor", lambda e, hh=hh, j=j, hf=hf, vsl=vsl,
                                 first=first: e.matmul(
                                     psC[:, hh * DH:(hh + 1) * DH],
                                     [rr1, rr2][hf][:, j * 128:(j + 1) * 128], vsl,
                                     start=True, stop=first),
                                 sig="t" if (first and hh == H - 1) else None)
                            if not first:
                                p.op("tensor", lambda e, hh=hh, base=base: e.matmul(
                                    psC[:, hh * DH:(hh + 1) * DH],
                                    _qst_ap(rrT, hh, lc),
                                    ssb[base:base + 64, hh * DH:(hh + 1) * DH],
                                    start=False, stop=True),
                                    sig="t" if hh == H - 1 else None)
                    t_heads[0] = p.n("t")

                gn_c = xnT[:, 3 * D:4 * D]

                def emit_S(b, lc, first):
                    th = t_heads[0]
                    p.wait("vector", "t", th)
                    if not first:
                        p.op("vector", lambda e: e.tensor_tensor(
                            psD[0:64, :].rearrange("p (h u) -> p h u", h=H),
                            psD[0:64, :].rearrange("p (h u) -> p h u", h=H),
                            scc[0:64, :].rearrange("p (h a) -> p h a", a=1).broadcast_to(
                                [64, H, DH]), ALU.mult), sig="v")
                        p.wait("tensor", "v", p.n("v"))
                    for hh in range(H):
                        ksl2 = kbq[:, lc * D + hh * DH: lc * D + (hh + 1) * DH]
                        vsl2 = vq_t[:, lc * D + hh * DH: lc * D + (hh + 1) * DH]
                        st_flag = first and (hh % 8 == 0)
                        sp_flag = (hh % 8 == 7)
                        p.op("tensor", lambda e, hh=hh, ksl2=ksl2, vsl2=vsl2,
                             st_flag=st_flag, sp_flag=sp_flag, sk=not first: e.matmul(
                                 psD[0:64, hh * DH:(hh + 1) * DH], ksl2, vsl2,
                                 start=st_flag, stop=sp_flag, skip_group_check=sk),
                             sig="t" if hh == H - 1 else None)
                    p.wait("vector", "t", p.n("t"))
                    p.op("vector", lambda e: e.tensor_copy(ssb[0:64, :], psD[0:64, :]),
                         sig="v")
                    p.wait("sync", "v", p.n("v"))
                    s_dma[0] = p.dma("sync", ssb[64:128, :], ssb[0:64, :], sig="sd")
                    p.op("vector", lambda e: e.tensor_copy(gn_c, psC[:]), sig="v")
                    gn_psC[0] = p.n("v")

                def emit_gnWO(b, lc, first):
                    p.wait("scalar", "v", gn_psC[0])
                    p.op("scalar", lambda e: e.activation(dump[:], gn_c, ACT.Square),
                         sig="sc")
                    p.wait("vector", "sc", p.n("sc"))
                    p.op("vector", lambda e: e.tensor_reduce(
                        gnm[:], gn_c.rearrange("p (h u) -> p h u", h=H), AX.X, ALU.add))
                    p.op("vector", lambda e: e.tensor_reduce(
                        gnq[:], dump[:].rearrange("p (h u) -> p h u", h=H), AX.X, ALU.add))
                    p.op("vector", lambda e: e.tensor_single_scalar(gnm[:], gnm[:],
                                                                    1.0 / DH, ALU.mult))
                    p.op("vector", lambda e: e.tensor_tensor(ta[:, 0:H], gnm[:], gnm[:],
                                                             ALU.mult))
                    p.op("vector", lambda e: e.scalar_tensor_tensor(
                        gnr[:], gnq[:], 1.0 / DH, ta[:, 0:H], ALU.mult, ALU.subtract),
                        sig="v")
                    p.wait("scalar", "v", p.n("v"))
                    p.op("scalar", lambda e: e.activation(gnr[:], gnr[:], ACT.Ln,
                                                          bias=C_EPS))
                    p.op("scalar", lambda e: e.activation(gnr[:], gnr[:], ACT.Exp,
                                                          scale=C_NEGH), sig="sc")
                    p.wait("vector", "sc", p.n("sc"))
                    hview = lambda t: t.rearrange("p (h u) -> p h u", h=H)
                    hbc = lambda t: t.rearrange("p (h a) -> p h a", a=1).broadcast_to(
                        [128, H, DH])
                    p.op("vector", lambda e: e.tensor_tensor(
                        hview(gn_a), gn_c.rearrange("p (h u) -> p h u", h=H),
                        hbc(gnm[:]), ALU.subtract))
                    p.op("vector", lambda e: e.tensor_tensor(hview(gn_b), hview(gbc[:]),
                                                             hbc(gnr[:]), ALU.mult))
                    p.op("vector", lambda e: e.tensor_tensor(gn_a, gn_a, gn_b, ALU.mult))
                    p.op("vector", lambda e: e.tensor_tensor(gn_a, gn_a, bbc[:],
                                                             ALU.add), sig="v")
                    p.wait("tensor", "v", p.n("v"))
                    for c_ in range(8):
                        p.op("tensor", lambda e, c_=c_: e.transpose(
                            psA[:, c_ * 128:(c_ + 1) * 128], gn_a[:, c_ * 128:(c_ + 1) * 128],
                            idf[:]), sig="t" if c_ == 7 else None)
                    p.wait("vector", "t", p.n("t"))
                    p.op("vector", lambda e: e.tensor_copy(woT[:], psA[:]), sig="v")
                    p.wait("tensor", "v", p.n("v"))
                    for i in range(8):
                        for d2 in range(2):
                            p.op("tensor", lambda e, i=i, d2=d2: e.matmul(
                                psA[:, d2 * 512:(d2 + 1) * 512], woT[:, i * 128:(i + 1) * 128],
                                wstg[:, i * D + d2 * 512: i * D + d2 * 512 + 512],
                                start=(i == 0), stop=(i == 7)),
                                sig="t" if (i == 7 and d2 == 1) else None)
                    wo_t = p.n("t")
                    wstg_gate[0] = wo_t
                    xb2 = xab[:, (lc % 2) * D:(lc % 2) * D + D]
                    lxs2 = "lx%d" % (lc % 2)
                    p.wait("sync", "v", xab_free[lc % 2])
                    l2 = p.dma("sync", xb2[:], src_hbm[b * 128:(b + 1) * 128, :], sig=lxs2)
                    p.wait("vector", "t", wo_t)
                    p.wait("vector", lxs2, l2)
                    xob2 = xo2[:, (lc % 2) * D:(lc % 2) * D + D]
                    p.wait("vector", "stX%d" % (lc % 2), xo_free2[lc % 2])
                    p.op("vector", lambda e, xb2=xb2, xob2=xob2: e.tensor_tensor(
                        xob2[:], psA[:], xb2[:], ALU.add), sig="v")
                    xab_free[lc % 2] = p.n("v")
                    p.wait("scalar", "v", p.n("v"))
                    if post_stats is not None:
                        mun, rsn = post_stats
                        ln_stats_tile(xob2, mun[:, b:b + 1], rsn[:, b:b + 1])
                    xo_free2[lc % 2] = p.dma("scalar", dst_hbm[b * 128:(b + 1) * 128, :],
                                             xob2[:], sig="stX%d" % (lc % 2))

                pending = None
                for lc, b in enumerate(blocks):
                    first = (b == 0)
                    av = emit_A(b, lc, first)
                    if pending is not None:
                        emit_S(*pending)
                    emit_R(b, lc, first, av)
                    if pending is not None:
                        emit_gnWO(*pending)
                    pending = (b, lc, first)
                emit_S(*pending)
                emit_gnWO(*pending)
            p.barrier()

    prelude()
    ln1_stats()

    only_ffn1 = (stage == "ffn1")
    ternarize(wgt, 8, INNER, wmg, None, Mgs)
    ternarize(wut, 8, INNER, wmu, None, Mus)
    mgg = nc.sbuf_tensor(_uniq("mg_pre"), [128, 8 * INNER], BF16)
    mug = nc.sbuf_tensor(_uniq("mu_pre"), [128, 8 * INNER], BF16)
    mg_pre = mgg.__enter__().ap()
    mu_pre = mug.__enter__().ap()
    for c_ in range(8):
        p.dma("sync", mg_pre[:, c_ * INNER:(c_ + 1) * INNER],
              Mgs[:, c_ * INNER:(c_ + 1) * INNER], sig="lw")
    for c_ in range(8):
        p.dma("sync", mu_pre[:, c_ * INNER:(c_ + 1) * INNER],
              Mus[:, c_ * INNER:(c_ + 1) * INNER], sig="lw")
    mpre_t = p.n("lw")
    ternarize(wdt, 32, D, wmd, None, Mds)
    p.wait("tensor", "lw", mpre_t)
    ffn_phase(x_in, y_out if only_ffn1 else Xs, mu1, rs1, f2b1, mg_pre, mu_pre,
              post_stats=None if only_ffn1 else (mu2, rs2),
              adopt_guards=(mgg, mug))

    if stage in ("attn", "full"):
        attention(Xs, Xs if stage == "full" else y_out, mu2, rs2,
                  post_stats=(mu3, rs3) if stage == "full" else None)

    if stage == "full":
        ffn_phase(Xs, y_out, mu3, rs3, f2b2, None, None, post_stats=None)

    for s_ in ("stK", "stX0", "stX1", "stM0", "stM1"):
        p.wait("sync", s_, p.n(s_))
        p.wait("vector", s_, p.n(s_))

    p.emit()
    return nc


def _host_layouts(WQ, WK, WV, WO, w_gate, w_up, w_down, gamma, beta):
    perm = np.concatenate([np.arange(0, DH, 2), np.arange(1, DH, 2)])
    wq = WQ[:, perm, :]
    wk = WK[:, perm, :]
    return dict(
        wgT=np.ascontiguousarray(w_gate.T),
        wuT=np.ascontiguousarray(w_up.T),
        wdT=np.ascontiguousarray(w_down.T),
        wqT=np.ascontiguousarray(wq.transpose(2, 0, 1).reshape(D, D)),
        wkT=np.ascontiguousarray(wk.transpose(2, 0, 1).reshape(D, D)),
        wvT=np.ascontiguousarray(WV.transpose(2, 0, 1).reshape(D, D)),
        wo=np.ascontiguousarray(WO),
        gam=np.ascontiguousarray(gamma.reshape(1, D)),
        bet=np.ascontiguousarray(beta.reshape(1, D)))


def kernel(L0_attn_x, L0_attn_WQ, L0_attn_WK, L0_attn_WV, L0_attn_WO,
           L0_attn_gamma, L0_attn_beta, L0_ffn_w_gate, L0_ffn_w_up,
           L0_ffn_w_down):
    shared = _host_layouts(L0_attn_WQ, L0_attn_WK, L0_attn_WV, L0_attn_WO,
                           L0_ffn_w_gate, L0_ffn_w_up, L0_ffn_w_down,
                           L0_attn_gamma, L0_attn_beta)
    nc = build_nc("full")
    in_maps = []
    for i in range(8):
        m = dict(shared)
        m["x"] = np.ascontiguousarray(np.asarray(L0_attn_x)[i])
        in_maps.append(m)
    res = run_bass_kernel_spmd(nc, in_maps, core_ids=list(range(8)))
    return np.stack([np.asarray(r["y"]) for r in res.results], axis=0)
